# version 05
# speedup vs baseline: 1.5317x; 1.1027x over previous
import contextlib
from collections import defaultdict

import numpy as np

import concourse.bass as bass
import concourse.mybir as mybir
from concourse.bass_utils import run_bass_kernel_spmd

F32 = mybir.dt.float32
BF16 = mybir.dt.bfloat16
AF = mybir.ActivationFunctionType
ALU = mybir.AluOpType
AX = mybir.AxisListType

NCORE = 8
RPC = 2
T = 4096
SEQ = 2048
NT = 8
CH = 512
D = 1024
DTI = 8
BIG = 1e30


class Prog:
    def __init__(self):
        self.nc = bass.Bass("TRN2", target_bir_lowering=False)
        self.es = contextlib.ExitStack()
        self.root_es = self.es
        self.block = self.es.enter_context(self.nc.Block())
        self.sems = {}
        self.cnt = defaultdict(int)
        self._names = defaultdict(int)
        self.sem("q")

    def sem(self, name):
        if name not in self.sems:
            self.sems[name] = self.root_es.enter_context(self.nc.semaphore(name))
        return self.sems[name]

    def sb(self, shape, dtype=F32, tag="sb"):
        self._names[tag] += 1
        return self.es.enter_context(
            self.nc.sbuf_tensor(f"{tag}_{self._names[tag]}", shape, dtype))

    def psum(self, shape, dtype=F32, tag="ps"):
        self._names[tag] += 1
        return self.es.enter_context(
            self.nc.psum_tensor(f"{tag}_{self._names[tag]}", shape, dtype))

    def dram(self, name, shape, dtype=F32, kind="Internal"):
        return self.nc.dram_tensor(name, shape, dtype, kind=kind).ap()

    @contextlib.contextmanager
    def scoped(self):
        es = contextlib.ExitStack()
        old = self.es
        self.es = es
        try:
            yield
        finally:
            self.es = old
            es.close()

    def dma(self, e, out, in_, **kw):
        instr = e.dma_start(out=out, in_=in_, **kw)
        instr.then_inc(self.sem("q"), 16)
        self.cnt["q"] += 16
        return instr

    def dma_pf(self, e, out, in_, **kw):
        instr = e.dma_start(out=out, in_=in_, **kw)
        instr.then_inc(self.sem("pf"), 16)
        self.cnt["pf"] += 16
        return instr

    def stage(self, **fns):
        b = self.block
        start = self.cnt["q"]
        decs = dict(sync=b.sync, tensor=b.tensor, vector=b.vector,
                    scalar=b.scalar, gpsimd=b.gpsimd)
        for name, fn in fns.items():
            dec = decs[name]

            @dec
            def _(e, fn=fn, start=start):
                e.wait_ge(self.sem("q"), start)
                last = fn(e)
                if last is not None:
                    last.then_inc(self.sem("q"), 1)
                    self.cnt["q"] += 1

    def final_wait(self):
        b = self.block
        tgt = self.cnt["q"]

        @b.sync
        def _(e):
            e.wait_ge(self.sem("q"), tgt)

    def finish(self):
        self.es.close()
        return self.nc


def csl(c):
    return slice(c * CH, (c + 1) * CH)



def emit_consts(p):
    C = {}
    C["ones_row"] = p.sb([1, 128], F32, "onesr")
    C["ones_col"] = p.sb([128, 1], F32, "onesc")
    C["ident"] = p.sb([128, 128], BF16, "idb")
    C["identf"] = p.sb([128, 128], F32, "idf")
    C["sel32"] = p.sb([32, 32 * 128], F32, "sel32")

    def f(g):
        g.memset(C["ones_row"][:], 1.0)
        g.memset(C["ones_col"][:], 1.0)
        g.memset(C["ident"][:], 0.0)
        g.affine_select(out=C["ident"][:], in_=C["ident"][:], pattern=[[-1, 128]],
                        base=0, channel_multiplier=1, compare_op=ALU.not_equal,
                        fill=1.0)
        g.memset(C["identf"][:], 0.0)
        g.affine_select(out=C["identf"][:], in_=C["identf"][:], pattern=[[-1, 128]],
                        base=0, channel_multiplier=1, compare_op=ALU.not_equal,
                        fill=1.0)
        g.memset(C["sel32"][:], 0.0)
        return g.affine_select(
            out=C["sel32"][:].rearrange("p (e c) -> p e c", e=32),
            in_=C["sel32"][:].rearrange("p (e c) -> p e c", e=32),
            pattern=[[-1, 32], [0, 128]], base=0, channel_multiplier=1,
            compare_op=ALU.not_equal, fill=1.0)

    p.stage(gpsimd=f)
    return C



def emit_ln(p, C, x_hbm, xn32_hbm, xnbf_hbm, hbf_hbm=None, normw_hbm=None,
            x2_hbm=None):
    with p.scoped():
        ones_row = C["ones_row"]
        ones_col = C["ones_col"]
        xt = [p.sb([128, CH], F32, "lnx") for _ in range(DTI)]
        x2t = p.sb([128, CH], F32, "lnx2") if x2_hbm is not None else None
        sqt = [p.sb([128, CH], F32, "lnsq") for _ in range(DTI)]
        s1 = p.psum([1, CH], F32, "lns1")
        s2 = p.psum([1, CH], F32, "lns2")
        rows = p.sb([1, 8 * CH], F32, "lnrow")
        meanR = p.sb([1, T], F32, "lnmean")
        rstdR = p.sb([1, T], F32, "lnrstd")
        rmsR = p.sb([1, T], F32, "lnrms") if hbf_hbm is not None else None
        bc = [p.psum([128, CH], F32, "lnbc") for _ in range(3)]
        t1 = p.sb([128, CH], F32, "lnt1")
        xnt = p.sb([128, CH], F32, "lnxnt")
        xnb = p.sb([128, CH], BF16, "lnxnb")
        hbt = p.sb([128, CH], BF16, "lnhbt")
        nwc = p.sb([128, DTI], F32, "lnnw") if hbf_hbm is not None else None

        def S(i):
            return rows[:, i * CH:(i + 1) * CH]

        if nwc is not None:
            p.stage(sync=lambda s: (p.dma(s, nwc[:], normw_hbm), None)[1])

        def load_x(s, c):
            for i in range(DTI):
                p.dma(s, xt[i][:], x_hbm[i, :, csl(c)])
                if x2_hbm is not None:
                    p.dma(s, x2t[:], x2_hbm[i, :, csl(c)])
            return None

        for c in range(NT):
            if x2_hbm is None:
                p.stage(sync=lambda s, c=c: load_x(s, c))
            else:
                for i in range(DTI):
                    def ld1(s, c=c, i=i):
                        p.dma(s, xt[i][:], x_hbm[i, :, csl(c)])
                        p.dma(s, x2t[:], x2_hbm[i, :, csl(c)])
                    p.stage(sync=ld1)
                    p.stage(vector=lambda v, i=i: v.tensor_add(
                        xt[i][:], xt[i][:], x2t[:]))

            def sq(sc, c=c):
                for i in range(DTI):
                    last = sc.activation(sqt[i][:], xt[i][:], AF.Square)
                return last
            p.stage(scalar=sq)

            def mm(t, c=c):
                for i in range(DTI):
                    t.matmul(s1[:], ones_col[:], xt[i][:],
                             start=(i == 0), stop=(i == DTI - 1))
                for i in range(DTI):
                    last = t.matmul(s2[:], ones_col[:], sqt[i][:],
                                    start=(i == 0), stop=(i == DTI - 1))
                return last
            p.stage(tensor=mm)

            def stats_v(v, c=c):
                v.tensor_scalar(meanR[:, csl(c)], s1[:], 1.0 / D, None, ALU.mult)
                v.tensor_scalar(S(1), s2[:], 1.0 / D, None, ALU.mult)
                v.tensor_mul(S(2), meanR[:, csl(c)], meanR[:, csl(c)])
                v.tensor_sub(S(1), S(1), S(2))
                return v.tensor_scalar(S(2), S(1), 1e-5, None, ALU.add)
            p.stage(vector=stats_v)

            def stats_s(sc, c=c):
                sc.activation(S(3), S(2), AF.Ln)
                last = sc.activation(rstdR[:, csl(c)], S(3), AF.Exp, scale=-0.5)
                if rmsR is not None:
                    last = sc.activation(S(4), S(3), AF.Exp, scale=-1.0)
                return last
            p.stage(scalar=stats_s)

            if rmsR is not None:
                def rms_v(v, c=c):
                    v.tensor_mul(S(5), S(1), S(4))
                    return v.tensor_scalar(S(5), S(5), 1e-5, None, ALU.add)
                p.stage(vector=rms_v)

                def rms_s(sc, c=c):
                    sc.activation(S(6), S(5), AF.Ln)
                    return sc.activation(rmsR[:, csl(c)], S(6), AF.Exp, scale=-0.5)
                p.stage(scalar=rms_s)

            def bcmm(t, c=c):
                t.matmul(bc[0][:], ones_row[0:1, :], meanR[:, csl(c)],
                         start=True, stop=True)
                last = t.matmul(bc[1][:], ones_row[0:1, :], rstdR[:, csl(c)],
                                start=True, stop=True)
                if rmsR is not None:
                    last = t.matmul(bc[2][:], ones_row[0:1, :], rmsR[:, csl(c)],
                                    start=True, stop=True)
                return last
            p.stage(tensor=bcmm)

            for i in range(DTI):
                def ap_v(v, c=c, i=i):
                    v.tensor_sub(t1[:], xt[i][:], bc[0][:])
                    last = v.tensor_mul(xnt[:], t1[:], bc[1][:])
                    if hbf_hbm is not None:
                        last = v.scalar_tensor_tensor(
                            hbt[:], xnt[:], nwc[:, i:i + 1], bc[2][:],
                            ALU.mult, ALU.mult)
                    return last
                p.stage(vector=ap_v)
                p.stage(scalar=lambda sc: sc.activation(xnb[:], xnt[:], AF.Copy))

                def st(s, c=c, i=i):
                    if xn32_hbm is not None:
                        p.dma(s, xn32_hbm[i, :, csl(c)], xnt[:])
                    p.dma(s, xnbf_hbm[i, :, csl(c)], xnb[:])
                    if hbf_hbm is not None:
                        p.dma(s, hbf_hbm[i, :, csl(c)], hbt[:])
                    return None
                p.stage(sync=st)



def emit_router_grouped(p, C, xn32_hbm, rwT_hbm, bias_hbm, combT, E, GRP, K,
                        scale, mask_fill=-1.0, dbg=None):
    EPG = E // GRP
    with p.scoped():
        ones_row, identf = C["ones_row"], C["identf"]
        rwt = p.sb([128, DTI * E], F32, "rwt")
        br = p.sb([1, E], F32, "rbr")
        bb_ps = p.psum([128, CH], F32, "rbb")
        bb = p.sb([128, E], F32, "rbbs")
        xs = [p.sb([128, CH], F32, "rxs") for _ in range(DTI)]
        lg = p.psum([128, CH], F32, "rlg")
        sc_t = p.sb([128, E], F32, "rsc")
        s_t = p.sb([128, E], F32, "rst")
        g1 = p.sb([128, GRP], F32, "rg1")
        g2 = p.sb([128, GRP], F32, "rg2")
        gm = p.sb([128, GRP], F32, "rgm")
        m1 = p.sb([128, 1], F32, "rm1")
        ms = p.sb([128, E], F32, "rmsk")
        tmp = p.sb([128, E], F32, "rtmp")
        selk = p.sb([128, E], F32, "rselk")
        twk = p.sb([128, 1], F32, "rtwk")
        twsum = p.sb([128, 1], F32, "rtws")
        comb = p.sb([128, E], F32, "rcomb")
        ct_ps = p.psum([E, CH], F32, "rct")

        def ld0(s):
            p.dma(s, rwt[:].rearrange("p (a e) -> p a e", a=DTI),
                  rwT_hbm.rearrange("(a p) e -> p a e", p=128))
            p.dma(s, br[:], bias_hbm[None, :])
            return None
        p.stage(sync=ld0)
        p.stage(vector=lambda v: v.memset(combT[:], 0.0))
        p.stage(tensor=lambda t: t.matmul(bb_ps[:, 0:E], ones_row[0:1, :], br[:],
                                          start=True, stop=True))
        p.stage(vector=lambda v: v.tensor_copy(bb[:], bb_ps[:, 0:E]))

        for c in range(NT):
            def ld(s, c=c):
                for i in range(DTI):
                    p.dma(s, xs[i][:], xn32_hbm[i, :, csl(c)])
                return None
            p.stage(sync=ld)

            for tt in range(4):
                def mm(t, c=c, tt=tt):
                    for i in range(DTI):
                        last = t.matmul(
                            lg[0:128, 0:E], xs[i][:, tt * 128:(tt + 1) * 128],
                            rwt[:, i * E:(i + 1) * E],
                            start=(i == 0), stop=(i == DTI - 1))
                    return last
                p.stage(tensor=mm)
                p.stage(scalar=lambda sc: sc.activation(sc_t[:], lg[0:128, 0:E],
                                                        AF.Sigmoid))

                def route_v(v):
                    def W(instr):
                        instr.then_inc(p.sem("vv"), 1)
                        p.cnt["vv"] += 1
                        v.wait_ge(p.sem("vv"), p.cnt["vv"])
                        return instr
                    W(v.tensor_add(s_t[:], sc_t[:], bb[:]))
                    def gsl(j):
                        return s_t[:].rearrange("p (g e) -> p g e", g=GRP)[:, :, j]
                    first = True
                    for jA in range(EPG):
                        for jB in range(jA + 1, EPG):
                            W(v.tensor_add(g2[:], gsl(jA), gsl(jB)))
                            if first:
                                W(v.tensor_copy(g1[:], g2[:]))
                                first = False
                            else:
                                W(v.tensor_max(g1[:], g1[:], g2[:]))
                    W(v.tensor_reduce(m1[:], g1[:], AX.X, ALU.max))
                    W(v.tensor_scalar(gm[:], g1[:], m1[:, 0:1], None, ALU.is_ge))
                    W(v.scalar_tensor_tensor(g2[:], gm[:], -BIG, g1[:], ALU.mult,
                                             ALU.add))
                    W(v.tensor_reduce(m1[:], g2[:], AX.X, ALU.max))
                    W(v.tensor_scalar(g2[:], g2[:], m1[:, 0:1], None, ALU.is_ge))
                    W(v.tensor_add(gm[:], gm[:], g2[:]))
                    W(v.tensor_scalar(ms[:], s_t[:], -mask_fill, None, ALU.add))
                    for j in range(EPG):
                        W(v.tensor_mul(
                            ms[:].rearrange("p (g e) -> p g e", g=GRP)[:, :, j],
                            ms[:].rearrange("p (g e) -> p g e", g=GRP)[:, :, j],
                            gm[:]))
                    W(v.tensor_scalar(ms[:], ms[:], mask_fill, None, ALU.add))
                    W(v.memset(comb[:], 0.0))
                    W(v.memset(twsum[:], 0.0))
                    for k in range(K):
                        W(v.tensor_reduce(m1[:], ms[:], AX.X, ALU.max))
                        W(v.tensor_scalar(selk[:], ms[:], m1[:, 0:1], None,
                                          ALU.is_ge))
                        W(v.tensor_mul(tmp[:], sc_t[:], selk[:]))
                        W(v.tensor_reduce(twk[:], tmp[:], AX.X, ALU.add))
                        W(v.scalar_tensor_tensor(comb[:], selk[:], twk[:, 0:1],
                                                 comb[:], ALU.mult, ALU.add))
                        W(v.tensor_add(twsum[:], twsum[:], twk[:]))
                        W(v.scalar_tensor_tensor(ms[:], selk[:], -BIG, ms[:],
                                                 ALU.mult, ALU.add))
                    W(v.tensor_scalar(twsum[:], twsum[:], 1e-20, None, ALU.add))
                    W(v.reciprocal(twsum[:], twsum[:]))
                    W(v.tensor_scalar(twsum[:], twsum[:], float(scale), None,
                                      ALU.mult))
                    return v.tensor_scalar(comb[:], comb[:], twsum[:, 0:1],
                                           None, ALU.mult)
                p.stage(vector=route_v)

                if dbg is not None and c == 0 and tt == 0:
                    def dmp(s):
                        p.dma(s, dbg["sc"][:, :], sc_t[:])
                        p.dma(s, dbg["st"][:, :], s_t[:])
                        p.dma(s, dbg["g1"][:, :], g1[:])
                        p.dma(s, dbg["gm"][:, :], gm[:])
                        p.dma(s, dbg["ms"][:, :], ms[:])
                        p.dma(s, dbg["comb"][:, :], comb[:])
                        return None
                    p.stage(sync=dmp)
                p.stage(tensor=lambda t: t.transpose(ct_ps[0:E, 0:128],
                                                     comb[:, 0:E], identf[:]))
                p.stage(vector=lambda v, c=c, tt=tt: v.tensor_copy(
                    combT[0:E, (c * 4 + tt) * 128:(c * 4 + tt + 1) * 128],
                    ct_ps[0:E, 0:128]))



def emit_moe_ffn(p, C, xnbf_hbm, x_seed_hbm, out_hbm, combT, experts, tag="moe"):
    sel32 = C["sel32"]
    NG = 4
    CPG = NT // NG
    with p.scoped():
        max_fti = max(ex["FTI"] for ex in experts)
        xnb = [[p.sb([128, CH], BF16, tag + "xnb") for _ in range(DTI)]
               for _ in range(CPG)]
        acc = [[p.sb([128, CH], F32, tag + "acc") for _ in range(DTI)]
               for _ in range(CPG)]
        wg = [p.sb([128, DTI * 128 * max_fti], BF16, tag + "wg")
              for _ in range(2)]
        wu = [p.sb([128, DTI * 128 * max_fti], BF16, tag + "wu")
              for _ in range(2)]
        wd = [p.sb([128, max_fti * 128 * DTI], BF16, tag + "wd")
              for _ in range(2)]
        ps_g = [p.psum([128, CH], F32, tag + "psg") for _ in range(2)]
        ps_u = [p.psum([128, CH], F32, tag + "psu") for _ in range(3)]
        ps_cb = p.psum([128, CH], F32, tag + "pscb")
        cb_s = [p.sb([128, CH], F32, tag + "cbs") for _ in range(2)]
        ps_dn = [p.psum([128, CH], F32, tag + "psdn") for _ in range(2)]
        sg_t = [p.sb([128, CH], BF16, tag + "sg") for _ in range(2)]
        tmp_t = p.sb([128, CH], F32, tag + "tmp")
        h_t = [p.sb([128, CH], BF16, tag + "h") for _ in range(max_fti)]

        def ldw(gp, ex, par):
            FTI = ex["FTI"]
            p.dma_pf(gp, wg[par][:, 0:DTI * 128 * FTI].rearrange(
                "p (a f) -> p a f", a=DTI),
                ex["gate"].rearrange("(a p) f -> p a f", p=128))
            if ex["up"] is not None:
                p.dma_pf(gp, wu[par][:, 0:DTI * 128 * FTI].rearrange(
                    "p (a f) -> p a f", a=DTI),
                    ex["up"].rearrange("(a p) f -> p a f", p=128))
            p.dma_pf(gp, wd[par][:, 0:FTI * 128 * DTI].rearrange(
                "p (a f) -> p a f", a=FTI),
                ex["down"].rearrange("(a p) f -> p a f", p=128))
            return None

        for g in range(NG):
            chunks = list(range(g * CPG, (g + 1) * CPG))
            pf_after = {}

            def seed(s, g=g, chunks=chunks):
                for ci, c in enumerate(chunks):
                    for i in range(DTI):
                        p.dma(s, acc[ci][i][:], x_seed_hbm[i, :, csl(c)])
                        p.dma(s, xnb[ci][i][:], xnbf_hbm[i, :, csl(c)])
                return None
            p.stage(sync=seed, gpsimd=lambda gp: ldw(gp, experts[0], 0))
            pf_after[0] = p.cnt["pf"]

            for ei, ex in enumerate(experts):
                FTI = ex["FTI"]
                par = ei % 2
                if ei + 1 < len(experts):
                    p.stage(gpsimd=lambda gp, ei=ei: ldw(
                        gp, experts[ei + 1], (ei + 1) % 2))
                    pf_after[ei + 1] = p.cnt["pf"]

                units = [(ci, fi) for ci in range(CPG) for fi in range(FTI)]
                NU = len(units)
                pf_tgt = pf_after[ei]

                for u in range(NU + 2):
                    fns = {}
                    if u < NU:
                        ci_u, fi_u = units[u]

                        def tf(t, ex=ex, par=par, ci=ci_u, fi=fi_u, FTI=FTI,
                               u=u, pf_tgt=pf_tgt):
                            if u == 0:
                                t.wait_ge(p.sem("pf"), pf_tgt)
                            c = chunks[ci]
                            if ex["e"] is not None and fi == 0:
                                t.matmul(ps_cb[:],
                                         sel32[:, ex["e"] * 128:
                                               (ex["e"] + 1) * 128],
                                         combT[0:32, csl(c)],
                                         start=True, stop=True)
                            for i in range(DTI):
                                last = t.matmul(
                                    ps_g[u % 2][:],
                                    wg[par][:, (i * FTI + fi) * 128:
                                            (i * FTI + fi + 1) * 128],
                                    xnb[ci][i][:], start=(i == 0),
                                    stop=(i == DTI - 1))
                            if ex["up"] is not None:
                                for i in range(DTI):
                                    last = t.matmul(
                                        ps_u[u % 3][:],
                                        wu[par][:, (i * FTI + fi) * 128:
                                                (i * FTI + fi + 1) * 128],
                                        xnb[ci][i][:], start=(i == 0),
                                        stop=(i == DTI - 1))
                            return last
                        fns["tensor"] = tf
                    if 0 <= u - 1 < NU:
                        us = u - 1
                        fns["scalar"] = (lambda sc, us=us: sc.activation(
                            sg_t[us % 2][:], ps_g[us % 2][:], AF.Silu))
                    if 0 <= u - 2 < NU:
                        uv = u - 2
                        ci_v, fi_v = units[uv]

                        def vf(v, ex=ex, fi=fi_v, uv=uv, ci=ci_v):
                            if ex["up"] is not None:
                                v.tensor_mul(tmp_t[:], sg_t[uv % 2][:],
                                             ps_u[uv % 3][:])
                                if ex["e"] is not None:
                                    return v.tensor_mul(h_t[fi][:], tmp_t[:],
                                                        cb_s[ci % 2][:])
                                return v.tensor_copy(h_t[fi][:], tmp_t[:])
                            if ex["e"] is not None:
                                return v.tensor_mul(h_t[fi][:], sg_t[uv % 2][:],
                                                    cb_s[ci % 2][:])
                            return v.tensor_copy(h_t[fi][:], sg_t[uv % 2][:])
                        fns["vector"] = vf
                    p.stage(**fns)
                    if u < NU and ex["e"] is not None and units[u][1] == 0:
                        p.stage(vector=lambda v, pcb=units[u][0] % 2:
                                v.tensor_copy(cb_s[pcb][:], ps_cb[:]))

                    if 0 <= u - 2 < NU and units[u - 2][1] == FTI - 1:
                        ci_d = units[u - 2][0]
                        for i2 in range(0, DTI, 2):
                            def dmm(t, ex=ex, par=par, i2=i2, FTI=FTI):
                                for j in range(2):
                                    i = i2 + j
                                    for fi in range(FTI):
                                        last = t.matmul(
                                            ps_dn[j][:],
                                            wd[par][:, (fi * DTI + i) * 128:
                                                    (fi * DTI + i + 1) * 128],
                                            h_t[fi][:], start=(fi == 0),
                                            stop=(fi == FTI - 1))
                                return last
                            p.stage(tensor=dmm)

                            def dacc(v, ci=ci_d, i2=i2):
                                v.tensor_add(acc[ci][i2][:], acc[ci][i2][:],
                                             ps_dn[0][:])
                                return v.tensor_add(acc[ci][i2 + 1][:],
                                                    acc[ci][i2 + 1][:],
                                                    ps_dn[1][:])
                            p.stage(vector=dacc)

            def store(s, g=g, chunks=chunks):
                for ci, c in enumerate(chunks):
                    for i in range(DTI):
                        p.dma(s, out_hbm[i, :, csl(c)], acc[ci][i][:])
                return None
            p.stage(sync=store)



def emit_mamba(p, C, W, hbf_hbm, xn32_hbm, ffnacc_hbm, x1_hbm, scratch):
    DI = 2048
    UTI = 16
    ucbf, zbf, dtbf, dtubf, y3bf = (scratch["ucbf"], scratch["zbf"],
                                    scratch["dtbf"], scratch["dtubf"],
                                    scratch["y3bf"])
    sel96 = p.sb([96, 32 * 128], F32, "sel96")

    def selb(g):
        g.memset(sel96[:], 0.0)
        return g.affine_select(
            out=sel96[:].rearrange("p (e c) -> p e c", e=32),
            in_=sel96[:].rearrange("p (e c) -> p e c", e=32),
            pattern=[[-1, 32], [0, 128]], base=-64, channel_multiplier=1,
            compare_op=ALU.not_equal, fill=1.0)
    p.stage(gpsimd=selb)

    cw = p.sb([128, 64], F32, "mcw")
    cb = p.sb([128, 16], F32, "mcb")
    dcol = p.sb([128, 16], F32, "mD")
    dtb = p.sb([128, 16], F32, "mdtb")
    a_t = p.sb([128, 16 * 16], F32, "mA")

    def ldc(s):
        p.dma(s, cw[:], W["conv_w"])
        p.dma(s, cb[:], W["conv_b"])
        p.dma(s, dcol[:], W["D"])
        p.dma(s, dtb[:], W["dt_b"])
        p.dma(s, a_t[:], W["A_log"])
        return None
    p.stage(sync=ldc)
    p.stage(scalar=lambda sc: sc.activation(a_t[:], a_t[:], AF.Exp))
    p.stage(vector=lambda v: v.tensor_scalar(a_t[:], a_t[:], -1.0, None,
                                             ALU.mult))

    with p.scoped():
        hb = [p.sb([128, T], BF16, "mhb") for _ in range(DTI)]
        wip = p.sb([128, DTI * 128], BF16, "mwip")
        ps_x = [p.psum([128, CH], F32, "mpsx") for _ in range(2)]
        u_raw = p.sb([128, T], F32, "muraw")
        cvt = p.sb([128, SEQ], F32, "mcvt")
        ucb = p.sb([128, SEQ], BF16, "mucb")
        zb = p.sb([128, CH], BF16, "mzb")

        def ldh(s):
            for i in range(DTI):
                p.dma(s, hb[i][:], hbf_hbm[i, :, :])
            return None
        p.stage(sync=ldh)

        for fb in range(32):
            is_u = fb < 16
            p.stage(gpsimd=lambda gp, fb=fb: (p.dma(
                gp, wip[:].rearrange("p (a f) -> p a f", a=DTI),
                W["in_proj"].rearrange("(a p) f -> p a f", p=128)
                [:, :, fb * 128:(fb + 1) * 128]), None)[1])

            for c in range(NT):
                def mm(t, fb=fb, c=c):
                    for i in range(DTI):
                        last = t.matmul(ps_x[c % 2][:],
                                        wip[:, i * 128:(i + 1) * 128],
                                        hb[i][:, csl(c)], start=(i == 0),
                                        stop=(i == DTI - 1))
                    return last
                p.stage(tensor=mm)
                if is_u:
                    p.stage(vector=lambda v, c=c: v.tensor_copy(
                        u_raw[:, csl(c)], ps_x[c % 2][:]))
                else:
                    p.stage(scalar=lambda sc, c=c: sc.activation(
                        zb[:], ps_x[c % 2][:], AF.Copy))
                    p.stage(sync=lambda s, fb=fb, c=c: (p.dma(
                        s, zbf[fb - 16, :, csl(c)], zb[:]), None)[1])

            if is_u:
                for r in range(RPC):
                    base = r * SEQ

                    def conv(v, fb=fb, base=base):
                        u = u_raw
                        v.tensor_scalar(cvt[:], u[:, base:base + SEQ],
                                        cw[:, fb * 4 + 3:fb * 4 + 4], None,
                                        ALU.mult)
                        v.scalar_tensor_tensor(
                            cvt[:, 1:SEQ], u[:, base:base + SEQ - 1],
                            cw[:, fb * 4 + 2:fb * 4 + 3], cvt[:, 1:SEQ],
                            ALU.mult, ALU.add)
                        v.scalar_tensor_tensor(
                            cvt[:, 2:SEQ], u[:, base:base + SEQ - 2],
                            cw[:, fb * 4 + 1:fb * 4 + 2], cvt[:, 2:SEQ],
                            ALU.mult, ALU.add)
                        return v.scalar_tensor_tensor(
                            cvt[:, 3:SEQ], u[:, base:base + SEQ - 3],
                            cw[:, fb * 4 + 0:fb * 4 + 1], cvt[:, 3:SEQ],
                            ALU.mult, ALU.add)
                    p.stage(vector=conv)
                    p.stage(scalar=lambda sc, fb=fb: sc.activation(
                        ucb[:], cvt[:], AF.Silu, bias=cb[:, fb:fb + 1]))
                    p.stage(sync=lambda s, fb=fb, base=base: (p.dma(
                        s, ucbf[fb, :, base:base + SEQ], ucb[:]), None)[1])

    xdbl = p.sb([96, T], F32, "mxdbl")
    xdbl_bf = p.sb([64, T], BF16, "mxdblbf")
    with p.scoped():
        wxp = p.sb([128, 16 * 96], BF16, "mwxp")
        uct = [p.sb([128, CH], BF16, "muct") for _ in range(UTI)]
        ps_xd = p.psum([96, CH], F32, "mpsxd")

        p.stage(gpsimd=lambda gp: (p.dma(
            gp, wxp[:].rearrange("p (a f) -> p a f", a=16),
            W["x_proj"].rearrange("(a p) f -> p a f", p=128)), None)[1])

        for c in range(NT):
            def ldu(s, c=c):
                for k in range(UTI):
                    p.dma(s, uct[k][:], ucbf[k, :, csl(c)])
                return None
            p.stage(sync=ldu)

            def mm(t, c=c):
                for k in range(UTI):
                    last = t.matmul(ps_xd[:], wxp[:, k * 96:(k + 1) * 96],
                                    uct[k][:], start=(k == 0), stop=(k == 15))
                return last
            p.stage(tensor=mm)
            p.stage(vector=lambda v, c=c: v.tensor_copy(xdbl[:, csl(c)],
                                                        ps_xd[:]))
            p.stage(scalar=lambda sc, c=c: sc.activation(
                xdbl_bf[:, csl(c)], ps_xd[0:64, :], AF.Copy))

    with p.scoped():
        wdt = p.sb([64, 16 * 128], BF16, "mwdt")
        ps_dt = p.psum([128, CH], F32, "mpsdt")
        ta = p.sb([128, CH], F32, "mta")
        tb = p.sb([128, CH], F32, "mtb")
        td = p.sb([128, CH], F32, "mtd")
        dtf = p.sb([128, CH], F32, "mdtf")
        uct1 = p.sb([128, CH], BF16, "muct1")
        dtb16 = p.sb([128, CH], BF16, "mdtb16")
        dtub16 = p.sb([128, CH], BF16, "mdtub16")

        p.stage(gpsimd=lambda gp: (p.dma(
            gp, wdt[:], W["dt_w"]), None)[1])

        for f in range(16):
            for c in range(NT):
                def mm(t, f=f, c=c):
                    return t.matmul(ps_dt[:], wdt[:, f * 128:(f + 1) * 128],
                                    xdbl_bf[:, csl(c)], start=True, stop=True)
                p.stage(tensor=mm)

                def soft_s(sc, f=f):
                    sc.activation(ta[:], ps_dt[:], AF.Abs, bias=dtb[:, f:f + 1])
                    sc.activation(tb[:], ta[:], AF.Exp, scale=-1.0)
                    sc.activation(ta[:], tb[:], AF.Ln, bias=1.0)
                    return sc.activation(td[:], ps_dt[:], AF.Relu,
                                         bias=dtb[:, f:f + 1])
                p.stage(scalar=soft_s)
                p.stage(sync=lambda s, f=f, c=c: (p.dma(
                    s, uct1[:], ucbf[f, :, csl(c)]), None)[1])

                def fin_v(v):
                    v.tensor_add(dtf[:], ta[:], td[:])
                    v.tensor_copy(dtb16[:], dtf[:])
                    return v.tensor_mul(dtub16[:], dtf[:], uct1[:])
                p.stage(vector=fin_v)

                def st(s, f=f, c=c):
                    p.dma(s, dtbf[f, :, csl(c)], dtb16[:])
                    p.dma(s, dtubf[f, :, csl(c)], dtub16[:])
                    return None
                p.stage(sync=st)

    HS = SEQ // 2
    with p.scoped():
        dt_t = p.sb([128, SEQ], BF16, "mdt")
        dtu_t = p.sb([128, SEQ], BF16, "mdtu")
        ps_b = [p.psum([128, HS], F32, "mpsb") for _ in range(2)]
        ps_c = [p.psum([128, HS], F32, "mpsc") for _ in range(2)]
        da_s = [p.sb([128, HS], BF16, "mda") for _ in range(2)]
        dbu_t = p.sb([128, HS], BF16, "mdbu")
        h_t = [p.sb([128, HS], BF16, "mh") for _ in range(2)]
        hc_t = p.sb([128, HS], BF16, "mhc")
        yacc = p.sb([128, SEQ], BF16, "myacc")
        zero_col = p.sb([128, 1], F32, "mzc")
        uct2 = p.sb([128, SEQ], BF16, "muct2")
        zt2 = p.sb([128, SEQ], BF16, "mzt2")
        szt = p.sb([128, SEQ], F32, "mszt")
        y2t = p.sb([128, SEQ], F32, "my2")
        y3t = p.sb([128, SEQ], BF16, "my3")

        p.stage(gpsimd=lambda g: g.memset(zero_col[:], 0.0))

        for r in range(RPC):
            rsl = slice(r * SEQ, (r + 1) * SEQ)
            for f in range(16):
                def lddt(s, f=f, rsl=rsl):
                    p.dma(s, dt_t[:], dtbf[f, :, rsl])
                    p.dma(s, dtu_t[:], dtubf[f, :, rsl])
                    return None
                def ldz(s, f=f, rsl=rsl):
                    p.dma(s, uct2[:], ucbf[f, :, rsl])
                    p.dma(s, zt2[:], zbf[f, :, rsl])
                    return None
                p.stage(sync=lddt)

                units = [(n, tc) for n in range(16) for tc in range(2)]
                NV = len(units)

                def bcast(t, v, r=r):
                    n, tc = units[v]
                    for j in range(2):
                        jsl = slice(tc * HS + j * CH, tc * HS + (j + 1) * CH)
                        psl = slice(j * CH, (j + 1) * CH)
                        rs = slice(r * SEQ + tc * HS + j * CH,
                                   r * SEQ + tc * HS + (j + 1) * CH)
                        t.matmul(ps_b[v % 2][:, psl],
                                 sel96[:, n * 128:(n + 1) * 128],
                                 xdbl[:, rs], start=True, stop=True)
                        last = t.matmul(ps_c[v % 2][:, psl],
                                        sel96[:, (16 + n) * 128:(17 + n) * 128],
                                        xdbl[:, rs], start=True, stop=True)
                    return last

                for v in range(NV + 1):
                    fns = {}
                    if v < NV:
                        def tfm(t, v=v):
                            return bcast(t, v)
                        fns["tensor"] = tfm

                        def sfm(sc, v=v, f=f):
                            n, tc = units[v]
                            return sc.activation(
                                da_s[v % 2][:], dt_t[:, tc * HS:(tc + 1) * HS],
                                AF.Exp,
                                scale=a_t[:, f * 16 + n:f * 16 + n + 1])
                        fns["scalar"] = sfm
                    if 0 <= v - 1 < NV:
                        vv_ = v - 1

                        def vfm(v_, vv_=vv_):
                            n, tc = units[vv_]
                            tsl = slice(tc * HS, (tc + 1) * HS)
                            v_.tensor_mul(dbu_t[:],
                                          dtu_t[:, tsl], ps_b[vv_ % 2][:])
                            init = (zero_col[:, 0:1] if tc == 0
                                    else h_t[(vv_ - 1) % 2][:, HS - 1:HS])
                            v_.tensor_tensor_scan(h_t[vv_ % 2][:], da_s[vv_ % 2][:],
                                                  dbu_t[:], init,
                                                  ALU.mult, ALU.add)
                            v_.tensor_mul(hc_t[:], h_t[vv_ % 2][:],
                                          ps_c[vv_ % 2][:])
                            if units[vv_][0] == 0:
                                return v_.tensor_copy(yacc[:, tsl], hc_t[:])
                            return v_.tensor_add(yacc[:, tsl], yacc[:, tsl],
                                                 hc_t[:])
                        fns["vector"] = vfm
                    if v == NV:
                        fns["sync"] = lambda s, f=f, rsl=rsl: ldz(s, f, rsl)
                    p.stage(**fns)

                p.stage(scalar=lambda sc: sc.activation(szt[:], zt2[:], AF.Silu))

                def fin_v(v, f=f):
                    v.scalar_tensor_tensor(y2t[:], uct2[:], dcol[:, f:f + 1],
                                           yacc[:], ALU.mult, ALU.add)
                    return v.tensor_mul(y3t[:], y2t[:], szt[:])
                p.stage(vector=fin_v)
                p.stage(sync=lambda s, f=f, rsl=rsl: (p.dma(
                    s, y3bf[f, :, rsl], y3t[:]), None)[1])

    with p.scoped():
        wop = p.sb([128, 16 * 1024], BF16, "mwop")
        y3c = [p.sb([128, CH], BF16, "my3c") for _ in range(16)]
        xnc = p.sb([128, CH], F32, "mxnc")
        fac = p.sb([128, CH], F32, "mfac")
        ps_o = [p.psum([128, CH], F32, "mpso") for _ in range(2)]
        x1t = p.sb([128, CH], F32, "mx1")

        p.stage(gpsimd=lambda gp: (p.dma(
            gp, wop[:].rearrange("p (a f) -> p a f", a=16),
            W["out_proj"].rearrange("(a p) f -> p a f", p=128)), None)[1])

        for c in range(NT):
            def ldy(s, c=c):
                for k in range(16):
                    p.dma(s, y3c[k][:], y3bf[k, :, csl(c)])
                return None
            p.stage(sync=ldy)

            for i in range(DTI):
                def mm(t, c=c, i=i):
                    for k in range(16):
                        last = t.matmul(ps_o[i % 2][:],
                                        wop[:, (k * 8 + i) * 128:
                                            (k * 8 + i + 1) * 128],
                                        y3c[k][:], start=(k == 0), stop=(k == 15))
                    return last
                p.stage(tensor=mm)

                def ldxf(s, c=c, i=i):
                    p.dma(s, xnc[:], xn32_hbm[i, :, csl(c)])
                    p.dma(s, fac[:], ffnacc_hbm[i, :, csl(c)])
                    return None
                p.stage(sync=ldxf)

                def fin(v, c=c, i=i):
                    v.tensor_add(x1t[:], ps_o[i % 2][:], xnc[:])
                    return v.tensor_add(x1t[:], x1t[:], fac[:])
                p.stage(vector=fin)
                p.stage(sync=lambda s, c=c, i=i: (p.dma(
                    s, x1_hbm[i, :, csl(c)], x1t[:]), None)[1])



def declare_w(p, names):
    win = {}
    for nm, sh in names:
        win[nm] = p.dram(nm, sh, F32, kind="ExternalInput")
    return win


def emit_block0(p, C, win, x, x1, debug=False):
    dk = "ExternalOutput" if debug else "Internal"
    xn32 = p.dram("b0_xn32", [DTI, 128, T], F32, kind=dk)
    xnbf = p.dram("b0_xnbf", [DTI, 128, T], BF16)
    hbf = p.dram("b0_hbf", [DTI, 128, T], BF16, kind=dk)
    ffnacc = p.dram("b0_ffnacc", [DTI, 128, T], F32, kind=dk)
    scratch = dict(
        ucbf=p.dram("b0_ucbf", [16, 128, T], BF16, kind=dk),
        zbf=p.dram("b0_zbf", [16, 128, T], BF16),
        dtbf=p.dram("b0_dtbf", [16, 128, T], BF16, kind=dk),
        dtubf=p.dram("b0_dtubf", [16, 128, T], BF16),
        y3bf=p.dram("b0_y3bf", [16, 128, T], BF16, kind=dk),
    )

    emit_ln(p, C, x, xn32, xnbf, hbf_hbm=hbf, normw_hbm=win["l0a_norm_w"])

    with p.scoped():
        combT = p.sb([32, T], F32, "combT")
        emit_router_grouped(p, C, xn32, win["l0f_rwT"], win["l0f_bias"], combT,
                            E=16, GRP=4, K=4, scale=2.5)
        experts = []
        for e in range(16):
            experts.append(dict(gate=win["l0f_gate"][e], up=win["l0f_up"][e],
                                down=win["l0f_down"][e], e=e, FTI=8))
        for half in range(2):
            fs = slice(half * 1024, (half + 1) * 1024)
            experts.append(dict(gate=win["l0f_shg"][:, fs],
                                up=win["l0f_shu"][:, fs],
                                down=win["l0f_shd"][fs, :], e=None, FTI=8))
        emit_moe_ffn(p, C, xnbf, x, ffnacc, combT, experts, tag="m0f")

    W = dict(in_proj=win["l0a_in_proj"], conv_w=win["l0a_conv_w"],
             conv_b=win["l0a_conv_b"], x_proj=win["l0a_x_proj"],
             dt_w=win["l0a_dt_w"], dt_b=win["l0a_dt_b"], A_log=win["l0a_A_log"],
             D=win["l0a_D"], out_proj=win["l0a_out_proj"])
    with p.scoped():
        emit_mamba(p, C, W, hbf, xn32, ffnacc, x1, scratch)


W_BLOCK0 = [
    ("l0a_norm_w", [128, DTI]), ("l0a_in_proj", [D, 4096]),
    ("l0a_conv_w", [128, 64]),
    ("l0a_conv_b", [128, 16]), ("l0a_x_proj", [2048, 96]),
    ("l0a_dt_w", [64, 2048]),
    ("l0a_dt_b", [128, 16]), ("l0a_A_log", [128, 256]), ("l0a_D", [128, 16]),
    ("l0a_out_proj", [2048, D]),
    ("l0f_rwT", [D, 16]), ("l0f_bias", [16]),
    ("l0f_gate", [16, D, 1024]), ("l0f_up", [16, D, 1024]),
    ("l0f_down", [16, 1024, D]),
    ("l0f_shg", [D, 2048]), ("l0f_shu", [D, 2048]), ("l0f_shd", [2048, D]),
]


def build_block0(debug=False):
    p = Prog()
    x = p.dram("x", [DTI, 128, T], F32, kind="ExternalInput")
    x1 = p.dram("x1", [DTI, 128, T], F32, kind="ExternalOutput")
    win = declare_w(p, W_BLOCK0)
    C = emit_consts(p)
    emit_block0(p, C, win, x, x1, debug=debug)
    p.final_wait()
    return p.finish()



def _fm_shard(x_full):
    out = []
    for c in range(NCORE):
        xc = x_full[2 * c:2 * c + 2]
        fm = np.ascontiguousarray(
            xc.transpose(2, 0, 1).reshape(DTI, 128, T))
        out.append(fm)
    return out


def _fm_unshard(shards):
    x = np.empty((16, SEQ, D), np.float32)
    for c in range(NCORE):
        fm = shards[c].reshape(D, RPC, SEQ)
        x[2 * c:2 * c + 2] = fm.transpose(1, 2, 0)
    return x


def prep_block0_inputs(inputs):
    f32 = np.float32
    def col128(v):
        return np.ascontiguousarray(np.asarray(v, f32).reshape(-1, 128).T)

    w = {}
    w["l0a_norm_w"] = col128(inputs["L0_attn_norm_w"])
    w["l0a_in_proj"] = np.ascontiguousarray(inputs["L0_attn_in_proj_w"], f32)
    w["l0a_conv_w"] = np.ascontiguousarray(
        np.asarray(inputs["L0_attn_conv_w"], f32).reshape(16, 128, 4)
        .transpose(1, 0, 2).reshape(128, 64))
    w["l0a_conv_b"] = col128(inputs["L0_attn_conv_b"])
    w["l0a_x_proj"] = np.ascontiguousarray(inputs["L0_attn_x_proj_w"], f32)
    w["l0a_dt_w"] = np.ascontiguousarray(inputs["L0_attn_dt_proj_w"], f32)
    w["l0a_dt_b"] = col128(inputs["L0_attn_dt_proj_b"])
    w["l0a_A_log"] = np.ascontiguousarray(
        np.asarray(inputs["L0_attn_A_log"], f32).reshape(16, 128, 16)
        .transpose(1, 0, 2).reshape(128, 256))
    w["l0a_D"] = col128(inputs["L0_attn_D"])
    w["l0a_out_proj"] = np.ascontiguousarray(inputs["L0_attn_out_proj_w"], f32)
    w["l0f_rwT"] = np.ascontiguousarray(inputs["L0_ffn_router_w"].T, f32)
    w["l0f_bias"] = np.ascontiguousarray(inputs["L0_ffn_bias"], f32)
    w["l0f_gate"] = np.ascontiguousarray(inputs["L0_ffn_gate_w"], f32)
    w["l0f_up"] = np.ascontiguousarray(inputs["L0_ffn_up_w"], f32)
    w["l0f_down"] = np.ascontiguousarray(inputs["L0_ffn_down_w"], f32)
    w["l0f_shg"] = np.ascontiguousarray(inputs["L0_ffn_sh_gate_w"], f32)
    w["l0f_shu"] = np.ascontiguousarray(inputs["L0_ffn_sh_up_w"], f32)
    w["l0f_shd"] = np.ascontiguousarray(inputs["L0_ffn_sh_down_w"], f32)
    return w


def run_block0(inputs, x_full, debug=False):
    w = prep_block0_inputs(inputs)
    shards = _fm_shard(x_full)
    nc = build_block0(debug=debug)
    in_maps = [dict(w, x=shards[c]) for c in range(NCORE)]
    res = run_bass_kernel_spmd(nc, in_maps, core_ids=list(range(NCORE)))
    if debug:
        return res
    return _fm_unshard([res.results[c]["x1"] for c in range(NCORE)])


def kernel(**inputs):
    _ = inputs.get("L0_attn_hormone_vectors")
    x = np.asarray(inputs["L0_attn_x"], np.float32)
    wA = dict(prep_block0_inputs(inputs))
    w1 = prep_block1_inputs(inputs)
    for k in W_BLOCK1F:
        wA[k[0]] = w1[k[0]]
    wB = {k[0]: w1[k[0]] for k in W_BLOCK1A}
    wC = dict(prep_block2_inputs(inputs))
    wC.update(prep_block3_inputs(inputs))

    shards = _fm_shard(x)
    ncA = build_blockA()
    resA = run_bass_kernel_spmd(ncA, [dict(wA, x=shards[c]) for c in range(NCORE)],
                                core_ids=list(range(NCORE)))
    x1 = _fm_unshard([resA.results[c]["x1"] for c in range(NCORE)])
    y1p_shards = [resA.results[c]["y1p"] for c in range(NCORE)]

    ncB = build_blockB()
    ashards = _fm_shard_tau(x1)
    resB = run_bass_kernel_spmd(ncB, [dict(wB, xa=ashards[c])
                                      for c in range(NCORE)],
                               core_ids=list(range(NCORE)))
    att = _fm_unshard_tau([resB.results[c]["att1"] for c in range(NCORE)])
    att_shards = _fm_shard(att)

    ncC = build_blockC()
    resC = run_bass_kernel_spmd(
        ncC, [dict(wC, y1p=y1p_shards[c], att=att_shards[c])
              for c in range(NCORE)], core_ids=list(range(NCORE)))
    return _fm_unshard([resC.results[c]["x4"] for c in range(NCORE)])



def emit_add(p, a_hbm, b_hbm, out_hbm, nt=DTI):
    with p.scoped():
        at = p.sb([128, CH], F32, "adda")
        bt = p.sb([128, CH], F32, "addb")
        ot = p.sb([128, CH], F32, "addo")
        for i in range(nt):
            for c in range(NT):
                def ld(s, i=i, c=c):
                    p.dma(s, at[:], a_hbm[i, :, csl(c)])
                    p.dma(s, bt[:], b_hbm[i, :, csl(c)])
                    return None
                p.stage(sync=ld)
                p.stage(vector=lambda v: v.tensor_add(ot[:], at[:], bt[:]))
                p.stage(sync=lambda s, i=i, c=c: (p.dma(
                    s, out_hbm[i, :, csl(c)], ot[:]), None)[1])



def emit_router_top2softmax(p, C, xn32_hbm, rwT_hbm, combT, E):
    ones_row, identf = C["ones_row"], C["identf"]
    with p.scoped():
        rwt = p.sb([128, DTI * E], F32, "r2wt")
        xs = [p.sb([128, CH], F32, "r2xs") for _ in range(DTI)]
        lg = p.psum([128, CH], F32, "r2lg")
        lgs = p.sb([128, E], F32, "r2lgs")
        m1 = p.sb([128, 1], F32, "r2m1")
        m2 = p.sb([128, 1], F32, "r2m2")
        sel1 = p.sb([128, E], F32, "r2s1")
        sel2 = p.sb([128, E], F32, "r2s2")
        msk = p.sb([128, E], F32, "r2msk")
        w1c = p.sb([128, 1], F32, "r2w1")
        comb = p.sb([128, E], F32, "r2comb")
        ct_ps = p.psum([E, CH], F32, "r2ct")

        p.stage(sync=lambda s: (p.dma(
            s, rwt[:].rearrange("p (a e) -> p a e", a=DTI),
            rwT_hbm.rearrange("(a p) e -> p a e", p=128)), None)[1])
        p.stage(vector=lambda v: v.memset(combT[:], 0.0))

        for c in range(NT):
            def ld(s, c=c):
                for i in range(DTI):
                    p.dma(s, xs[i][:], xn32_hbm[i, :, csl(c)])
                return None
            p.stage(sync=ld)

            for tt in range(4):
                def mm(t, c=c, tt=tt):
                    for i in range(DTI):
                        last = t.matmul(
                            lg[0:128, 0:E], xs[i][:, tt * 128:(tt + 1) * 128],
                            rwt[:, i * E:(i + 1) * E],
                            start=(i == 0), stop=(i == DTI - 1))
                    return last
                p.stage(tensor=mm)

                def route_v(v):
                    def W(instr):
                        instr.then_inc(p.sem("vv"), 1)
                        p.cnt["vv"] += 1
                        v.wait_ge(p.sem("vv"), p.cnt["vv"])
                        return instr
                    W(v.tensor_copy(lgs[:], lg[0:128, 0:E]))
                    W(v.tensor_reduce(m1[:], lgs[:], AX.X, ALU.max))
                    W(v.tensor_scalar(sel1[:], lgs[:], m1[:, 0:1], None,
                                      ALU.is_ge))
                    W(v.scalar_tensor_tensor(msk[:], sel1[:], -BIG, lgs[:],
                                             ALU.mult, ALU.add))
                    W(v.tensor_reduce(m2[:], msk[:], AX.X, ALU.max))
                    W(v.tensor_scalar(sel2[:], msk[:], m2[:, 0:1], None,
                                      ALU.is_ge))
                    W(v.tensor_sub(m2[:], m2[:], m1[:]))
                    return None
                p.stage(vector=route_v)
                p.stage(scalar=lambda sc: sc.activation(m2[:], m2[:], AF.Exp))

                def route_v2(v):
                    def W(instr):
                        instr.then_inc(p.sem("vv"), 1)
                        p.cnt["vv"] += 1
                        v.wait_ge(p.sem("vv"), p.cnt["vv"])
                        return instr
                    W(v.tensor_scalar(w1c[:], m2[:], 1.0, None, ALU.add))
                    W(v.reciprocal(w1c[:], w1c[:]))
                    W(v.tensor_scalar(comb[:], sel1[:], w1c[:, 0:1], None,
                                      ALU.mult))
                    W(v.tensor_scalar(w1c[:], w1c[:], -1.0, None, ALU.mult))
                    W(v.tensor_scalar(w1c[:], w1c[:], 1.0, None, ALU.add))
                    return v.scalar_tensor_tensor(comb[:], sel2[:], w1c[:, 0:1],
                                                  comb[:], ALU.mult, ALU.add)
                p.stage(vector=route_v2)

                p.stage(tensor=lambda t: t.transpose(ct_ps[0:E, 0:128],
                                                     comb[:, 0:E], identf[:]))
                p.stage(vector=lambda v, c=c, tt=tt: v.tensor_copy(
                    combT[0:E, (c * 4 + tt) * 128:(c * 4 + tt + 1) * 128],
                    ct_ps[0:E, 0:128]))



def emit_l2_attn(p, C, W2, xnbf_hbm, ffnacc_hbm, out_hbm):
    ones_row = C["ones_row"]
    ident = C["ident"]
    CT = 2
    with p.scoped():
        bt = [p.sb([128, 1024], F32, "l2bt") for _ in range(CT)]
        cf = [p.sb([128, 256], F32, "l2cf") for _ in range(CT)]
        ps_w = p.psum([128, CH], F32, "l2psw")
        qwb = p.sb([128, DTI * 256], BF16, "l2qw")
        kwb = p.sb([128, DTI * 256], BF16, "l2kw")
        vwb = p.sb([128, DTI * 256], BF16, "l2vw")
        owT = p.sb([128, CT * 1024], BF16, "l2ow")
        sc2 = p.sb([1, 2], F32, "l2sc")
        ldc = p.sb([128, 1], F32, "l2ldc")
        osc = p.sb([128, 1], F32, "l2osc")
        ps_b1 = p.psum([128, CH], F32, "l2psb")

        def ldsc(s):
            p.dma(s, sc2[:, 0:1], W2["decay_logit"][None, :])
            p.dma(s, sc2[:, 1:2], W2["out_scale"][None, :])
            return None
        p.stage(sync=ldsc)
        p.stage(scalar=lambda sc: sc.activation(sc2[:, 0:1], sc2[:, 0:1],
                                                AF.Sigmoid))

        def lnd(sc):
            return sc.activation(sc2[:, 0:1], sc2[:, 0:1], AF.Ln)
        p.stage(scalar=lnd)

        def bc1(t):
            t.matmul(ps_b1[:, 0:1], ones_row[0:1, :], sc2[:, 0:1],
                     start=True, stop=True)
            return t.matmul(ps_b1[:, 1:2], ones_row[0:1, :], sc2[:, 1:2],
                            start=True, stop=True)
        p.stage(tensor=bc1)

        def cp1(v):
            v.tensor_copy(ldc[:], ps_b1[:, 0:1])
            return v.tensor_copy(osc[:], ps_b1[:, 1:2])
        p.stage(vector=cp1)

        wdec = [p.sb([128, CH], F32, "l2wd") for _ in range(19)]
        for mi, m in enumerate(range(-3, 16)):
            def bw(g, mi=mi, m=m):
                return g.iota(wdec[mi][:], [[-1, CH]], base=128 * m - 1,
                              channel_multiplier=1,
                              allow_small_or_imprecise_dtypes=True)
            p.stage(gpsimd=bw)

            def sw(v, mi=mi):
                return v.tensor_scalar(wdec[mi][:], wdec[mi][:], ldc[:, 0:1],
                                       None, ALU.mult)
            p.stage(vector=sw)
            p.stage(scalar=lambda sc, mi=mi: sc.activation(
                wdec[mi][:], wdec[mi][:], AF.Exp))

            def msk(g, mi=mi, m=m):
                return g.affine_select(out=wdec[mi][:], in_=wdec[mi][:],
                                       pattern=[[-1, CH]], base=128 * m,
                                       channel_multiplier=1,
                                       compare_op=ALU.is_gt, fill=0.0)
            p.stage(gpsimd=msk)

        def ldb(s):
            for j in range(CT):
                p.dma(s, bt[j][:], W2["basis_T"][j * 128:(j + 1) * 128, :])
            return None
        p.stage(sync=ldb)

        for nm, dst in (("qcT", qwb), ("kcT", kwb), ("vcT", vwb)):
            def ldcf(s, nm=nm):
                for j in range(CT):
                    p.dma(s, cf[j][:, 0:256], W2[nm][j * 128:(j + 1) * 128, :])
                return None
            p.stage(sync=ldcf)
            for vt in range(DTI):
                def mmw(t, vt=vt):
                    for j in range(CT):
                        last = t.matmul(ps_w[:, 0:256],
                                        bt[j][:, vt * 128:(vt + 1) * 128],
                                        cf[j][:, 0:256], start=(j == 0),
                                        stop=(j == CT - 1))
                    return last
                p.stage(tensor=mmw)
                p.stage(scalar=lambda sc, vt=vt, dst=dst: sc.activation(
                    dst[:, vt * 256:(vt + 1) * 256], ps_w[:, 0:256], AF.Copy))

        def ldcf2(s):
            for j in range(CT):
                p.dma(s, cf[j][:, 0:256], W2["ocT"][j * 128:(j + 1) * 128, :])
            return None
        p.stage(sync=ldcf2)
        for ct in range(CT):
            for vc in range(2):
                def mmo(t, ct=ct, vc=vc):
                    for j in range(CT):
                        last = t.matmul(
                            ps_w[:], cf[j][:, ct * 128:(ct + 1) * 128],
                            bt[j][:, vc * 512:(vc + 1) * 512],
                            start=(j == 0), stop=(j == CT - 1))
                    return last
                p.stage(tensor=mmo)
                p.stage(scalar=lambda sc, ct=ct, vc=vc: sc.activation(
                    owT[:, ct * 1024 + vc * 512: ct * 1024 + (vc + 1) * 512],
                    ps_w[:], AF.Copy))

        xnb = [p.sb([128, CH], BF16, "l2xnb") for _ in range(DTI)]
        qf = [p.sb([128, T], BF16, "l2qf") for _ in range(CT)]
        kf = [p.sb([128, T], BF16, "l2kf") for _ in range(CT)]
        vf = [p.sb([128, T], BF16, "l2vf") for _ in range(CT)]
        for c in range(NT):
            def ldx(s, c=c):
                for i in range(DTI):
                    p.dma(s, xnb[i][:], xnbf_hbm[i, :, csl(c)])
                return None
            p.stage(sync=ldx)
            for wsb, dst in ((qwb, qf), (kwb, kf), (vwb, vf)):
                for ct in range(CT):
                    def mmp(t, wsb=wsb, ct=ct, c=c):
                        for i in range(DTI):
                            last = t.matmul(
                                ps_w[:],
                                wsb[:, i * 256 + ct * 128:
                                    i * 256 + (ct + 1) * 128],
                                xnb[i][:], start=(i == 0), stop=(i == DTI - 1))
                        return last
                    p.stage(tensor=mmp)
                    p.stage(scalar=lambda sc, dst=dst, ct=ct, c=c: sc.activation(
                        dst[ct][:, csl(c)], ps_w[:], AF.Copy))

        vtm = [p.sb([128, 16 * 256], BF16, "l2vtm") for _ in range(RPC)]
        ps_t = p.psum([128, 128], BF16, "l2pst")
        for r in range(RPC):
            for t2 in range(16):
                for ct in range(CT):
                    def tp(t, r=r, t2=t2, ct=ct):
                        return t.transpose(
                            ps_t[:], vf[ct][:, r * SEQ + t2 * 128:
                                            r * SEQ + (t2 + 1) * 128], ident[:])
                    p.stage(tensor=tp)
                    p.stage(vector=lambda v, r=r, t2=t2, ct=ct: v.tensor_copy(
                        vtm[r][:, t2 * 256 + ct * 128:t2 * 256 + (ct + 1) * 128],
                        ps_t[:]))

        ps_s = p.psum([128, CH], F32, "l2pss")
        ps_r = [p.psum([128, CH], F32, "l2psr") for _ in range(CT)]
        sw_t = p.sb([128, CH], BF16, "l2sw")
        retr = [p.sb([128, CH], BF16, "l2retr") for _ in range(CT)]
        fac = p.sb([128, CH], F32, "l2fac")
        ot = p.sb([128, CH], F32, "l2ot")
        for r in range(RPC):
            for t1c in range(4):
                t2lo = max(0, 4 * t1c - 3)
                for idx, t2 in enumerate(range(t2lo, 16)):
                    def mms(t, r=r, t1c=t1c, t2=t2):
                        for ct in range(CT):
                            last = t.matmul(
                                ps_s[:],
                                kf[ct][:, r * SEQ + t2 * 128:
                                       r * SEQ + (t2 + 1) * 128],
                                qf[ct][:, r * SEQ + t1c * CH:
                                       r * SEQ + (t1c + 1) * CH],
                                start=(ct == 0), stop=(ct == CT - 1))
                        return last
                    p.stage(tensor=mms)

                    mi = t2 - 4 * t1c + 3
                    p.stage(vector=lambda v, mi=mi: v.tensor_mul(
                        sw_t[:], ps_s[:], wdec[mi][:]))

                    def mmr(t, r=r, t2=t2, idx=idx, last_t2=(t2 == 15)):
                        for ct in range(CT):
                            last = t.matmul(
                                ps_r[ct][:],
                                vtm[r][:, t2 * 256 + ct * 128:
                                       t2 * 256 + (ct + 1) * 128],
                                sw_t[:], start=(idx == 0), stop=last_t2,
                                skip_group_check=True)
                        return last
                    p.stage(tensor=mmr)

                def evr(sc, r=r):
                    for ct in range(CT):
                        last = sc.activation(retr[ct][:], ps_r[ct][:], AF.Copy,
                                             scale=osc[:, 0:1])
                    return last
                p.stage(scalar=evr)

                for vt in range(DTI):
                    def mmo2(t, vt=vt):
                        for ct in range(CT):
                            last = t.matmul(
                                ps_w[:],
                                owT[:, ct * 1024 + vt * 128:
                                    ct * 1024 + (vt + 1) * 128],
                                retr[ct][:], start=(ct == 0), stop=(ct == CT - 1))
                        return last
                    p.stage(tensor=mmo2)

                    cg = r * 4 + t1c
                    def ldf(s, vt=vt, cg=cg):
                        p.dma(s, fac[:], ffnacc_hbm[vt, :, csl(cg)])
                        return None
                    p.stage(sync=ldf)
                    p.stage(vector=lambda v: v.tensor_add(ot[:], ps_w[:],
                                                          fac[:]))
                    p.stage(sync=lambda s, vt=vt, cg=cg: (p.dma(
                        s, out_hbm[vt, :, csl(cg)], ot[:]), None)[1])



W_BLOCK2 = [
    ("l2a_basis_T", [256, 1024]), ("l2a_qcT", [256, 256]),
    ("l2a_kcT", [256, 256]), ("l2a_vcT", [256, 256]), ("l2a_ocT", [256, 256]),
    ("l2a_decay", [1]), ("l2a_oscale", [1]),
    ("l2f_rwT", [D, 16]),
    ("l2f_w1T", [16, D, 512]), ("l2f_w2T", [16, 512, D]),
    ("l2f_sfc1T", [D, 2048]), ("l2f_sfc2T", [D, 2048]),
    ("l2f_sfc3T", [2048, D]),
]


def emit_block2(p, C, win, x, x3):
    xn32 = p.dram("b2_xn32", [DTI, 128, T], F32)
    xnbf = p.dram("b2_xnbf", [DTI, 128, T], BF16)
    ffnacc = p.dram("b2_ffnacc", [DTI, 128, T], F32)

    emit_ln(p, C, x, xn32, xnbf)
    with p.scoped():
        combT = p.sb([32, T], F32, "combT")
        emit_router_top2softmax(p, C, xn32, win["l2f_rwT"], combT, E=16)
        experts = []
        for e in range(16):
            experts.append(dict(gate=win["l2f_w1T"][e], up=None,
                                down=win["l2f_w2T"][e], e=e, FTI=4))
        for half in range(2):
            fs = slice(half * 1024, (half + 1) * 1024)
            experts.append(dict(gate=win["l2f_sfc1T"][:, fs],
                                up=win["l2f_sfc2T"][:, fs],
                                down=win["l2f_sfc3T"][fs, :], e=None, FTI=8))
        emit_moe_ffn(p, C, xnbf, x, ffnacc, combT, experts, tag="m2f")

    W2 = dict(basis_T=win["l2a_basis_T"], qcT=win["l2a_qcT"], kcT=win["l2a_kcT"],
              vcT=win["l2a_vcT"], ocT=win["l2a_ocT"],
              decay_logit=win["l2a_decay"], out_scale=win["l2a_oscale"])
    emit_l2_attn(p, C, W2, xnbf, ffnacc, x3)


def build_block2(debug=False, add_in=False):
    p = Prog()
    x_in = p.dram("x", [DTI, 128, T], F32, kind="ExternalInput")
    if add_in:
        x2b = p.dram("x2b", [DTI, 128, T], F32, kind="ExternalInput")
        x = p.dram("xsum", [DTI, 128, T], F32)
    else:
        x2b = None
        x = x_in
    x3 = p.dram("x3", [DTI, 128, T], F32, kind="ExternalOutput")
    win = declare_w(p, W_BLOCK2)
    C = emit_consts(p)
    if add_in:
        emit_add(p, x_in, x2b, x)
    emit_block2(p, C, win, x, x3)
    p.final_wait()
    return p.finish()


def prep_block2_inputs(inputs):
    f32 = np.float32
    w = {}
    w["l2a_basis_T"] = np.ascontiguousarray(inputs["L2_attn_basis"].T, f32)
    w["l2a_qcT"] = np.ascontiguousarray(inputs["L2_attn_q_coeffs"].T, f32)
    w["l2a_kcT"] = np.ascontiguousarray(inputs["L2_attn_k_coeffs"].T, f32)
    w["l2a_vcT"] = np.ascontiguousarray(inputs["L2_attn_v_coeffs"].T, f32)
    w["l2a_ocT"] = np.ascontiguousarray(inputs["L2_attn_o_coeffs"].T, f32)
    w["l2a_decay"] = np.asarray(inputs["L2_attn_decay_logit"], f32).reshape(1)
    w["l2a_oscale"] = np.asarray(inputs["L2_attn_out_scale"], f32).reshape(1)
    w["l2f_rwT"] = np.ascontiguousarray(inputs["L2_ffn_gate_w"].T, f32)
    w["l2f_w1T"] = np.ascontiguousarray(
        np.asarray(inputs["L2_ffn_w1"], f32).transpose(0, 2, 1))
    w["l2f_w2T"] = np.ascontiguousarray(
        np.asarray(inputs["L2_ffn_w2"], f32).transpose(0, 2, 1))
    w["l2f_sfc1T"] = np.ascontiguousarray(inputs["L2_ffn_sfc1"].T, f32)
    w["l2f_sfc2T"] = np.ascontiguousarray(inputs["L2_ffn_sfc2"].T, f32)
    w["l2f_sfc3T"] = np.ascontiguousarray(inputs["L2_ffn_sfc3"].T, f32)
    return w


def run_block2(inputs, x_full, x2b_full=None, debug=False):
    w = prep_block2_inputs(inputs)
    shards = _fm_shard(x_full)
    nc = build_block2(debug=debug, add_in=x2b_full is not None)
    if x2b_full is not None:
        shards2 = _fm_shard(x2b_full)
        in_maps = [dict(w, x=shards[c], x2b=shards2[c]) for c in range(NCORE)]
    else:
        in_maps = [dict(w, x=shards[c]) for c in range(NCORE)]
    res = run_bass_kernel_spmd(nc, in_maps, core_ids=list(range(NCORE)))
    if debug:
        return res
    return _fm_unshard([res.results[c]["x3"] for c in range(NCORE)])



def emit_l3_attn(p, C, W3, xnbf_hbm, ffnacc_hbm, out_hbm, apre_hbm):
    ones_row = C["ones_row"]
    with p.scoped():
        xnb = [p.sb([128, SEQ], BF16, "l3xnb") for _ in range(DTI)]
        onesb = p.sb([128, 1], BF16, "l3ones")
        pen = [p.sb([128, CH], F32, "l3pen") for _ in range(4)]
        qf = [p.sb([128, SEQ], BF16, "l3qf") for _ in range(DTI)]
        kf = [p.sb([128, SEQ], BF16, "l3kf") for _ in range(DTI)]
        vtm = p.sb([128, 16 * 1024], BF16, "l3vtm")
        ps_p = p.psum([128, CH], F32, "l3psp")
        ps_s = p.psum([128, CH], F32, "l3pss")
        ps_pv = p.psum([64, CH], F32, "l3pspv")
        ps_den = p.psum([1, CH], F32, "l3psden")
        ps_db = p.psum([64, CH], F32, "l3psdb")
        pt_t = p.sb([128, CH], BF16, "l3pt")
        rd_t = p.sb([1, CH], F32, "l3rd")
        at_s = p.sb([64, CH], F32, "l3ats")
        bm8 = p.sb([128, 1], F32, "l3bm8")
        at_b = p.sb([64, CH], BF16, "l3atb")
        fac = p.sb([128, CH], F32, "l3fac")
        ot = p.sb([128, CH], F32, "l3ot")

        def mkpen(g):
            for m in range(4):
                g.memset(pen[m][:], -50.0)
                last = g.affine_select(out=pen[m][:], in_=pen[m][:],
                                       pattern=[[-1, CH]], base=128 * m,
                                       channel_multiplier=1,
                                       compare_op=ALU.is_gt, fill=0.0)
            g.memset(onesb[:], 1.0)
            g.memset(bm8[:], -8.0)
            return last
        p.stage(gpsimd=mkpen)

        for r in range(RPC):
            rsl = slice(r * SEQ, (r + 1) * SEQ)
            def ldx(s, rsl=rsl):
                for i in range(DTI):
                    p.dma(s, xnb[i][:], xnbf_hbm[i, :, rsl])
                return None
            p.stage(sync=ldx)
            for nm, dst, scl in (("Wq", qf, 0.125), ("Wk", kf, 0.125)):
                with p.scoped():
                    wt = p.sb([128, DTI * 1024], BF16, "l3w")
                    p.stage(gpsimd=lambda gp, nm=nm: (p.dma(
                        gp, wt[:].rearrange("p (a f) -> p a f", a=DTI),
                        W3[nm].rearrange("(a p) f -> p a f", p=128)), None)[1])
                    for fo in range(DTI):
                        for cc in range(4):
                            def mmq(t, fo=fo, cc=cc, r=r):
                                for i in range(DTI):
                                    last = t.matmul(
                                        ps_p[:],
                                        wt[:, (i * 8 + fo) * 128:
                                           (i * 8 + fo + 1) * 128],
                                        xnb[i][:, cc * CH:(cc + 1) * CH],
                                        start=(i == 0), stop=(i == DTI - 1))
                                return last
                            p.stage(tensor=mmq)
                            p.stage(scalar=lambda sc, dst=dst, fo=fo, cc=cc,
                                    scl=scl: sc.activation(
                                        dst[fo][:, cc * CH:(cc + 1) * CH],
                                        ps_p[:], AF.Copy, scale=scl))

            with p.scoped():
                wt = p.sb([128, DTI * 1024], BF16, "l3wv")
                p.stage(gpsimd=lambda gp: (p.dma(
                    gp, wt[:].rearrange("p (a f) -> p a f", a=DTI),
                    W3["Wv"].rearrange("(a p) f -> p a f", p=128)), None)[1])
                for t2 in range(16):
                    for dc in range(2):
                        def mmv(t, t2=t2, dc=dc, r=r):
                            for i in range(DTI):
                                last = t.matmul(
                                    ps_p[:],
                                    xnb[i][:, t2 * 128:(t2 + 1) * 128],
                                    wt[:, i * 1024 + dc * CH:
                                       i * 1024 + (dc + 1) * CH],
                                    start=(i == 0), stop=(i == DTI - 1))
                            return last
                        p.stage(tensor=mmv)
                        p.stage(scalar=lambda sc, t2=t2, dc=dc: sc.activation(
                            vtm[:, t2 * 1024 + dc * CH:t2 * 1024 + (dc + 1) * CH],
                            ps_p[:], AF.Copy))

            for h in range(16):
                hi, ho = h // 2, (h % 2) * 64
                for t1c in range(4):
                    n_t2 = 4 * t1c + 4
                    for t2 in range(n_t2):
                        def mms(t, h=h, hi=hi, ho=ho, t1c=t1c, t2=t2, r=r):
                            return t.matmul(
                                ps_s[:],
                                kf[hi][ho:ho + 64,
                                       t2 * 128:(t2 + 1) * 128],
                                qf[hi][ho:ho + 64,
                                       t1c * CH:(t1c + 1) * CH],
                                start=True, stop=True)
                        p.stage(tensor=mms)
                        m = t2 - 4 * t1c
                        if m >= 0:
                            p.stage(vector=lambda v, m=m: v.tensor_add(
                                ps_s[:], ps_s[:], pen[m][:]))
                        p.stage(scalar=lambda sc: sc.activation(
                            pt_t[:], ps_s[:], AF.Exp, bias=bm8[:, 0:1]))

                        def mmpv(t, t2=t2, h=h, first=(t2 == 0),
                                 last_=(t2 == n_t2 - 1)):
                            t.matmul(ps_pv[:],
                                     vtm[:, t2 * 1024 + h * 64:
                                         t2 * 1024 + h * 64 + 64],
                                     pt_t[:], start=first, stop=last_,
                                     skip_group_check=True)
                            return t.matmul(ps_den[:], onesb[:], pt_t[:],
                                            start=first, stop=last_,
                                            skip_group_check=True)
                        p.stage(tensor=mmpv)

                    p.stage(vector=lambda v: v.reciprocal(rd_t[:], ps_den[:]))
                    p.stage(tensor=lambda t: t.matmul(
                        ps_db[:], ones_row[0:1, 0:64], rd_t[:],
                        start=True, stop=True))
                    p.stage(scalar=lambda sc: sc.activation(at_s[:], ps_pv[:],
                                                            AF.Copy))
                    p.stage(vector=lambda v: v.tensor_mul(at_b[:], at_s[:],
                                                          ps_db[:]))

                    def stp(s, hi=hi, ho=ho, r=r, t1c=t1c):
                        p.dma(s, apre_hbm[hi, ho:ho + 64,
                                          r * SEQ + t1c * CH:
                                          r * SEQ + (t1c + 1) * CH], at_b[:])
                        return None
                    p.stage(sync=stp)

        with p.scoped():
            wt = p.sb([128, DTI * 1024], BF16, "l3wr")
            apc = [p.sb([128, CH], BF16, "l3apc") for _ in range(DTI)]
            p.stage(gpsimd=lambda gp: (p.dma(
                gp, wt[:].rearrange("p (a f) -> p a f", a=DTI),
                W3["Wr"].rearrange("(a p) f -> p a f", p=128)), None)[1])
            for c in range(NT):
                def lda(s, c=c):
                    for i in range(DTI):
                        p.dma(s, apc[i][:], apre_hbm[i, :, csl(c)])
                    return None
                p.stage(sync=lda)
                for fo in range(DTI):
                    def mmr2(t, fo=fo, c=c):
                        for i in range(DTI):
                            last = t.matmul(
                                ps_p[:],
                                wt[:, (i * 8 + fo) * 128:(i * 8 + fo + 1) * 128],
                                apc[i][:], start=(i == 0), stop=(i == DTI - 1))
                        return last
                    p.stage(tensor=mmr2)

                    def ldf(s, fo=fo, c=c):
                        p.dma(s, fac[:], ffnacc_hbm[fo, :, csl(c)])
                        return None
                    p.stage(sync=ldf)
                    p.stage(vector=lambda v: v.tensor_add(ot[:], ps_p[:], fac[:]))
                    p.stage(sync=lambda s, fo=fo, c=c: (p.dma(
                        s, out_hbm[fo, :, csl(c)], ot[:]), None)[1])



W_BLOCK3 = [
    ("l3a_Wq", [D, D]), ("l3a_Wk", [D, D]), ("l3a_Wv", [D, D]),
    ("l3a_Wr", [D, D]),
    ("l3f_rwT", [D, 32]), ("l3f_bias", [32]),
    ("l3f_gateT", [32, D, 512]), ("l3f_upT", [32, D, 512]),
    ("l3f_downT", [32, 512, D]),
    ("l3f_shgT", [D, 1024]), ("l3f_shuT", [D, 1024]), ("l3f_shdT", [1024, D]),
]


def emit_block3(p, C, win, x, x4):
    xn32 = p.dram("b3_xn32", [DTI, 128, T], F32)
    xnbf = p.dram("b3_xnbf", [DTI, 128, T], BF16)
    ffnacc = p.dram("b3_ffnacc", [DTI, 128, T], F32)
    apre = p.dram("b3_apre", [DTI, 128, T], BF16)

    emit_ln(p, C, x, xn32, xnbf)
    with p.scoped():
        combT = p.sb([32, T], F32, "combT")
        emit_router_grouped(p, C, xn32, win["l3f_rwT"], win["l3f_bias"], combT,
                            E=32, GRP=4, K=4, scale=2.5, mask_fill=0.0)
        experts = []
        for e in range(32):
            experts.append(dict(gate=win["l3f_gateT"][e], up=win["l3f_upT"][e],
                                down=win["l3f_downT"][e], e=e, FTI=4))
        experts.append(dict(gate=win["l3f_shgT"], up=win["l3f_shuT"],
                            down=win["l3f_shdT"], e=None, FTI=8))
        emit_moe_ffn(p, C, xnbf, x, ffnacc, combT, experts, tag="m3f")

    W3 = dict(Wq=win["l3a_Wq"], Wk=win["l3a_Wk"], Wv=win["l3a_Wv"],
              Wr=win["l3a_Wr"])
    emit_l3_attn(p, C, W3, xnbf, ffnacc, x4, apre)


def build_block3(debug=False):
    p = Prog()
    x = p.dram("x", [DTI, 128, T], F32, kind="ExternalInput")
    x4 = p.dram("x4", [DTI, 128, T], F32, kind="ExternalOutput")
    win = declare_w(p, W_BLOCK3)
    C = emit_consts(p)
    emit_block3(p, C, win, x, x4)
    p.final_wait()
    return p.finish()


def prep_block3_inputs(inputs):
    f32 = np.float32
    w = {}
    w["l3a_Wq"] = np.ascontiguousarray(inputs["L3_attn_Wq"], f32)
    w["l3a_Wk"] = np.ascontiguousarray(inputs["L3_attn_Wk"], f32)
    w["l3a_Wv"] = np.ascontiguousarray(inputs["L3_attn_Wv"], f32)
    w["l3a_Wr"] = np.ascontiguousarray(inputs["L3_attn_Wr"], f32)
    w["l3f_rwT"] = np.ascontiguousarray(inputs["L3_ffn_router_weight"].T, f32)
    w["l3f_bias"] = np.ascontiguousarray(inputs["L3_ffn_router_bias"], f32)
    w["l3f_gateT"] = np.ascontiguousarray(
        np.asarray(inputs["L3_ffn_gate_w"], f32).transpose(0, 2, 1))
    w["l3f_upT"] = np.ascontiguousarray(
        np.asarray(inputs["L3_ffn_up_w"], f32).transpose(0, 2, 1))
    w["l3f_downT"] = np.ascontiguousarray(
        np.asarray(inputs["L3_ffn_down_w"], f32).transpose(0, 2, 1))
    w["l3f_shgT"] = np.ascontiguousarray(inputs["L3_ffn_shared_gate_w"].T, f32)
    w["l3f_shuT"] = np.ascontiguousarray(inputs["L3_ffn_shared_up_w"].T, f32)
    w["l3f_shdT"] = np.ascontiguousarray(inputs["L3_ffn_shared_down_w"].T, f32)
    return w


def run_block3(inputs, x_full, debug=False):
    w = prep_block3_inputs(inputs)
    shards = _fm_shard(x_full)
    nc = build_block3(debug=debug)
    in_maps = [dict(w, x=shards[c]) for c in range(NCORE)]
    res = run_bass_kernel_spmd(nc, in_maps, core_ids=list(range(NCORE)))
    if debug:
        return res
    return _fm_unshard([res.results[c]["x4"] for c in range(NCORE)])



def emit_router_softmax_top2(p, C, xn32_hbm, rwT_hbm, mT, E=8):
    ones_row, identf = C["ones_row"], C["identf"]
    with p.scoped():
        rwt = p.sb([128, DTI * E], F32, "r1wt")
        xs = [p.sb([128, CH], F32, "r1xs") for _ in range(DTI)]
        lg = p.psum([128, CH], F32, "r1lg")
        lgs = p.sb([128, E], F32, "r1lgs")
        mx = p.sb([128, 1], F32, "r1mx")
        sm = p.sb([128, E], F32, "r1sm")
        ssum = p.sb([128, 1], F32, "r1ss")
        m1 = p.sb([128, 1], F32, "r1m1")
        sel1 = p.sb([128, E], F32, "r1s1")
        sel2 = p.sb([128, E], F32, "r1s2")
        msk = p.sb([128, E], F32, "r1msk")
        pack = p.sb([128, 18], F32, "r1pack")
        ct_ps = p.psum([18, CH], F32, "r1ct")

        p.stage(sync=lambda s: (p.dma(
            s, rwt[:].rearrange("p (a e) -> p a e", a=DTI),
            rwT_hbm.rearrange("(a p) e -> p a e", p=128)), None)[1])
        p.stage(vector=lambda v: v.memset(mT[:], 0.0))

        for c in range(NT):
            def ld(s, c=c):
                for i in range(DTI):
                    p.dma(s, xs[i][:], xn32_hbm[i, :, csl(c)])
                return None
            p.stage(sync=ld)

            for tt in range(4):
                def mm(t, c=c, tt=tt):
                    for i in range(DTI):
                        last = t.matmul(
                            lg[0:128, 0:E], xs[i][:, tt * 128:(tt + 1) * 128],
                            rwt[:, i * E:(i + 1) * E],
                            start=(i == 0), stop=(i == DTI - 1))
                    return last
                p.stage(tensor=mm)

                def r1(v):
                    def W(instr):
                        instr.then_inc(p.sem("vv"), 1)
                        p.cnt["vv"] += 1
                        v.wait_ge(p.sem("vv"), p.cnt["vv"])
                        return instr
                    W(v.tensor_copy(lgs[:], lg[0:128, 0:E]))
                    W(v.tensor_reduce(mx[:], lgs[:], AX.X, ALU.max))
                    return v.tensor_scalar(mx[:], mx[:], -1.0, None, ALU.mult)
                p.stage(vector=r1)
                p.stage(scalar=lambda sc: sc.activation(
                    sm[:], lgs[:], AF.Exp, bias=mx[:, 0:1],
                    accum_out=ssum[:]))

                def r2(v):
                    def W(instr):
                        instr.then_inc(p.sem("vv"), 1)
                        p.cnt["vv"] += 1
                        v.wait_ge(p.sem("vv"), p.cnt["vv"])
                        return instr
                    W(v.reciprocal(ssum[:], ssum[:]))
                    W(v.tensor_scalar(sm[:], sm[:], ssum[:, 0:1], None, ALU.mult))
                    W(v.tensor_reduce(m1[:], sm[:], AX.X, ALU.max))
                    W(v.tensor_scalar(sel1[:], sm[:], m1[:, 0:1], None, ALU.is_ge))
                    W(v.tensor_copy(pack[:, 16:17], m1[:]))
                    W(v.scalar_tensor_tensor(msk[:], sel1[:], -BIG, sm[:],
                                             ALU.mult, ALU.add))
                    W(v.tensor_reduce(m1[:], msk[:], AX.X, ALU.max))
                    W(v.tensor_scalar(sel2[:], msk[:], m1[:, 0:1], None, ALU.is_ge))
                    W(v.tensor_copy(pack[:, 17:18], m1[:]))
                    W(v.tensor_copy(pack[:, 0:E], sel1[:]))
                    return v.tensor_copy(pack[:, 8:8 + E], sel2[:])
                p.stage(vector=r2)

                p.stage(tensor=lambda t: t.transpose(ct_ps[0:18, 0:128],
                                                     pack[:, 0:18], identf[:]))
                p.stage(vector=lambda v, c=c, tt=tt: v.tensor_copy(
                    mT[0:18, (c * 4 + tt) * 128:(c * 4 + tt + 1) * 128],
                    ct_ps[0:18, 0:128]))



def emit_l1_ffn(p, C, xnbf_hbm, seed_hbm, out_hbm, mT, W1, tag="l1f"):
    sel32 = C["sel32"]
    NG = 8
    CPG = NT // NG
    E = 8
    with p.scoped():
        xnb = [[p.sb([128, CH], BF16, tag + "xnb") for _ in range(DTI)]
               for _ in range(CPG)]
        xsb = [[[p.sb([128, CH], BF16, tag + "xsb") for _ in range(DTI)]
                for _ in range(CPG)] for _ in range(2)]
        acc = [[p.sb([128, CH], F32, tag + "acc") for _ in range(DTI)]
               for _ in range(CPG)]
        w1t = [p.sb([128, DTI * 1024], BF16, tag + "w1") for _ in range(2)]
        w2t = [p.sb([128, DTI * 1024], BF16, tag + "w2") for _ in range(2)]
        w3t = [p.sb([128, DTI * 1024], BF16, tag + "w3") for _ in range(2)]
        bcols = [p.sb([128, 3 * DTI], F32, tag + "bc") for _ in range(2)]
        ps_h = [p.psum([128, CH], F32, tag + "psh") for _ in range(2)]
        ps_x3 = [p.psum([128, CH], F32, tag + "psx3") for _ in range(2)]
        ps_h2 = [p.psum([128, CH], F32, tag + "psh2") for _ in range(2)]
        ps_cb = p.psum([128, CH], F32, tag + "pscb")
        cb_s = p.sb([128, CH], F32, tag + "cbs")
        h_t = [p.sb([128, CH], BF16, tag + "h") for _ in range(DTI)]
        x3_t = [p.sb([128, CH], BF16, tag + "x3") for _ in range(DTI)]
        x2_t = [p.sb([128, CH], BF16, tag + "x2") for _ in range(2)]
        tmp_t = p.sb([128, CH], F32, tag + "tmp")

        def ldw(gp, e, par):
            p.dma_pf(gp, w1t[par][:].rearrange("p (a f) -> p a f", a=DTI),
                     W1["w1T"][e].rearrange("(a p) f -> p a f", p=128))
            p.dma_pf(gp, w2t[par][:].rearrange("p (a f) -> p a f", a=DTI),
                     W1["w2T"][e].rearrange("(a p) f -> p a f", p=128))
            p.dma_pf(gp, w3t[par][:].rearrange("p (a f) -> p a f", a=DTI),
                     W1["w3T"][e].rearrange("(a p) f -> p a f", p=128))
            p.dma_pf(gp, bcols[par][:, 0:DTI], W1["b1c"][e])
            p.dma_pf(gp, bcols[par][:, DTI:2 * DTI], W1["b2c"][e])
            p.dma_pf(gp, bcols[par][:, 2 * DTI:3 * DTI], W1["b3c"][e])
            return None

        for g in range(NG):
            chunks = list(range(g * CPG, (g + 1) * CPG))
            pf_after = {}

            def seed(s, g=g, chunks=chunks):
                for ci, c in enumerate(chunks):
                    for i in range(DTI):
                        p.dma(s, acc[ci][i][:], seed_hbm[i, :, csl(c)])
                        p.dma(s, xnb[ci][i][:], xnbf_hbm[i, :, csl(c)])
                return None
            p.stage(sync=seed, gpsimd=lambda gp: ldw(gp, 0, 0))
            pf_after[0] = p.cnt["pf"]

            for k in range(2):
                for ci, c in enumerate(chunks):
                    def bc(t, k=k, c=c):
                        return t.matmul(ps_cb[:],
                                        sel32[:, (16 + k) * 128:(17 + k) * 128],
                                        mT[0:32, csl(c)], start=True, stop=True)
                    p.stage(tensor=bc)
                    for i in range(DTI):
                        p.stage(vector=lambda v, k=k, ci=ci, i=i: v.tensor_mul(
                            xsb[k][ci][i][:], xnb[ci][i][:], ps_cb[:]))

            for e in range(E):
                par = e % 2
                if e + 1 < E:
                    p.stage(gpsimd=lambda gp, e=e: ldw(gp, e + 1, (e + 1) % 2))
                    pf_after[e + 1] = p.cnt["pf"]
                pf_tgt = pf_after[e]

                for k in range(2):
                    for ci, c in enumerate(chunks):
                        def bcm(t, e=e, k=k, c=c, pf_tgt=pf_tgt, first=(k == 0 and ci == 0)):
                            if first:
                                t.wait_ge(p.sem("pf"), pf_tgt)
                            return t.matmul(
                                ps_cb[:], sel32[:, (k * 8 + e) * 128:
                                                (k * 8 + e + 1) * 128],
                                mT[0:32, csl(c)], start=True, stop=True)
                        p.stage(tensor=bcm)
                        p.stage(vector=lambda v: v.tensor_copy(cb_s[:],
                                                               ps_cb[:]))

                        for u in range(DTI + 2):
                            fns = {}
                            if u < DTI:
                                def tf(t, par=par, k=k, ci=ci, fo=u, u=u):
                                    for i in range(DTI):
                                        t.matmul(
                                            ps_h[u % 2][:],
                                            w1t[par][:, (i * 8 + fo) * 128:
                                                     (i * 8 + fo + 1) * 128],
                                            xsb[k][ci][i][:], start=(i == 0),
                                            stop=(i == DTI - 1))
                                    for i in range(DTI):
                                        last = t.matmul(
                                            ps_x3[u % 2][:],
                                            w3t[par][:, (i * 8 + fo) * 128:
                                                     (i * 8 + fo + 1) * 128],
                                            xsb[k][ci][i][:], start=(i == 0),
                                            stop=(i == DTI - 1))
                                    return last
                                fns["tensor"] = tf
                            if 0 <= u - 1 < DTI:
                                us = u - 1
                                fns["scalar"] = (
                                    lambda sc, us=us, par=par: sc.activation(
                                        h_t[us][:], ps_h[us % 2][:], AF.Silu,
                                        bias=bcols[par][:, us:us + 1]))
                            if 0 <= u - 2 < DTI:
                                uv = u - 2
                                fns["vector"] = (
                                    lambda v, uv=uv, par=par: v.tensor_scalar(
                                        x3_t[uv][:], ps_x3[uv % 2][:],
                                        bcols[par][:, 2 * DTI + uv:
                                                    2 * DTI + uv + 1],
                                        None, ALU.add))
                            p.stage(**fns)

                        for w in range(DTI + 1):
                            fns = {}
                            if w < DTI:
                                def tf2(t, par=par, fo=w, w=w):
                                    for fi in range(DTI):
                                        last = t.matmul(
                                            ps_h2[w % 2][:],
                                            w2t[par][:, (fi * 8 + fo) * 128:
                                                     (fi * 8 + fo + 1) * 128],
                                            h_t[fi][:], start=(fi == 0),
                                            stop=(fi == DTI - 1))
                                    return last
                                fns["tensor"] = tf2
                            if 0 <= w - 1 < DTI:
                                wv = w - 1

                                def vf2(v, wv=wv, par=par, ci=ci):
                                    def W(instr):
                                        instr.then_inc(p.sem("vv"), 1)
                                        p.cnt["vv"] += 1
                                        v.wait_ge(p.sem("vv"), p.cnt["vv"])
                                        return instr
                                    W(v.tensor_scalar(
                                        x2_t[wv % 2][:], ps_h2[wv % 2][:],
                                        bcols[par][:, DTI + wv:DTI + wv + 1],
                                        None, ALU.add))
                                    v.tensor_mul(tmp_t[:], x2_t[wv % 2][:],
                                                 x3_t[wv][:])
                                    v.tensor_mul(tmp_t[:], tmp_t[:], cb_s[:])
                                    return v.tensor_add(acc[ci][wv][:],
                                                        acc[ci][wv][:], tmp_t[:])
                                fns["vector"] = vf2
                            p.stage(**fns)

            def store(s, g=g, chunks=chunks):
                for ci, c in enumerate(chunks):
                    for i in range(DTI):
                        p.dma(s, out_hbm[i, :, csl(c)], acc[ci][i][:])
                return None
            p.stage(sync=store)



def emit_l1_attn(p, C, W1a, xnabf_hbm, att1_hbm):
    ones_row, identf, ident = C["ones_row"], C["identf"], C["ident"]
    with p.scoped():
        beta_r = p.sb([1, 16], F32, "l1beta")
        ps_bb = p.psum([128, CH], F32, "l1psbb")
        ebb = p.sb([128, 16], F32, "l1ebb")
        qsc = p.sb([128, DTI], F32, "l1qsc")
        qbe = p.sb([128, DTI], F32, "l1qbe")
        kbe = p.sb([128, DTI], F32, "l1kbe")
        bqc = p.sb([128, DTI], F32, "l1bqc")
        bkc = p.sb([128, DTI], F32, "l1bkc")
        bvc = p.sb([128, DTI], F32, "l1bvc")
        boc = p.sb([128, DTI], F32, "l1boc")
        mask128 = p.sb([128, 128], F32, "l1mask")
        onesb = p.sb([128, 1], BF16, "l1ones")

        def ldb(s):
            p.dma(s, beta_r[:], W1a["beta"][None, :])
            p.dma(s, bqc[:], W1a["bqc"])
            p.dma(s, bkc[:], W1a["bkc"])
            p.dma(s, bvc[:], W1a["bvc"])
            p.dma(s, boc[:], W1a["boc"])
            return None
        p.stage(sync=ldb)
        p.stage(scalar=lambda sc: sc.activation(beta_r[:], beta_r[:], AF.Exp))
        p.stage(tensor=lambda t: t.matmul(ps_bb[:, 0:16], ones_row[0:1, :],
                                          beta_r[:], start=True, stop=True))
        p.stage(vector=lambda v: v.tensor_copy(ebb[:], ps_bb[:, 0:16]))

        def mkscale(v):
            def W(instr):
                instr.then_inc(p.sem("vv"), 1)
                p.cnt["vv"] += 1
                v.wait_ge(p.sem("vv"), p.cnt["vv"])
                return instr
            for ft in range(DTI):
                W(v.tensor_copy(qsc[0:64, ft:ft + 1], ebb[0:64, 2 * ft:2 * ft + 1]))
                W(v.tensor_copy(qsc[64:128, ft:ft + 1],
                                ebb[64:128, 2 * ft + 1:2 * ft + 2]))
            W(v.tensor_scalar(qsc[:], qsc[:], 8.0, None, ALU.mult))
            W(v.reciprocal(qsc[:], qsc[:]))
            W(v.tensor_mul(qbe[:], bqc[:], qsc[:]))
            return v.tensor_mul(kbe[:], bkc[:], qsc[:])
        p.stage(vector=mkscale)

        def mkmask(g):
            g.memset(mask128[:], 1.0)
            g.affine_select(out=mask128[:].rearrange("p (a b) -> p a b", a=8),
                            in_=mask128[:].rearrange("p (a b) -> p a b", a=8),
                            pattern=[[-16, 8], [0, 16]], base=0,
                            channel_multiplier=1, compare_op=ALU.is_ge, fill=0.0)
            g.affine_select(out=mask128[:].rearrange("p (a b) -> p a b", a=8),
                            in_=mask128[:].rearrange("p (a b) -> p a b", a=8),
                            pattern=[[16, 8], [0, 16]], base=15,
                            channel_multiplier=-1, compare_op=ALU.is_ge, fill=0.0)
            g.affine_select(out=mask128[:].rearrange("p (a b) -> p a b", a=8),
                            in_=mask128[:].rearrange("p (a b) -> p a b", a=8),
                            pattern=[[16, 8], [1, 16]], base=0,
                            channel_multiplier=-1, compare_op=ALU.is_ge, fill=0.0)
            g.memset(onesb[:], 1.0)
            return g.memset(mask128[0:1, 0:1], 1.0)
        p.stage(gpsimd=mkmask)

        ps_p = p.psum([128, CH], F32, "l1psp")
        ps_s = p.psum([128, 128], F32, "l1pss")
        ps_pv = p.psum([128, 128], F32, "l1pspv")
        ps_t2 = p.psum([128, 128], BF16, "l1pst")
        ta = p.sb([128, CH], F32, "l1ta")
        tb = p.sb([128, CH], F32, "l1tb")
        sm_t = p.sb([128, 128], BF16, "l1sm")
        rden = p.sb([128, 1], F32, "l1rden")
        atm = p.sb([128, 16 * 64], BF16, "l1atm")
        fac = p.sb([128, CH], F32, "l1fac")

        PS = 1024
        NTG = PS // 128
        HT = T // PS
        for hh in range(HT):
            hsl = slice(hh * PS, (hh + 1) * PS)
            with p.scoped():
                xnb = [p.sb([128, PS], BF16, "l1xnb") for _ in range(DTI)]
                qf = [p.sb([128, PS], BF16, "l1qf") for _ in range(DTI)]
                kf = [p.sb([128, PS], BF16, "l1kf") for _ in range(DTI)]
                vtm = p.sb([128, NTG * 1024], BF16, "l1vtm")
                apre = [p.sb([128, PS], BF16, "l1apre") for _ in range(DTI)]

                def ldx(s, hsl=hsl):
                    for i in range(DTI):
                        p.dma(s, xnb[i][:], xnabf_hbm[i, :, hsl])
                    return None
                p.stage(sync=ldx)

                for nm, dst, be in (("WqT", qf, qbe), ("WkT", kf, kbe)):
                    with p.scoped():
                        wt = p.sb([128, DTI * 1024], BF16, "l1w")
                        p.stage(gpsimd=lambda gp, nm=nm: (p.dma(
                            gp, wt[:].rearrange("p (a f) -> p a f", a=DTI),
                            W1a[nm].rearrange("(a p) f -> p a f", p=128)),
                            None)[1])
                        for fo in range(DTI):
                            for cc in range(PS // CH):
                                def mmq(t, fo=fo, cc=cc):
                                    for i in range(DTI):
                                        last = t.matmul(
                                            ps_p[:],
                                            wt[:, (i * 8 + fo) * 128:
                                               (i * 8 + fo + 1) * 128],
                                            xnb[i][:, cc * CH:(cc + 1) * CH],
                                            start=(i == 0), stop=(i == DTI - 1))
                                    return last
                                p.stage(tensor=mmq)

                                def sp1(sc, fo=fo, be=be):
                                    sc.activation(ta[:], ps_p[:], AF.Abs,
                                                  bias=be[:, fo:fo + 1],
                                                  scale=qsc[:, fo:fo + 1])
                                    sc.activation(tb[:], ta[:], AF.Exp,
                                                  scale=-1.0)
                                    sc.activation(ta[:], tb[:], AF.Ln, bias=1.0)
                                    return sc.activation(
                                        tb[:], ps_p[:], AF.Relu,
                                        bias=be[:, fo:fo + 1],
                                        scale=qsc[:, fo:fo + 1])
                                p.stage(scalar=sp1)
                                p.stage(vector=lambda v, dst=dst, fo=fo, cc=cc:
                                        v.tensor_add(
                                            dst[fo][:, cc * CH:(cc + 1) * CH],
                                            ta[:], tb[:]))

                with p.scoped():
                    wt = p.sb([128, DTI * 1024], BF16, "l1wv")
                    bvrow = p.sb([1, 1024], F32, "l1bvr")
                    ps_bv = p.psum([128, CH], F32, "l1psbv")
                    bv_s = p.sb([128, CH], F32, "l1bvs")
                    vv_t = p.sb([128, CH], F32, "l1vvt")
                    p.stage(gpsimd=lambda gp: (p.dma(
                        gp, wt[:].rearrange("p (a f) -> p a f", a=DTI),
                        W1a["WvT"].rearrange("(a p) f -> p a f", p=128)), None)[1])
                    p.stage(sync=lambda s: (p.dma(
                        s, bvrow[:], W1a["bv"][None, :]), None)[1])

                    for dc in range(2):
                        def bvb(t, dc=dc):
                            return t.matmul(ps_bv[:], ones_row[0:1, :],
                                            bvrow[:, dc * CH:(dc + 1) * CH],
                                            start=True, stop=True)
                        p.stage(tensor=bvb)
                        p.stage(vector=lambda v: v.tensor_copy(bv_s[:],
                                                               ps_bv[:]))
                        for tg in range(NTG):
                            def mmv(t, tg=tg, dc=dc):
                                for i in range(DTI):
                                    last = t.matmul(
                                        ps_p[:],
                                        xnb[i][:, tg * 128:(tg + 1) * 128],
                                        wt[:, i * 1024 + dc * CH:
                                           i * 1024 + (dc + 1) * CH],
                                        start=(i == 0), stop=(i == DTI - 1))
                                return last
                            p.stage(tensor=mmv)
                            p.stage(vector=lambda v: v.tensor_add(
                                vv_t[:], ps_p[:], bv_s[:]))
                            p.stage(scalar=lambda sc, tg=tg, dc=dc: sc.activation(
                                vtm[:, tg * 1024 + dc * CH:
                                    tg * 1024 + (dc + 1) * CH],
                                vv_t[:], AF.Copy))

                for tg in range(NTG):
                    gsl = slice(tg * 128, (tg + 1) * 128)
                    for h in range(16):
                        hi, ho = h // 2, (h % 2) * 64

                        def mms(t, hi=hi, ho=ho, gsl=gsl):
                            return t.matmul(ps_s[:], kf[hi][ho:ho + 64, gsl],
                                            qf[hi][ho:ho + 64, gsl],
                                            start=True, stop=True)
                        p.stage(tensor=mms)
                        p.stage(vector=lambda v: v.tensor_mul(
                            sm_t[:], ps_s[:], mask128[:]))

                        def mmpv(t, tg=tg, h=h):
                            t.matmul(ps_pv[:, 0:64], sm_t[:],
                                     vtm[:, tg * 1024 + h * 64:
                                         tg * 1024 + h * 64 + 64],
                                     start=True, stop=True)
                            return t.matmul(ps_pv[:, 64:65], sm_t[:], onesb[:],
                                            start=True, stop=True)
                        p.stage(tensor=mmpv)

                        def nrm(v, h=h):
                            def W(instr):
                                instr.then_inc(p.sem("vv"), 1)
                                p.cnt["vv"] += 1
                                v.wait_ge(p.sem("vv"), p.cnt["vv"])
                                return instr
                            W(v.reciprocal(rden[:], ps_pv[:, 64:65]))
                            return v.tensor_scalar(atm[:, h * 64:(h + 1) * 64],
                                                   ps_pv[:, 0:64],
                                                   rden[:, 0:1], None, ALU.mult)
                        p.stage(vector=nrm)

                    for hp in range(DTI):
                        def tp(t, hp=hp):
                            return t.transpose(ps_t2[:],
                                               atm[:, hp * 128:(hp + 1) * 128],
                                               ident[:])
                        p.stage(tensor=tp)
                        p.stage(vector=lambda v, hp=hp, tg=tg: v.tensor_copy(
                            apre[hp][:, tg * 128:(tg + 1) * 128], ps_t2[:]))

                with p.scoped():
                    wt = p.sb([128, DTI * 1024], BF16, "l1wo")
                    p.stage(gpsimd=lambda gp: (p.dma(
                        gp, wt[:].rearrange("p (a f) -> p a f", a=DTI),
                        W1a["WoT"].rearrange("(a p) f -> p a f", p=128)), None)[1])
                    for fo in range(DTI):
                        for cc in range(PS // CH):
                            def mmo(t, fo=fo, cc=cc):
                                for i in range(DTI):
                                    last = t.matmul(
                                        ps_p[:],
                                        wt[:, (i * 8 + fo) * 128:
                                           (i * 8 + fo + 1) * 128],
                                        apre[i][:, cc * CH:(cc + 1) * CH],
                                        start=(i == 0), stop=(i == DTI - 1))
                                return last
                            p.stage(tensor=mmo)
                            p.stage(vector=lambda v, fo=fo: v.tensor_scalar(
                                fac[:], ps_p[:], boc[:, fo:fo + 1], None,
                                ALU.add))

                            def sto(s, fo=fo, cc=cc, hh=hh):
                                p.dma(s, att1_hbm[fo, :,
                                                  hh * PS + cc * CH:
                                                  hh * PS + (cc + 1) * CH],
                                      fac[:])
                                return None
                            p.stage(sync=sto)



W_BLOCK1F = [
    ("l1f_rwT", [D, 8]),
    ("l1f_w1T", [8, D, D]), ("l1f_w2T", [8, D, D]), ("l1f_w3T", [8, D, D]),
    ("l1f_b1c", [8, 128, DTI]), ("l1f_b2c", [8, 128, DTI]),
    ("l1f_b3c", [8, 128, DTI]),
    ("l1f_sw1T", [D, 2048]), ("l1f_sw2T", [D, 2048]), ("l1f_sw3T", [2048, D]),
]
W_BLOCK1A = [
    ("l1a_beta", [16]),
    ("l1a_WqT", [D, D]), ("l1a_WkT", [D, D]), ("l1a_WvT", [D, D]),
    ("l1a_WoT", [D, D]),
    ("l1a_bqc", [128, DTI]), ("l1a_bkc", [128, DTI]), ("l1a_bvc", [128, DTI]),
    ("l1a_bv", [D]), ("l1a_boc", [128, DTI]),
]


def emit_block1_ffn(p, C, win, x, y1p):
    xn32 = p.dram("b1_xn32", [DTI, 128, T], F32)
    xnbf = p.dram("b1_xnbf", [DTI, 128, T], BF16)
    shacc = p.dram("b1_shacc", [DTI, 128, T], F32)

    emit_ln(p, C, x, xn32, xnbf)
    with p.scoped():
        mT = p.sb([32, T], F32, "mT")
        emit_router_softmax_top2(p, C, xn32, win["l1f_rwT"], mT, E=8)
        experts = []
        for half in range(2):
            fs = slice(half * 1024, (half + 1) * 1024)
            experts.append(dict(gate=win["l1f_sw1T"][:, fs],
                                up=win["l1f_sw2T"][:, fs],
                                down=win["l1f_sw3T"][fs, :], e=None, FTI=8))
        emit_moe_ffn(p, C, xnbf, x, shacc, None, experts, tag="l1sh")
        W1 = dict(w1T=win["l1f_w1T"], w2T=win["l1f_w2T"], w3T=win["l1f_w3T"],
                  b1c=win["l1f_b1c"], b2c=win["l1f_b2c"], b3c=win["l1f_b3c"])
        emit_l1_ffn(p, C, xnbf, shacc, y1p, mT, W1)


def emit_block1_attn(p, C, win, xa, att1):
    xnabf = p.dram("b1_xnabf", [DTI, 128, T], BF16)
    emit_ln(p, C, xa, None, xnabf)
    W1a = dict(beta=win["l1a_beta"], WqT=win["l1a_WqT"], WkT=win["l1a_WkT"],
               WvT=win["l1a_WvT"], WoT=win["l1a_WoT"], bqc=win["l1a_bqc"],
               bkc=win["l1a_bkc"], bvc=win["l1a_bvc"], bv=win["l1a_bv"],
               boc=win["l1a_boc"])
    emit_l1_attn(p, C, W1a, xnabf, att1)


def build_blockA(debug=False):
    p = Prog()
    x = p.dram("x", [DTI, 128, T], F32, kind="ExternalInput")
    x1 = p.dram("x1", [DTI, 128, T], F32, kind="ExternalOutput")
    y1p = p.dram("y1p", [DTI, 128, T], F32, kind="ExternalOutput")
    win = declare_w(p, W_BLOCK0 + W_BLOCK1F)
    C = emit_consts(p)
    emit_block0(p, C, win, x, x1, debug=debug)
    emit_block1_ffn(p, C, win, x1, y1p)
    p.final_wait()
    return p.finish()


def build_blockB(debug=False):
    p = Prog()
    xa = p.dram("xa", [DTI, 128, T], F32, kind="ExternalInput")
    att1 = p.dram("att1", [DTI, 128, T], F32, kind="ExternalOutput")
    win = declare_w(p, W_BLOCK1A)
    C = emit_consts(p)
    emit_block1_attn(p, C, win, xa, att1)
    p.final_wait()
    return p.finish()


def build_blockC(debug=False):
    p = Prog()
    y1p = p.dram("y1p", [DTI, 128, T], F32, kind="ExternalInput")
    att = p.dram("att", [DTI, 128, T], F32, kind="ExternalInput")
    x2 = p.dram("x2", [DTI, 128, T], F32)
    x3 = p.dram("x3", [DTI, 128, T], F32)
    x4 = p.dram("x4", [DTI, 128, T], F32, kind="ExternalOutput")
    win = declare_w(p, W_BLOCK2 + W_BLOCK3)
    C = emit_consts(p)
    emit_add(p, y1p, att, x2)
    emit_block2(p, C, win, x2, x3)
    emit_block3(p, C, win, x3, x4)
    p.final_wait()
    return p.finish()


def prep_block1_inputs(inputs):
    f32 = np.float32

    def col128(v):
        return np.ascontiguousarray(np.asarray(v, f32).reshape(-1, 128).T)

    w = {}
    w["l1a_beta"] = np.ascontiguousarray(inputs["L1_attn_beta"], f32)
    w["l1a_WqT"] = np.ascontiguousarray(inputs["L1_attn_Wq"].T, f32)
    w["l1a_WkT"] = np.ascontiguousarray(inputs["L1_attn_Wk"].T, f32)
    w["l1a_WvT"] = np.ascontiguousarray(inputs["L1_attn_Wv"].T, f32)
    w["l1a_WoT"] = np.ascontiguousarray(inputs["L1_attn_Wo"].T, f32)
    w["l1a_bqc"] = col128(inputs["L1_attn_bq"])
    w["l1a_bkc"] = col128(inputs["L1_attn_bk"])
    w["l1a_bvc"] = col128(inputs["L1_attn_bv"])
    w["l1a_bv"] = np.ascontiguousarray(inputs["L1_attn_bv"], f32)
    w["l1a_boc"] = col128(inputs["L1_attn_bo"])
    w["l1f_rwT"] = np.ascontiguousarray(inputs["L1_ffn_gate_w"].T, f32)
    w["l1f_w1T"] = np.ascontiguousarray(
        np.asarray(inputs["L1_ffn_w1"], f32).transpose(0, 2, 1))
    w["l1f_w2T"] = np.ascontiguousarray(
        np.asarray(inputs["L1_ffn_w2"], f32).transpose(0, 2, 1))
    w["l1f_w3T"] = np.ascontiguousarray(
        np.asarray(inputs["L1_ffn_w3"], f32).transpose(0, 2, 1))
    w["l1f_b1c"] = np.ascontiguousarray(
        np.asarray(inputs["L1_ffn_b1"], f32).reshape(8, 8, 128).transpose(0, 2, 1))
    w["l1f_b2c"] = np.ascontiguousarray(
        np.asarray(inputs["L1_ffn_b2"], f32).reshape(8, 8, 128).transpose(0, 2, 1))
    w["l1f_b3c"] = np.ascontiguousarray(
        np.asarray(inputs["L1_ffn_b3"], f32).reshape(8, 8, 128).transpose(0, 2, 1))
    w["l1f_sw1T"] = np.ascontiguousarray(inputs["L1_ffn_sw1"].T, f32)
    w["l1f_sw2T"] = np.ascontiguousarray(inputs["L1_ffn_sw2"].T, f32)
    w["l1f_sw3T"] = np.ascontiguousarray(inputs["L1_ffn_sw3"].T, f32)
    return w


def _fm_shard_tau(x_full):
    out = []
    for c in range(NCORE):
        xc = x_full[:, 256 * c:256 * (c + 1), :]
        fm = np.ascontiguousarray(
            xc.transpose(2, 1, 0).reshape(DTI, 128, T))
        out.append(fm)
    return out


def _fm_unshard_tau(shards):
    x = np.empty((16, SEQ, D), np.float32)
    for c in range(NCORE):
        fm = shards[c].reshape(D, 256, 16)
        x[:, 256 * c:256 * (c + 1), :] = fm.transpose(2, 1, 0)
    return x


def _build_b1_test():
    p = Prog()
    x = p.dram("x", [DTI, 128, T], F32, kind="ExternalInput")
    xa = p.dram("xa", [DTI, 128, T], F32, kind="ExternalInput")
    y1p = p.dram("y1p", [DTI, 128, T], F32, kind="ExternalOutput")
    att1 = p.dram("att1", [DTI, 128, T], F32, kind="ExternalOutput")
    win = declare_w(p, W_BLOCK1F + W_BLOCK1A)
    C = emit_consts(p)
    emit_block1_ffn(p, C, win, x, y1p)
    emit_block1_attn(p, C, win, xa, att1)
    p.final_wait()
    return p.finish()


def run_block1(inputs, x_full, debug=False):
    w = prep_block1_inputs(inputs)
    shards = _fm_shard(x_full)
    ashards = _fm_shard_tau(x_full)
    nc = _build_b1_test()
    in_maps = [dict(w, x=shards[c], xa=ashards[c]) for c in range(NCORE)]
    res = run_bass_kernel_spmd(nc, in_maps, core_ids=list(range(NCORE)))
    if debug:
        return res
    y1p = _fm_unshard([res.results[c]["y1p"] for c in range(NCORE)])
    att = _fm_unshard_tau([res.results[c]["att1"] for c in range(NCORE)])
    return y1p, att


# version 07
# speedup vs baseline: 1.8039x; 1.1777x over previous
import contextlib
from collections import defaultdict

import numpy as np

import concourse.bass as bass
import concourse.mybir as mybir
from concourse.bass_utils import run_bass_kernel_spmd

F32 = mybir.dt.float32
BF16 = mybir.dt.bfloat16
AF = mybir.ActivationFunctionType
ALU = mybir.AluOpType
AX = mybir.AxisListType

NCORE = 8
RPC = 2
T = 4096
SEQ = 2048
NT = 8
CH = 512
D = 1024
DTI = 8
BIG = 1e30


class Prog:
    def __init__(self):
        self.nc = bass.Bass("TRN2", target_bir_lowering=False)
        self.es = contextlib.ExitStack()
        self.root_es = self.es
        self.block = self.es.enter_context(self.nc.Block())
        self.sems = {}
        self.cnt = defaultdict(int)
        self._names = defaultdict(int)
        self.sem("q")

    def sem(self, name):
        if name not in self.sems:
            self.sems[name] = self.root_es.enter_context(self.nc.semaphore(name))
        return self.sems[name]

    def sb(self, shape, dtype=F32, tag="sb"):
        self._names[tag] += 1
        return self.es.enter_context(
            self.nc.sbuf_tensor(f"{tag}_{self._names[tag]}", shape, dtype))

    def psum(self, shape, dtype=F32, tag="ps"):
        self._names[tag] += 1
        return self.es.enter_context(
            self.nc.psum_tensor(f"{tag}_{self._names[tag]}", shape, dtype))

    def dram(self, name, shape, dtype=F32, kind="Internal"):
        return self.nc.dram_tensor(name, shape, dtype, kind=kind).ap()

    @contextlib.contextmanager
    def scoped(self):
        es = contextlib.ExitStack()
        old = self.es
        self.es = es
        try:
            yield
        finally:
            self.es = old
            es.close()

    def dma(self, e, out, in_, **kw):
        instr = e.dma_start(out=out, in_=in_, **kw)
        instr.then_inc(self.sem("q"), 16)
        self.cnt["q"] += 16
        return instr

    def dma_pf(self, e, out, in_, **kw):
        instr = e.dma_start(out=out, in_=in_, **kw)
        instr.then_inc(self.sem("pf"), 16)
        self.cnt["pf"] += 16
        return instr

    def stage(self, **fns):
        b = self.block
        start = self.cnt["q"]
        decs = dict(sync=b.sync, tensor=b.tensor, vector=b.vector,
                    scalar=b.scalar, gpsimd=b.gpsimd)
        for name, fn in fns.items():
            dec = decs[name]

            @dec
            def _(e, fn=fn, start=start):
                e.wait_ge(self.sem("q"), start)
                last = fn(e)
                if last is not None:
                    last.then_inc(self.sem("q"), 1)
                    self.cnt["q"] += 1

    def final_wait(self):
        b = self.block
        tgt = self.cnt["q"]

        @b.sync
        def _(e):
            e.wait_ge(self.sem("q"), tgt)

    def finish(self):
        self.es.close()
        return self.nc


def csl(c):
    return slice(c * CH, (c + 1) * CH)



def emit_consts(p):
    C = {}
    C["ones_row"] = p.sb([1, 128], F32, "onesr")
    C["ones_col"] = p.sb([128, 1], F32, "onesc")
    C["ident"] = p.sb([128, 128], BF16, "idb")
    C["identf"] = p.sb([128, 128], F32, "idf")
    C["sel32"] = p.sb([32, 32 * 128], F32, "sel32")

    def f(g):
        g.memset(C["ones_row"][:], 1.0)
        g.memset(C["ones_col"][:], 1.0)
        g.memset(C["ident"][:], 0.0)
        g.affine_select(out=C["ident"][:], in_=C["ident"][:], pattern=[[-1, 128]],
                        base=0, channel_multiplier=1, compare_op=ALU.not_equal,
                        fill=1.0)
        g.memset(C["identf"][:], 0.0)
        g.affine_select(out=C["identf"][:], in_=C["identf"][:], pattern=[[-1, 128]],
                        base=0, channel_multiplier=1, compare_op=ALU.not_equal,
                        fill=1.0)
        g.memset(C["sel32"][:], 0.0)
        return g.affine_select(
            out=C["sel32"][:].rearrange("p (e c) -> p e c", e=32),
            in_=C["sel32"][:].rearrange("p (e c) -> p e c", e=32),
            pattern=[[-1, 32], [0, 128]], base=0, channel_multiplier=1,
            compare_op=ALU.not_equal, fill=1.0)

    p.stage(gpsimd=f)
    return C



def emit_ln(p, C, x_hbm, xn32_hbm, xnbf_hbm, hbf_hbm=None, normw_hbm=None,
            x2_hbm=None):
    with p.scoped():
        ones_row = C["ones_row"]
        ones_col = C["ones_col"]
        xt = [p.sb([128, CH], F32, "lnx") for _ in range(DTI)]
        x2t = p.sb([128, CH], F32, "lnx2") if x2_hbm is not None else None
        sqt = [p.sb([128, CH], F32, "lnsq") for _ in range(DTI)]
        s1 = p.psum([1, CH], F32, "lns1")
        s2 = p.psum([1, CH], F32, "lns2")
        rows = p.sb([1, 8 * CH], F32, "lnrow")
        meanR = p.sb([1, T], F32, "lnmean")
        rstdR = p.sb([1, T], F32, "lnrstd")
        rmsR = p.sb([1, T], F32, "lnrms") if hbf_hbm is not None else None
        bc = [p.psum([128, CH], F32, "lnbc") for _ in range(3)]
        t1 = p.sb([128, CH], F32, "lnt1")
        xnt = [p.sb([128, CH], F32, "lnxnt") for _ in range(3)]
        xnb = [p.sb([128, CH], BF16, "lnxnb") for _ in range(2)]
        hbt = [p.sb([128, CH], BF16, "lnhbt") for _ in range(3)]
        nwc = p.sb([128, DTI], F32, "lnnw") if hbf_hbm is not None else None

        def S(i):
            return rows[:, i * CH:(i + 1) * CH]

        if nwc is not None:
            p.stage(sync=lambda s: (p.dma(s, nwc[:], normw_hbm), None)[1])

        def load_x(s, c):
            for i in range(DTI):
                p.dma(s, xt[i][:], x_hbm[i, :, csl(c)])
                if x2_hbm is not None:
                    p.dma(s, x2t[:], x2_hbm[i, :, csl(c)])
            return None

        for c in range(NT):
            if x2_hbm is None:
                p.stage(sync=lambda s, c=c: load_x(s, c))
            else:
                for i in range(DTI):
                    def ld1(s, c=c, i=i):
                        p.dma(s, xt[i][:], x_hbm[i, :, csl(c)])
                        p.dma(s, x2t[:], x2_hbm[i, :, csl(c)])
                    p.stage(sync=ld1)
                    p.stage(vector=lambda v, i=i: v.tensor_add(
                        xt[i][:], xt[i][:], x2t[:]))

            def sq(sc, c=c):
                for i in range(DTI):
                    last = sc.activation(sqt[i][:], xt[i][:], AF.Square)
                return last
            p.stage(scalar=sq)

            def mm(t, c=c):
                for i in range(DTI):
                    t.matmul(s1[:], ones_col[:], xt[i][:],
                             start=(i == 0), stop=(i == DTI - 1))
                for i in range(DTI):
                    last = t.matmul(s2[:], ones_col[:], sqt[i][:],
                                    start=(i == 0), stop=(i == DTI - 1))
                return last
            p.stage(tensor=mm)

            def stats_v(v, c=c):
                v.tensor_scalar(meanR[:, csl(c)], s1[:], 1.0 / D, None, ALU.mult)
                v.tensor_scalar(S(1), s2[:], 1.0 / D, None, ALU.mult)
                v.tensor_mul(S(2), meanR[:, csl(c)], meanR[:, csl(c)])
                v.tensor_sub(S(1), S(1), S(2))
                return v.tensor_scalar(S(2), S(1), 1e-5, None, ALU.add)
            p.stage(vector=stats_v)

            def stats_s(sc, c=c):
                sc.activation(S(3), S(2), AF.Ln)
                last = sc.activation(rstdR[:, csl(c)], S(3), AF.Exp, scale=-0.5)
                if rmsR is not None:
                    last = sc.activation(S(4), S(3), AF.Exp, scale=-1.0)
                return last
            p.stage(scalar=stats_s)

            if rmsR is not None:
                def rms_v(v, c=c):
                    v.tensor_mul(S(5), S(1), S(4))
                    return v.tensor_scalar(S(5), S(5), 1e-5, None, ALU.add)
                p.stage(vector=rms_v)

                def rms_s(sc, c=c):
                    sc.activation(S(6), S(5), AF.Ln)
                    return sc.activation(rmsR[:, csl(c)], S(6), AF.Exp, scale=-0.5)
                p.stage(scalar=rms_s)

            def bcmm(t, c=c):
                t.matmul(bc[0][:], ones_row[0:1, :], meanR[:, csl(c)],
                         start=True, stop=True)
                last = t.matmul(bc[1][:], ones_row[0:1, :], rstdR[:, csl(c)],
                                start=True, stop=True)
                if rmsR is not None:
                    last = t.matmul(bc[2][:], ones_row[0:1, :], rmsR[:, csl(c)],
                                    start=True, stop=True)
                return last
            p.stage(tensor=bcmm)

            for u in range(DTI + 2):
                fns = {}
                if u < DTI:
                    def ap_v(v, i=u, u=u):
                        v.tensor_sub(t1[:], xt[i][:], bc[0][:])
                        last = v.tensor_mul(xnt[u % 3][:], t1[:], bc[1][:])
                        if hbf_hbm is not None:
                            last = v.scalar_tensor_tensor(
                                hbt[u % 3][:], xnt[u % 3][:], nwc[:, i:i + 1],
                                bc[2][:], ALU.mult, ALU.mult)
                        return last
                    fns["vector"] = ap_v
                if 0 <= u - 1 < DTI:
                    fns["scalar"] = (lambda sc, us=u - 1: sc.activation(
                        xnb[us % 2][:], xnt[us % 3][:], AF.Copy))
                if 0 <= u - 2 < DTI:
                    def st(s, c=c, i=u - 2, uv=u - 2):
                        if xn32_hbm is not None:
                            p.dma(s, xn32_hbm[i, :, csl(c)], xnt[uv % 3][:])
                        p.dma(s, xnbf_hbm[i, :, csl(c)], xnb[uv % 2][:])
                        if hbf_hbm is not None:
                            p.dma(s, hbf_hbm[i, :, csl(c)], hbt[uv % 3][:])
                        return None
                    fns["sync"] = st
                p.stage(**fns)



def emit_router_grouped(p, C, xn32_hbm, rwT_hbm, bias_hbm, combT, E, GRP, K,
                        scale, mask_fill=-1.0, dbg=None):
    EPG = E // GRP
    with p.scoped():
        ones_row, identf = C["ones_row"], C["identf"]
        rwt = p.sb([128, DTI * E], F32, "rwt")
        br = p.sb([1, E], F32, "rbr")
        bb_ps = p.psum([128, CH], F32, "rbb")
        bb = p.sb([128, E], F32, "rbbs")
        xs = [p.sb([128, CH], F32, "rxs") for _ in range(DTI)]
        lg = p.psum([128, CH], F32, "rlg")
        sc_t = p.sb([128, E], F32, "rsc")
        s_t = p.sb([128, E], F32, "rst")
        g1 = p.sb([128, GRP], F32, "rg1")
        g2 = p.sb([128, GRP], F32, "rg2")
        gm = p.sb([128, GRP], F32, "rgm")
        m1 = p.sb([128, 1], F32, "rm1")
        ms = p.sb([128, E], F32, "rmsk")
        tmp = p.sb([128, E], F32, "rtmp")
        selk = p.sb([128, E], F32, "rselk")
        twk = p.sb([128, 1], F32, "rtwk")
        twsum = p.sb([128, 1], F32, "rtws")
        comb = p.sb([128, E], F32, "rcomb")
        ct_ps = p.psum([E, CH], F32, "rct")

        def ld0(s):
            p.dma(s, rwt[:].rearrange("p (a e) -> p a e", a=DTI),
                  rwT_hbm.rearrange("(a p) e -> p a e", p=128))
            p.dma(s, br[:], bias_hbm[None, :])
            return None
        p.stage(sync=ld0)
        p.stage(vector=lambda v: v.memset(combT[:], 0.0))
        p.stage(tensor=lambda t: t.matmul(bb_ps[:, 0:E], ones_row[0:1, :], br[:],
                                          start=True, stop=True))
        p.stage(vector=lambda v: v.tensor_copy(bb[:], bb_ps[:, 0:E]))

        for c in range(NT):
            def ld(s, c=c):
                for i in range(DTI):
                    p.dma(s, xs[i][:], xn32_hbm[i, :, csl(c)])
                return None
            p.stage(sync=ld)

            for tt in range(4):
                def mm(t, c=c, tt=tt):
                    for i in range(DTI):
                        last = t.matmul(
                            lg[0:128, 0:E], xs[i][:, tt * 128:(tt + 1) * 128],
                            rwt[:, i * E:(i + 1) * E],
                            start=(i == 0), stop=(i == DTI - 1))
                    return last
                p.stage(tensor=mm)
                p.stage(scalar=lambda sc: sc.activation(sc_t[:], lg[0:128, 0:E],
                                                        AF.Sigmoid))

                def route_v(v):
                    def W(instr):
                        instr.then_inc(p.sem("vv"), 1)
                        p.cnt["vv"] += 1
                        v.wait_ge(p.sem("vv"), p.cnt["vv"])
                        return instr
                    W(v.tensor_add(s_t[:], sc_t[:], bb[:]))
                    def gsl(j):
                        return s_t[:].rearrange("p (g e) -> p g e", g=GRP)[:, :, j]
                    first = True
                    for jA in range(EPG):
                        for jB in range(jA + 1, EPG):
                            W(v.tensor_add(g2[:], gsl(jA), gsl(jB)))
                            if first:
                                W(v.tensor_copy(g1[:], g2[:]))
                                first = False
                            else:
                                W(v.tensor_max(g1[:], g1[:], g2[:]))
                    W(v.tensor_reduce(m1[:], g1[:], AX.X, ALU.max))
                    W(v.tensor_scalar(gm[:], g1[:], m1[:, 0:1], None, ALU.is_ge))
                    W(v.scalar_tensor_tensor(g2[:], gm[:], -BIG, g1[:], ALU.mult,
                                             ALU.add))
                    W(v.tensor_reduce(m1[:], g2[:], AX.X, ALU.max))
                    W(v.tensor_scalar(g2[:], g2[:], m1[:, 0:1], None, ALU.is_ge))
                    W(v.tensor_add(gm[:], gm[:], g2[:]))
                    W(v.tensor_scalar(ms[:], s_t[:], -mask_fill, None, ALU.add))
                    for j in range(EPG):
                        W(v.tensor_mul(
                            ms[:].rearrange("p (g e) -> p g e", g=GRP)[:, :, j],
                            ms[:].rearrange("p (g e) -> p g e", g=GRP)[:, :, j],
                            gm[:]))
                    W(v.tensor_scalar(ms[:], ms[:], mask_fill, None, ALU.add))
                    W(v.memset(comb[:], 0.0))
                    W(v.memset(twsum[:], 0.0))
                    for k in range(K):
                        W(v.tensor_reduce(m1[:], ms[:], AX.X, ALU.max))
                        W(v.tensor_scalar(selk[:], ms[:], m1[:, 0:1], None,
                                          ALU.is_ge))
                        W(v.tensor_mul(tmp[:], sc_t[:], selk[:]))
                        W(v.tensor_reduce(twk[:], tmp[:], AX.X, ALU.add))
                        W(v.scalar_tensor_tensor(comb[:], selk[:], twk[:, 0:1],
                                                 comb[:], ALU.mult, ALU.add))
                        W(v.tensor_add(twsum[:], twsum[:], twk[:]))
                        W(v.scalar_tensor_tensor(ms[:], selk[:], -BIG, ms[:],
                                                 ALU.mult, ALU.add))
                    W(v.tensor_scalar(twsum[:], twsum[:], 1e-20, None, ALU.add))
                    W(v.reciprocal(twsum[:], twsum[:]))
                    W(v.tensor_scalar(twsum[:], twsum[:], float(scale), None,
                                      ALU.mult))
                    return v.tensor_scalar(comb[:], comb[:], twsum[:, 0:1],
                                           None, ALU.mult)
                p.stage(vector=route_v)

                if dbg is not None and c == 0 and tt == 0:
                    def dmp(s):
                        p.dma(s, dbg["sc"][:, :], sc_t[:])
                        p.dma(s, dbg["st"][:, :], s_t[:])
                        p.dma(s, dbg["g1"][:, :], g1[:])
                        p.dma(s, dbg["gm"][:, :], gm[:])
                        p.dma(s, dbg["ms"][:, :], ms[:])
                        p.dma(s, dbg["comb"][:, :], comb[:])
                        return None
                    p.stage(sync=dmp)
                p.stage(tensor=lambda t: t.transpose(ct_ps[0:E, 0:128],
                                                     comb[:, 0:E], identf[:]))
                p.stage(vector=lambda v, c=c, tt=tt: v.tensor_copy(
                    combT[0:E, (c * 4 + tt) * 128:(c * 4 + tt + 1) * 128],
                    ct_ps[0:E, 0:128]))



def emit_moe_ffn(p, C, xnbf_hbm, x_seed_hbm, out_hbm, combT, experts, tag="moe"):
    sel32 = C["sel32"]
    NG = 4
    CPG = NT // NG
    with p.scoped():
        max_fti = max(ex["FTI"] for ex in experts)
        xnb = [[p.sb([128, CH], BF16, tag + "xnb") for _ in range(DTI)]
               for _ in range(CPG)]
        acc = [[p.sb([128, CH], F32, tag + "acc") for _ in range(DTI)]
               for _ in range(CPG)]
        wg = [p.sb([128, DTI * 128 * max_fti], BF16, tag + "wg")
              for _ in range(2)]
        wu = [p.sb([128, DTI * 128 * max_fti], BF16, tag + "wu")
              for _ in range(2)]
        wd = [p.sb([128, max_fti * 128 * DTI], BF16, tag + "wd")
              for _ in range(2)]
        ps_g = [p.psum([128, CH], F32, tag + "psg") for _ in range(2)]
        ps_u = [p.psum([128, CH], F32, tag + "psu") for _ in range(3)]
        ps_cb = p.psum([128, CH], F32, tag + "pscb")
        cb_s = [p.sb([128, CH], F32, tag + "cbs") for _ in range(2)]
        ps_dn = [p.psum([128, CH], F32, tag + "psdn") for _ in range(2)]
        sg_t = [p.sb([128, CH], BF16, tag + "sg") for _ in range(2)]
        tmp_t = p.sb([128, CH], F32, tag + "tmp")
        h_t = [p.sb([128, CH], BF16, tag + "h") for _ in range(max_fti)]

        def ldw(gp, ex, par):
            FTI = ex["FTI"]
            p.dma_pf(gp, wg[par][:, 0:DTI * 128 * FTI].rearrange(
                "p (a f) -> p a f", a=DTI),
                ex["gate"].rearrange("(a p) f -> p a f", p=128))
            if ex["up"] is not None:
                p.dma_pf(gp, wu[par][:, 0:DTI * 128 * FTI].rearrange(
                    "p (a f) -> p a f", a=DTI),
                    ex["up"].rearrange("(a p) f -> p a f", p=128))
            p.dma_pf(gp, wd[par][:, 0:FTI * 128 * DTI].rearrange(
                "p (a f) -> p a f", a=FTI),
                ex["down"].rearrange("(a p) f -> p a f", p=128))
            return None

        for g in range(NG):
            chunks = list(range(g * CPG, (g + 1) * CPG))
            pf_after = {}

            def seed(s, g=g, chunks=chunks):
                for ci, c in enumerate(chunks):
                    for i in range(DTI):
                        p.dma(s, acc[ci][i][:], x_seed_hbm[i, :, csl(c)])
                        p.dma(s, xnb[ci][i][:], xnbf_hbm[i, :, csl(c)])
                return None
            p.stage(sync=seed, gpsimd=lambda gp: ldw(gp, experts[0], 0))
            pf_after[0] = p.cnt["pf"]

            for ei, ex in enumerate(experts):
                FTI = ex["FTI"]
                par = ei % 2
                if ei + 1 < len(experts):
                    p.stage(gpsimd=lambda gp, ei=ei: ldw(
                        gp, experts[ei + 1], (ei + 1) % 2))
                    pf_after[ei + 1] = p.cnt["pf"]

                units = [(ci, fi) for ci in range(CPG) for fi in range(FTI)]
                NU = len(units)
                pf_tgt = pf_after[ei]

                for u in range(NU + 2):
                    fns = {}
                    if u < NU:
                        ci_u, fi_u = units[u]

                        def tf(t, ex=ex, par=par, ci=ci_u, fi=fi_u, FTI=FTI,
                               u=u, pf_tgt=pf_tgt):
                            if u == 0:
                                t.wait_ge(p.sem("pf"), pf_tgt)
                            c = chunks[ci]
                            if ex["e"] is not None and fi == 0:
                                t.matmul(ps_cb[:],
                                         sel32[:, ex["e"] * 128:
                                               (ex["e"] + 1) * 128],
                                         combT[0:32, csl(c)],
                                         start=True, stop=True)
                            for i in range(DTI):
                                last = t.matmul(
                                    ps_g[u % 2][:],
                                    wg[par][:, (i * FTI + fi) * 128:
                                            (i * FTI + fi + 1) * 128],
                                    xnb[ci][i][:], start=(i == 0),
                                    stop=(i == DTI - 1))
                            if ex["up"] is not None:
                                for i in range(DTI):
                                    last = t.matmul(
                                        ps_u[u % 3][:],
                                        wu[par][:, (i * FTI + fi) * 128:
                                                (i * FTI + fi + 1) * 128],
                                        xnb[ci][i][:], start=(i == 0),
                                        stop=(i == DTI - 1))
                            return last
                        fns["tensor"] = tf
                    if 0 <= u - 1 < NU:
                        us = u - 1
                        fns["scalar"] = (lambda sc, us=us: sc.activation(
                            sg_t[us % 2][:], ps_g[us % 2][:], AF.Silu))
                    if 0 <= u - 2 < NU:
                        uv = u - 2
                        ci_v, fi_v = units[uv]

                        def vf(v, ex=ex, fi=fi_v, uv=uv, ci=ci_v):
                            if ex["up"] is not None:
                                v.tensor_mul(tmp_t[:], sg_t[uv % 2][:],
                                             ps_u[uv % 3][:])
                                if ex["e"] is not None:
                                    return v.tensor_mul(h_t[fi][:], tmp_t[:],
                                                        cb_s[ci % 2][:])
                                return v.tensor_copy(h_t[fi][:], tmp_t[:])
                            if ex["e"] is not None:
                                return v.tensor_mul(h_t[fi][:], sg_t[uv % 2][:],
                                                    cb_s[ci % 2][:])
                            return v.tensor_copy(h_t[fi][:], sg_t[uv % 2][:])
                        fns["vector"] = vf
                    p.stage(**fns)
                    if u < NU and ex["e"] is not None and units[u][1] == 0:
                        p.stage(vector=lambda v, pcb=units[u][0] % 2:
                                v.tensor_copy(cb_s[pcb][:], ps_cb[:]))

                    if 0 <= u - 2 < NU and units[u - 2][1] == FTI - 1:
                        ci_d = units[u - 2][0]
                        for w2 in range(DTI + 1):
                            fns2 = {}
                            if w2 < DTI:
                                def dmm(t, ex=ex, par=par, i=w2, FTI=FTI,
                                        w2=w2):
                                    for fi in range(FTI):
                                        last = t.matmul(
                                            ps_dn[w2 % 2][:],
                                            wd[par][:, (fi * DTI + i) * 128:
                                                    (fi * DTI + i + 1) * 128],
                                            h_t[fi][:], start=(fi == 0),
                                            stop=(fi == FTI - 1))
                                    return last
                                fns2["tensor"] = dmm
                            if 0 <= w2 - 1 < DTI:
                                def dacc(v, ci=ci_d, i=w2 - 1, w2=w2):
                                    return v.tensor_add(acc[ci][i][:],
                                                        acc[ci][i][:],
                                                        ps_dn[(w2 - 1) % 2][:])
                                fns2["vector"] = dacc
                            p.stage(**fns2)

            def store(s, g=g, chunks=chunks):
                for ci, c in enumerate(chunks):
                    for i in range(DTI):
                        p.dma(s, out_hbm[i, :, csl(c)], acc[ci][i][:])
                return None
            p.stage(sync=store)



def emit_mamba(p, C, W, hbf_hbm, xn32_hbm, ffnacc_hbm, x1_hbm, scratch):
    DI = 2048
    UTI = 16
    ucbf, zbf, dtbf, dtubf, y3bf = (scratch["ucbf"], scratch["zbf"],
                                    scratch["dtbf"], scratch["dtubf"],
                                    scratch["y3bf"])
    sel96 = p.sb([96, 32 * 128], F32, "sel96")

    def selb(g):
        g.memset(sel96[:], 0.0)
        return g.affine_select(
            out=sel96[:].rearrange("p (e c) -> p e c", e=32),
            in_=sel96[:].rearrange("p (e c) -> p e c", e=32),
            pattern=[[-1, 32], [0, 128]], base=-64, channel_multiplier=1,
            compare_op=ALU.not_equal, fill=1.0)
    p.stage(gpsimd=selb)

    cw = p.sb([128, 64], F32, "mcw")
    cb = p.sb([128, 16], F32, "mcb")
    dcol = p.sb([128, 16], F32, "mD")
    dtb = p.sb([128, 16], F32, "mdtb")
    a_t = p.sb([128, 16 * 16], F32, "mA")

    def ldc(s):
        p.dma(s, cw[:], W["conv_w"])
        p.dma(s, cb[:], W["conv_b"])
        p.dma(s, dcol[:], W["D"])
        p.dma(s, dtb[:], W["dt_b"])
        p.dma(s, a_t[:], W["A_log"])
        return None
    p.stage(sync=ldc)
    p.stage(scalar=lambda sc: sc.activation(a_t[:], a_t[:], AF.Exp))
    p.stage(vector=lambda v: v.tensor_scalar(a_t[:], a_t[:], -1.0, None,
                                             ALU.mult))

    with p.scoped():
        hb = [p.sb([128, T], BF16, "mhb") for _ in range(DTI)]
        wip = p.sb([128, DTI * 128], BF16, "mwip")
        ps_x = [p.psum([128, CH], F32, "mpsx") for _ in range(2)]
        u_raw = p.sb([128, T], F32, "muraw")
        cvt = p.sb([128, SEQ], F32, "mcvt")
        ucb = p.sb([128, SEQ], BF16, "mucb")
        zb = [p.sb([128, CH], BF16, "mzb") for _ in range(2)]

        def ldh(s):
            for i in range(DTI):
                p.dma(s, hb[i][:], hbf_hbm[i, :, :])
            return None
        p.stage(sync=ldh)

        for fb in range(32):
            is_u = fb < 16
            p.stage(gpsimd=lambda gp, fb=fb: (p.dma(
                gp, wip[:].rearrange("p (a f) -> p a f", a=DTI),
                W["in_proj"].rearrange("(a p) f -> p a f", p=128)
                [:, :, fb * 128:(fb + 1) * 128]), None)[1])

            for u in range(NT + 1):
                fns = {}
                if u < NT:
                    def mm(t, fb=fb, c=u):
                        for i in range(DTI):
                            last = t.matmul(ps_x[c % 2][:],
                                            wip[:, i * 128:(i + 1) * 128],
                                            hb[i][:, csl(c)], start=(i == 0),
                                            stop=(i == DTI - 1))
                        return last
                    fns["tensor"] = mm
                if 0 <= u - 1 < NT:
                    c = u - 1
                    if is_u:
                        fns["vector"] = (lambda v, c=c: v.tensor_copy(
                            u_raw[:, csl(c)], ps_x[c % 2][:]))
                    else:
                        fns["scalar"] = (lambda sc, c=c: sc.activation(
                            zb[c % 2][:], ps_x[c % 2][:], AF.Copy))
                        if c >= 1:
                            fns["sync"] = (lambda s, fb=fb, c=c - 1: (p.dma(
                                s, zbf[fb - 16, :, csl(c)], zb[c % 2][:]),
                                None)[1])
                p.stage(**fns)
            if not is_u:
                p.stage(sync=lambda s, fb=fb: (p.dma(
                    s, zbf[fb - 16, :, csl(NT - 1)], zb[(NT - 1) % 2][:]),
                    None)[1])

            if is_u:
                for r in range(RPC):
                    base = r * SEQ

                    def conv(v, fb=fb, base=base):
                        u = u_raw
                        v.tensor_scalar(cvt[:], u[:, base:base + SEQ],
                                        cw[:, fb * 4 + 3:fb * 4 + 4], None,
                                        ALU.mult)
                        v.scalar_tensor_tensor(
                            cvt[:, 1:SEQ], u[:, base:base + SEQ - 1],
                            cw[:, fb * 4 + 2:fb * 4 + 3], cvt[:, 1:SEQ],
                            ALU.mult, ALU.add)
                        v.scalar_tensor_tensor(
                            cvt[:, 2:SEQ], u[:, base:base + SEQ - 2],
                            cw[:, fb * 4 + 1:fb * 4 + 2], cvt[:, 2:SEQ],
                            ALU.mult, ALU.add)
                        return v.scalar_tensor_tensor(
                            cvt[:, 3:SEQ], u[:, base:base + SEQ - 3],
                            cw[:, fb * 4 + 0:fb * 4 + 1], cvt[:, 3:SEQ],
                            ALU.mult, ALU.add)
                    p.stage(vector=conv)
                    p.stage(scalar=lambda sc, fb=fb: sc.activation(
                        ucb[:], cvt[:], AF.Silu, bias=cb[:, fb:fb + 1]))
                    p.stage(sync=lambda s, fb=fb, base=base: (p.dma(
                        s, ucbf[fb, :, base:base + SEQ], ucb[:]), None)[1])

    xdbl = p.sb([96, T], F32, "mxdbl")
    xdbl_bf = p.sb([64, T], BF16, "mxdblbf")
    with p.scoped():
        wxp = p.sb([128, 16 * 96], BF16, "mwxp")
        uct = [[p.sb([128, CH], BF16, "muct") for _ in range(UTI)]
               for _ in range(2)]
        ps_xd = [p.psum([96, CH], F32, "mpsxd") for _ in range(2)]

        p.stage(gpsimd=lambda gp: (p.dma(
            gp, wxp[:].rearrange("p (a f) -> p a f", a=16),
            W["x_proj"].rearrange("(a p) f -> p a f", p=128)), None)[1])

        def ldu(s, c, par):
            for k in range(UTI):
                p.dma(s, uct[par][k][:], ucbf[k, :, csl(c)])
            return None
        p.stage(sync=lambda s: ldu(s, 0, 0))
        for u in range(NT + 1):
            fns = {}
            if u < NT:
                if u + 1 < NT:
                    fns["sync"] = (lambda s, u=u: ldu(s, u + 1, (u + 1) % 2))

                def mm(t, c=u):
                    for k in range(UTI):
                        last = t.matmul(ps_xd[c % 2][:],
                                        wxp[:, k * 96:(k + 1) * 96],
                                        uct[c % 2][k][:], start=(k == 0),
                                        stop=(k == 15))
                    return last
                fns["tensor"] = mm
            if 0 <= u - 1 < NT:
                c = u - 1
                fns["vector"] = (lambda v, c=c: v.tensor_copy(
                    xdbl[:, csl(c)], ps_xd[c % 2][:]))
                fns["scalar"] = (lambda sc, c=c: sc.activation(
                    xdbl_bf[:, csl(c)], ps_xd[c % 2][0:64, :], AF.Copy))
            p.stage(**fns)

    with p.scoped():
        wdt = p.sb([64, 16 * 128], BF16, "mwdt")
        ps_dt = [p.psum([128, CH], F32, "mpsdt") for _ in range(2)]
        ta = [p.sb([128, CH], F32, "mta") for _ in range(2)]
        tb = p.sb([128, CH], F32, "mtb")
        td = [p.sb([128, CH], F32, "mtd") for _ in range(2)]
        dtf = p.sb([128, CH], F32, "mdtf")
        uct1 = [p.sb([128, CH], BF16, "muct1") for _ in range(2)]
        dtb16 = [p.sb([128, CH], BF16, "mdtb16") for _ in range(2)]
        dtub16 = [p.sb([128, CH], BF16, "mdtub16") for _ in range(2)]

        p.stage(gpsimd=lambda gp: (p.dma(
            gp, wdt[:], W["dt_w"]), None)[1])
        units3 = [(f, c) for f in range(16) for c in range(NT)]
        N3 = len(units3)
        p.stage(sync=lambda s: (p.dma(
            s, uct1[0][:], ucbf[units3[0][0], :, csl(units3[0][1])]), None)[1])
        for u in range(N3 + 3):
            fns = {}
            if u < N3:
                f_u, c_u = units3[u]

                def mm3(t, f=f_u, c=c_u, u=u):
                    return t.matmul(ps_dt[u % 2][:],
                                    wdt[:, f * 128:(f + 1) * 128],
                                    xdbl_bf[:, csl(c)], start=True, stop=True)
                fns["tensor"] = mm3
                sy = []
                if u + 1 < N3:
                    sy.append((u + 1,))
            if 0 <= u - 1 < N3:
                us = u - 1
                f_s = units3[us][0]

                def sp3(sc, f=f_s, us=us):
                    sc.activation(ta[us % 2][:], ps_dt[us % 2][:], AF.Abs,
                                  bias=dtb[:, f:f + 1])
                    sc.activation(tb[:], ta[us % 2][:], AF.Exp, scale=-1.0)
                    sc.activation(ta[us % 2][:], tb[:], AF.Ln, bias=1.0)
                    return sc.activation(td[us % 2][:], ps_dt[us % 2][:],
                                         AF.Relu, bias=dtb[:, f:f + 1])
                fns["scalar"] = sp3
            if 0 <= u - 2 < N3:
                uv = u - 2

                def fv3(v, uv=uv):
                    v.tensor_add(dtf[:], ta[uv % 2][:], td[uv % 2][:])
                    v.tensor_copy(dtb16[uv % 2][:], dtf[:])
                    return v.tensor_mul(dtub16[uv % 2][:], dtf[:],
                                        uct1[uv % 2][:])
                fns["vector"] = fv3

            def sy3(s, u=u):
                if u + 1 < N3:
                    f2, c2 = units3[u + 1]
                    p.dma(s, uct1[(u + 1) % 2][:], ucbf[f2, :, csl(c2)])
                if 0 <= u - 3 < N3:
                    f3, c3 = units3[u - 3]
                    p.dma(s, dtbf[f3, :, csl(c3)], dtb16[(u - 3) % 2][:])
                    p.dma(s, dtubf[f3, :, csl(c3)], dtub16[(u - 3) % 2][:])
                return None
            fns["sync"] = sy3
            p.stage(**fns)

    HS = SEQ // 2
    with p.scoped():
        dt_t = p.sb([128, SEQ], BF16, "mdt")
        dtu_t = p.sb([128, SEQ], BF16, "mdtu")
        ps_b = [p.psum([128, HS], F32, "mpsb") for _ in range(2)]
        ps_c = [p.psum([128, HS], F32, "mpsc") for _ in range(2)]
        da_s = [p.sb([128, HS], BF16, "mda") for _ in range(2)]
        dbu_t = p.sb([128, HS], BF16, "mdbu")
        h_t = [p.sb([128, HS], BF16, "mh") for _ in range(2)]
        hc_t = p.sb([128, HS], BF16, "mhc")
        yacc = p.sb([128, SEQ], BF16, "myacc")
        zero_col = p.sb([128, 1], F32, "mzc")
        uct2 = p.sb([128, SEQ], BF16, "muct2")
        zt2 = p.sb([128, SEQ], BF16, "mzt2")
        szt = p.sb([128, SEQ], F32, "mszt")
        y2t = p.sb([128, SEQ], F32, "my2")
        y3t = p.sb([128, SEQ], BF16, "my3")

        p.stage(gpsimd=lambda g: g.memset(zero_col[:], 0.0))

        for r in range(RPC):
            rsl = slice(r * SEQ, (r + 1) * SEQ)
            for f in range(16):
                def lddt(s, f=f, rsl=rsl):
                    p.dma(s, dt_t[:], dtbf[f, :, rsl])
                    p.dma(s, dtu_t[:], dtubf[f, :, rsl])
                    return None
                def ldz(s, f=f, rsl=rsl):
                    p.dma(s, uct2[:], ucbf[f, :, rsl])
                    p.dma(s, zt2[:], zbf[f, :, rsl])
                    return None
                p.stage(sync=lddt)

                units = [(n, tc) for n in range(16) for tc in range(2)]
                NV = len(units)

                def bcast(t, v, r=r):
                    n, tc = units[v]
                    for j in range(2):
                        jsl = slice(tc * HS + j * CH, tc * HS + (j + 1) * CH)
                        psl = slice(j * CH, (j + 1) * CH)
                        rs = slice(r * SEQ + tc * HS + j * CH,
                                   r * SEQ + tc * HS + (j + 1) * CH)
                        t.matmul(ps_b[v % 2][:, psl],
                                 sel96[:, n * 128:(n + 1) * 128],
                                 xdbl[:, rs], start=True, stop=True)
                        last = t.matmul(ps_c[v % 2][:, psl],
                                        sel96[:, (16 + n) * 128:(17 + n) * 128],
                                        xdbl[:, rs], start=True, stop=True)
                    return last

                for v in range(NV + 1):
                    fns = {}
                    if v < NV:
                        def tfm(t, v=v):
                            return bcast(t, v)
                        fns["tensor"] = tfm

                        def sfm(sc, v=v, f=f):
                            n, tc = units[v]
                            return sc.activation(
                                da_s[v % 2][:], dt_t[:, tc * HS:(tc + 1) * HS],
                                AF.Exp,
                                scale=a_t[:, f * 16 + n:f * 16 + n + 1])
                        fns["scalar"] = sfm
                    if 0 <= v - 1 < NV:
                        vv_ = v - 1

                        def vfm(v_, vv_=vv_):
                            n, tc = units[vv_]
                            tsl = slice(tc * HS, (tc + 1) * HS)
                            v_.tensor_mul(dbu_t[:],
                                          dtu_t[:, tsl], ps_b[vv_ % 2][:])
                            init = (zero_col[:, 0:1] if tc == 0
                                    else h_t[(vv_ - 1) % 2][:, HS - 1:HS])
                            v_.tensor_tensor_scan(h_t[vv_ % 2][:], da_s[vv_ % 2][:],
                                                  dbu_t[:], init,
                                                  ALU.mult, ALU.add)
                            v_.tensor_mul(hc_t[:], h_t[vv_ % 2][:],
                                          ps_c[vv_ % 2][:])
                            if units[vv_][0] == 0:
                                return v_.tensor_copy(yacc[:, tsl], hc_t[:])
                            return v_.tensor_add(yacc[:, tsl], yacc[:, tsl],
                                                 hc_t[:])
                        fns["vector"] = vfm
                    if v == NV:
                        fns["sync"] = lambda s, f=f, rsl=rsl: ldz(s, f, rsl)
                    p.stage(**fns)

                p.stage(scalar=lambda sc: sc.activation(szt[:], zt2[:], AF.Silu))

                def fin_v(v, f=f):
                    v.scalar_tensor_tensor(y2t[:], uct2[:], dcol[:, f:f + 1],
                                           yacc[:], ALU.mult, ALU.add)
                    return v.tensor_mul(y3t[:], y2t[:], szt[:])
                p.stage(vector=fin_v)
                p.stage(sync=lambda s, f=f, rsl=rsl: (p.dma(
                    s, y3bf[f, :, rsl], y3t[:]), None)[1])

    with p.scoped():
        wop = p.sb([128, 16 * 1024], BF16, "mwop")
        y3c = [[p.sb([128, CH], BF16, "my3c") for _ in range(16)]
               for _ in range(2)]
        xnc = [p.sb([128, CH], F32, "mxnc") for _ in range(2)]
        fac = [p.sb([128, CH], F32, "mfac") for _ in range(2)]
        ps_o = [p.psum([128, CH], F32, "mpso") for _ in range(2)]
        x1t = [p.sb([128, CH], F32, "mx1") for _ in range(2)]

        p.stage(gpsimd=lambda gp: (p.dma(
            gp, wop[:].rearrange("p (a f) -> p a f", a=16),
            W["out_proj"].rearrange("(a p) f -> p a f", p=128)), None)[1])

        units5 = [(c, i) for c in range(NT) for i in range(DTI)]
        N5 = len(units5)

        def ldxf(s, u):
            c, i = units5[u]
            p.dma(s, xnc[u % 2][:], xn32_hbm[i, :, csl(c)])
            p.dma(s, fac[u % 2][:], ffnacc_hbm[i, :, csl(c)])
            return None

        def ldy(s, c, par):
            for k in range(16):
                p.dma(s, y3c[par][k][:], y3bf[k, :, csl(c)])
            return None

        p.stage(sync=lambda s: (ldy(s, 0, 0), ldxf(s, 0), None)[2])
        for u in range(N5 + 2):
            fns = {}
            if u < N5:
                c_u, i_u = units5[u]

                def mm(t, c=c_u, i=i_u, u=u):
                    for k in range(16):
                        last = t.matmul(ps_o[u % 2][:],
                                        wop[:, (k * 8 + i) * 128:
                                            (k * 8 + i + 1) * 128],
                                        y3c[(c % 2)][k][:], start=(k == 0),
                                        stop=(k == 15))
                    return last
                fns["tensor"] = mm
            if 0 <= u - 1 < N5:
                uv = u - 1

                def fin(v, uv=uv):
                    v.tensor_add(x1t[uv % 2][:], ps_o[uv % 2][:],
                                 xnc[uv % 2][:])
                    return v.tensor_add(x1t[uv % 2][:], x1t[uv % 2][:],
                                        fac[uv % 2][:])
                fns["vector"] = fin

            def sy5(s, u=u):
                if u + 1 < N5:
                    ldxf(s, u + 1)
                    if units5[u + 1][1] == 0 and u + 2 < N5 + 16:
                        pass
                if u + 8 < N5 and units5[u + 8][1] == 0:
                    ldy(s, units5[u + 8][0], units5[u + 8][0] % 2)
                if 0 <= u - 2 < N5:
                    c2, i2 = units5[u - 2]
                    p.dma(s, x1_hbm[i2, :, csl(c2)], x1t[(u - 2) % 2][:])
                return None
            fns["sync"] = sy5
            p.stage(**fns)



def declare_w(p, names):
    win = {}
    for nm, sh in names:
        win[nm] = p.dram(nm, sh, F32, kind="ExternalInput")
    return win


def emit_block0(p, C, win, x, x1, debug=False):
    dk = "ExternalOutput" if debug else "Internal"
    xn32 = p.dram("b0_xn32", [DTI, 128, T], F32, kind=dk)
    xnbf = p.dram("b0_xnbf", [DTI, 128, T], BF16)
    hbf = p.dram("b0_hbf", [DTI, 128, T], BF16, kind=dk)
    ffnacc = p.dram("b0_ffnacc", [DTI, 128, T], F32, kind=dk)
    scratch = dict(
        ucbf=p.dram("b0_ucbf", [16, 128, T], BF16, kind=dk),
        zbf=p.dram("b0_zbf", [16, 128, T], BF16),
        dtbf=p.dram("b0_dtbf", [16, 128, T], BF16, kind=dk),
        dtubf=p.dram("b0_dtubf", [16, 128, T], BF16),
        y3bf=p.dram("b0_y3bf", [16, 128, T], BF16, kind=dk),
    )

    emit_ln(p, C, x, xn32, xnbf, hbf_hbm=hbf, normw_hbm=win["l0a_norm_w"])

    with p.scoped():
        combT = p.sb([32, T], F32, "combT")
        emit_router_grouped(p, C, xn32, win["l0f_rwT"], win["l0f_bias"], combT,
                            E=16, GRP=4, K=4, scale=2.5)
        experts = []
        for e in range(16):
            experts.append(dict(gate=win["l0f_gate"][e], up=win["l0f_up"][e],
                                down=win["l0f_down"][e], e=e, FTI=8))
        for half in range(2):
            fs = slice(half * 1024, (half + 1) * 1024)
            experts.append(dict(gate=win["l0f_shg"][:, fs],
                                up=win["l0f_shu"][:, fs],
                                down=win["l0f_shd"][fs, :], e=None, FTI=8))
        emit_moe_ffn(p, C, xnbf, x, ffnacc, combT, experts, tag="m0f")

    W = dict(in_proj=win["l0a_in_proj"], conv_w=win["l0a_conv_w"],
             conv_b=win["l0a_conv_b"], x_proj=win["l0a_x_proj"],
             dt_w=win["l0a_dt_w"], dt_b=win["l0a_dt_b"], A_log=win["l0a_A_log"],
             D=win["l0a_D"], out_proj=win["l0a_out_proj"])
    with p.scoped():
        emit_mamba(p, C, W, hbf, xn32, ffnacc, x1, scratch)


W_BLOCK0 = [
    ("l0a_norm_w", [128, DTI]), ("l0a_in_proj", [D, 4096]),
    ("l0a_conv_w", [128, 64]),
    ("l0a_conv_b", [128, 16]), ("l0a_x_proj", [2048, 96]),
    ("l0a_dt_w", [64, 2048]),
    ("l0a_dt_b", [128, 16]), ("l0a_A_log", [128, 256]), ("l0a_D", [128, 16]),
    ("l0a_out_proj", [2048, D]),
    ("l0f_rwT", [D, 16]), ("l0f_bias", [16]),
    ("l0f_gate", [16, D, 1024]), ("l0f_up", [16, D, 1024]),
    ("l0f_down", [16, 1024, D]),
    ("l0f_shg", [D, 2048]), ("l0f_shu", [D, 2048]), ("l0f_shd", [2048, D]),
]


def build_block0(debug=False):
    p = Prog()
    x = p.dram("x", [DTI, 128, T], F32, kind="ExternalInput")
    x1 = p.dram("x1", [DTI, 128, T], F32, kind="ExternalOutput")
    win = declare_w(p, W_BLOCK0)
    C = emit_consts(p)
    emit_block0(p, C, win, x, x1, debug=debug)
    p.final_wait()
    return p.finish()



def _fm_shard(x_full):
    out = []
    for c in range(NCORE):
        xc = x_full[2 * c:2 * c + 2]
        fm = np.ascontiguousarray(
            xc.transpose(2, 0, 1).reshape(DTI, 128, T))
        out.append(fm)
    return out


def _fm_unshard(shards):
    x = np.empty((16, SEQ, D), np.float32)
    for c in range(NCORE):
        fm = shards[c].reshape(D, RPC, SEQ)
        x[2 * c:2 * c + 2] = fm.transpose(1, 2, 0)
    return x


def prep_block0_inputs(inputs):
    f32 = np.float32
    def col128(v):
        return np.ascontiguousarray(np.asarray(v, f32).reshape(-1, 128).T)

    w = {}
    w["l0a_norm_w"] = col128(inputs["L0_attn_norm_w"])
    w["l0a_in_proj"] = np.ascontiguousarray(inputs["L0_attn_in_proj_w"], f32)
    w["l0a_conv_w"] = np.ascontiguousarray(
        np.asarray(inputs["L0_attn_conv_w"], f32).reshape(16, 128, 4)
        .transpose(1, 0, 2).reshape(128, 64))
    w["l0a_conv_b"] = col128(inputs["L0_attn_conv_b"])
    w["l0a_x_proj"] = np.ascontiguousarray(inputs["L0_attn_x_proj_w"], f32)
    w["l0a_dt_w"] = np.ascontiguousarray(inputs["L0_attn_dt_proj_w"], f32)
    w["l0a_dt_b"] = col128(inputs["L0_attn_dt_proj_b"])
    w["l0a_A_log"] = np.ascontiguousarray(
        np.asarray(inputs["L0_attn_A_log"], f32).reshape(16, 128, 16)
        .transpose(1, 0, 2).reshape(128, 256))
    w["l0a_D"] = col128(inputs["L0_attn_D"])
    w["l0a_out_proj"] = np.ascontiguousarray(inputs["L0_attn_out_proj_w"], f32)
    w["l0f_rwT"] = np.ascontiguousarray(inputs["L0_ffn_router_w"].T, f32)
    w["l0f_bias"] = np.ascontiguousarray(inputs["L0_ffn_bias"], f32)
    w["l0f_gate"] = np.ascontiguousarray(inputs["L0_ffn_gate_w"], f32)
    w["l0f_up"] = np.ascontiguousarray(inputs["L0_ffn_up_w"], f32)
    w["l0f_down"] = np.ascontiguousarray(inputs["L0_ffn_down_w"], f32)
    w["l0f_shg"] = np.ascontiguousarray(inputs["L0_ffn_sh_gate_w"], f32)
    w["l0f_shu"] = np.ascontiguousarray(inputs["L0_ffn_sh_up_w"], f32)
    w["l0f_shd"] = np.ascontiguousarray(inputs["L0_ffn_sh_down_w"], f32)
    return w


def run_block0(inputs, x_full, debug=False):
    w = prep_block0_inputs(inputs)
    shards = _fm_shard(x_full)
    nc = build_block0(debug=debug)
    in_maps = [dict(w, x=shards[c]) for c in range(NCORE)]
    res = run_bass_kernel_spmd(nc, in_maps, core_ids=list(range(NCORE)))
    if debug:
        return res
    return _fm_unshard([res.results[c]["x1"] for c in range(NCORE)])


def kernel(**inputs):
    _ = inputs.get("L0_attn_hormone_vectors")
    x = np.asarray(inputs["L0_attn_x"], np.float32)
    wA = dict(prep_block0_inputs(inputs))
    w1 = prep_block1_inputs(inputs)
    for k in W_BLOCK1F:
        wA[k[0]] = w1[k[0]]
    wB = {k[0]: w1[k[0]] for k in W_BLOCK1A}
    wC = dict(prep_block2_inputs(inputs))
    wC.update(prep_block3_inputs(inputs))

    shards = _fm_shard(x)
    ncA = build_blockA()
    resA = run_bass_kernel_spmd(ncA, [dict(wA, x=shards[c]) for c in range(NCORE)],
                                core_ids=list(range(NCORE)))
    x1 = _fm_unshard([resA.results[c]["x1"] for c in range(NCORE)])
    y1p_shards = [resA.results[c]["y1p"] for c in range(NCORE)]

    ncB = build_blockB()
    ashards = _fm_shard_tau(x1)
    resB = run_bass_kernel_spmd(ncB, [dict(wB, xa=ashards[c])
                                      for c in range(NCORE)],
                               core_ids=list(range(NCORE)))
    att = _fm_unshard_tau([resB.results[c]["att1"] for c in range(NCORE)])
    att_shards = _fm_shard(att)

    ncC = build_blockC()
    resC = run_bass_kernel_spmd(
        ncC, [dict(wC, y1p=y1p_shards[c], att=att_shards[c])
              for c in range(NCORE)], core_ids=list(range(NCORE)))
    return _fm_unshard([resC.results[c]["x4"] for c in range(NCORE)])



def emit_add(p, a_hbm, b_hbm, out_hbm, nt=DTI):
    with p.scoped():
        at = p.sb([128, CH], F32, "adda")
        bt = p.sb([128, CH], F32, "addb")
        ot = p.sb([128, CH], F32, "addo")
        for i in range(nt):
            for c in range(NT):
                def ld(s, i=i, c=c):
                    p.dma(s, at[:], a_hbm[i, :, csl(c)])
                    p.dma(s, bt[:], b_hbm[i, :, csl(c)])
                    return None
                p.stage(sync=ld)
                p.stage(vector=lambda v: v.tensor_add(ot[:], at[:], bt[:]))
                p.stage(sync=lambda s, i=i, c=c: (p.dma(
                    s, out_hbm[i, :, csl(c)], ot[:]), None)[1])



def emit_router_top2softmax(p, C, xn32_hbm, rwT_hbm, combT, E):
    ones_row, identf = C["ones_row"], C["identf"]
    with p.scoped():
        rwt = p.sb([128, DTI * E], F32, "r2wt")
        xs = [p.sb([128, CH], F32, "r2xs") for _ in range(DTI)]
        lg = p.psum([128, CH], F32, "r2lg")
        lgs = p.sb([128, E], F32, "r2lgs")
        m1 = p.sb([128, 1], F32, "r2m1")
        m2 = p.sb([128, 1], F32, "r2m2")
        sel1 = p.sb([128, E], F32, "r2s1")
        sel2 = p.sb([128, E], F32, "r2s2")
        msk = p.sb([128, E], F32, "r2msk")
        w1c = p.sb([128, 1], F32, "r2w1")
        comb = p.sb([128, E], F32, "r2comb")
        ct_ps = p.psum([E, CH], F32, "r2ct")

        p.stage(sync=lambda s: (p.dma(
            s, rwt[:].rearrange("p (a e) -> p a e", a=DTI),
            rwT_hbm.rearrange("(a p) e -> p a e", p=128)), None)[1])
        p.stage(vector=lambda v: v.memset(combT[:], 0.0))

        for c in range(NT):
            def ld(s, c=c):
                for i in range(DTI):
                    p.dma(s, xs[i][:], xn32_hbm[i, :, csl(c)])
                return None
            p.stage(sync=ld)

            for tt in range(4):
                def mm(t, c=c, tt=tt):
                    for i in range(DTI):
                        last = t.matmul(
                            lg[0:128, 0:E], xs[i][:, tt * 128:(tt + 1) * 128],
                            rwt[:, i * E:(i + 1) * E],
                            start=(i == 0), stop=(i == DTI - 1))
                    return last
                p.stage(tensor=mm)

                def route_v(v):
                    def W(instr):
                        instr.then_inc(p.sem("vv"), 1)
                        p.cnt["vv"] += 1
                        v.wait_ge(p.sem("vv"), p.cnt["vv"])
                        return instr
                    W(v.tensor_copy(lgs[:], lg[0:128, 0:E]))
                    W(v.tensor_reduce(m1[:], lgs[:], AX.X, ALU.max))
                    W(v.tensor_scalar(sel1[:], lgs[:], m1[:, 0:1], None,
                                      ALU.is_ge))
                    W(v.scalar_tensor_tensor(msk[:], sel1[:], -BIG, lgs[:],
                                             ALU.mult, ALU.add))
                    W(v.tensor_reduce(m2[:], msk[:], AX.X, ALU.max))
                    W(v.tensor_scalar(sel2[:], msk[:], m2[:, 0:1], None,
                                      ALU.is_ge))
                    W(v.tensor_sub(m2[:], m2[:], m1[:]))
                    return None
                p.stage(vector=route_v)
                p.stage(scalar=lambda sc: sc.activation(m2[:], m2[:], AF.Exp))

                def route_v2(v):
                    def W(instr):
                        instr.then_inc(p.sem("vv"), 1)
                        p.cnt["vv"] += 1
                        v.wait_ge(p.sem("vv"), p.cnt["vv"])
                        return instr
                    W(v.tensor_scalar(w1c[:], m2[:], 1.0, None, ALU.add))
                    W(v.reciprocal(w1c[:], w1c[:]))
                    W(v.tensor_scalar(comb[:], sel1[:], w1c[:, 0:1], None,
                                      ALU.mult))
                    W(v.tensor_scalar(w1c[:], w1c[:], -1.0, None, ALU.mult))
                    W(v.tensor_scalar(w1c[:], w1c[:], 1.0, None, ALU.add))
                    return v.scalar_tensor_tensor(comb[:], sel2[:], w1c[:, 0:1],
                                                  comb[:], ALU.mult, ALU.add)
                p.stage(vector=route_v2)

                p.stage(tensor=lambda t: t.transpose(ct_ps[0:E, 0:128],
                                                     comb[:, 0:E], identf[:]))
                p.stage(vector=lambda v, c=c, tt=tt: v.tensor_copy(
                    combT[0:E, (c * 4 + tt) * 128:(c * 4 + tt + 1) * 128],
                    ct_ps[0:E, 0:128]))



def emit_l2_attn(p, C, W2, xnbf_hbm, ffnacc_hbm, out_hbm):
    ones_row = C["ones_row"]
    ident = C["ident"]
    CT = 2
    with p.scoped():
        bt = [p.sb([128, 1024], F32, "l2bt") for _ in range(CT)]
        cf = [p.sb([128, 256], F32, "l2cf") for _ in range(CT)]
        ps_w = p.psum([128, CH], F32, "l2psw")
        qwb = p.sb([128, DTI * 256], BF16, "l2qw")
        kwb = p.sb([128, DTI * 256], BF16, "l2kw")
        vwb = p.sb([128, DTI * 256], BF16, "l2vw")
        owT = p.sb([128, CT * 1024], BF16, "l2ow")
        sc2 = p.sb([1, 2], F32, "l2sc")
        ldc = p.sb([128, 1], F32, "l2ldc")
        osc = p.sb([128, 1], F32, "l2osc")
        ps_b1 = p.psum([128, CH], F32, "l2psb")

        def ldsc(s):
            p.dma(s, sc2[:, 0:1], W2["decay_logit"][None, :])
            p.dma(s, sc2[:, 1:2], W2["out_scale"][None, :])
            return None
        p.stage(sync=ldsc)
        p.stage(scalar=lambda sc: sc.activation(sc2[:, 0:1], sc2[:, 0:1],
                                                AF.Sigmoid))

        def lnd(sc):
            return sc.activation(sc2[:, 0:1], sc2[:, 0:1], AF.Ln)
        p.stage(scalar=lnd)

        def bc1(t):
            t.matmul(ps_b1[:, 0:1], ones_row[0:1, :], sc2[:, 0:1],
                     start=True, stop=True)
            return t.matmul(ps_b1[:, 1:2], ones_row[0:1, :], sc2[:, 1:2],
                            start=True, stop=True)
        p.stage(tensor=bc1)

        def cp1(v):
            v.tensor_copy(ldc[:], ps_b1[:, 0:1])
            return v.tensor_copy(osc[:], ps_b1[:, 1:2])
        p.stage(vector=cp1)

        wdec = [p.sb([128, CH], F32, "l2wd") for _ in range(19)]
        for mi, m in enumerate(range(-3, 16)):
            def bw(g, mi=mi, m=m):
                return g.iota(wdec[mi][:], [[-1, CH]], base=128 * m - 1,
                              channel_multiplier=1,
                              allow_small_or_imprecise_dtypes=True)
            p.stage(gpsimd=bw)

            def sw(v, mi=mi):
                return v.tensor_scalar(wdec[mi][:], wdec[mi][:], ldc[:, 0:1],
                                       None, ALU.mult)
            p.stage(vector=sw)
            p.stage(scalar=lambda sc, mi=mi: sc.activation(
                wdec[mi][:], wdec[mi][:], AF.Exp))

            def msk(g, mi=mi, m=m):
                return g.affine_select(out=wdec[mi][:], in_=wdec[mi][:],
                                       pattern=[[-1, CH]], base=128 * m,
                                       channel_multiplier=1,
                                       compare_op=ALU.is_gt, fill=0.0)
            p.stage(gpsimd=msk)

        def ldb(s):
            for j in range(CT):
                p.dma(s, bt[j][:], W2["basis_T"][j * 128:(j + 1) * 128, :])
            return None
        p.stage(sync=ldb)

        for nm, dst in (("qcT", qwb), ("kcT", kwb), ("vcT", vwb)):
            def ldcf(s, nm=nm):
                for j in range(CT):
                    p.dma(s, cf[j][:, 0:256], W2[nm][j * 128:(j + 1) * 128, :])
                return None
            p.stage(sync=ldcf)
            for vt in range(DTI):
                def mmw(t, vt=vt):
                    for j in range(CT):
                        last = t.matmul(ps_w[:, 0:256],
                                        bt[j][:, vt * 128:(vt + 1) * 128],
                                        cf[j][:, 0:256], start=(j == 0),
                                        stop=(j == CT - 1))
                    return last
                p.stage(tensor=mmw)
                p.stage(scalar=lambda sc, vt=vt, dst=dst: sc.activation(
                    dst[:, vt * 256:(vt + 1) * 256], ps_w[:, 0:256], AF.Copy))

        def ldcf2(s):
            for j in range(CT):
                p.dma(s, cf[j][:, 0:256], W2["ocT"][j * 128:(j + 1) * 128, :])
            return None
        p.stage(sync=ldcf2)
        for ct in range(CT):
            for vc in range(2):
                def mmo(t, ct=ct, vc=vc):
                    for j in range(CT):
                        last = t.matmul(
                            ps_w[:], cf[j][:, ct * 128:(ct + 1) * 128],
                            bt[j][:, vc * 512:(vc + 1) * 512],
                            start=(j == 0), stop=(j == CT - 1))
                    return last
                p.stage(tensor=mmo)
                p.stage(scalar=lambda sc, ct=ct, vc=vc: sc.activation(
                    owT[:, ct * 1024 + vc * 512: ct * 1024 + (vc + 1) * 512],
                    ps_w[:], AF.Copy))

        xnb = [p.sb([128, CH], BF16, "l2xnb") for _ in range(DTI)]
        qf = [p.sb([128, T], BF16, "l2qf") for _ in range(CT)]
        kf = [p.sb([128, T], BF16, "l2kf") for _ in range(CT)]
        vf = [p.sb([128, T], BF16, "l2vf") for _ in range(CT)]
        for c in range(NT):
            def ldx(s, c=c):
                for i in range(DTI):
                    p.dma(s, xnb[i][:], xnbf_hbm[i, :, csl(c)])
                return None
            p.stage(sync=ldx)
            for wsb, dst in ((qwb, qf), (kwb, kf), (vwb, vf)):
                for ct in range(CT):
                    def mmp(t, wsb=wsb, ct=ct, c=c):
                        for i in range(DTI):
                            last = t.matmul(
                                ps_w[:],
                                wsb[:, i * 256 + ct * 128:
                                    i * 256 + (ct + 1) * 128],
                                xnb[i][:], start=(i == 0), stop=(i == DTI - 1))
                        return last
                    p.stage(tensor=mmp)
                    p.stage(scalar=lambda sc, dst=dst, ct=ct, c=c: sc.activation(
                        dst[ct][:, csl(c)], ps_w[:], AF.Copy))

        vtm = [p.sb([128, 16 * 256], BF16, "l2vtm") for _ in range(RPC)]
        ps_t = p.psum([128, 128], BF16, "l2pst")
        for r in range(RPC):
            for t2 in range(16):
                for ct in range(CT):
                    def tp(t, r=r, t2=t2, ct=ct):
                        return t.transpose(
                            ps_t[:], vf[ct][:, r * SEQ + t2 * 128:
                                            r * SEQ + (t2 + 1) * 128], ident[:])
                    p.stage(tensor=tp)
                    p.stage(vector=lambda v, r=r, t2=t2, ct=ct: v.tensor_copy(
                        vtm[r][:, t2 * 256 + ct * 128:t2 * 256 + (ct + 1) * 128],
                        ps_t[:]))

        ps_s = [p.psum([128, CH], F32, "l2pss") for _ in range(2)]
        ps_r = [p.psum([128, CH], F32, "l2psr") for _ in range(CT)]
        sw_t = [p.sb([128, CH], BF16, "l2sw") for _ in range(2)]
        retr = [p.sb([128, CH], BF16, "l2retr") for _ in range(CT)]
        fac = p.sb([128, CH], F32, "l2fac")
        ot = p.sb([128, CH], F32, "l2ot")
        for r in range(RPC):
            for t1c in range(4):
                t2lo = max(0, 4 * t1c - 3)
                t2s = list(range(t2lo, 16))
                NU2 = len(t2s)
                for u in range(NU2 + 2):
                    fns = {}

                    def tf2(t, u=u, r=r, t1c=t1c, t2s=t2s, NU2=NU2):
                        last = None
                        if u < NU2:
                            t2 = t2s[u]
                            for ct in range(CT):
                                last = t.matmul(
                                    ps_s[u % 2][:],
                                    kf[ct][:, r * SEQ + t2 * 128:
                                           r * SEQ + (t2 + 1) * 128],
                                    qf[ct][:, r * SEQ + t1c * CH:
                                           r * SEQ + (t1c + 1) * CH],
                                    start=(ct == 0), stop=(ct == CT - 1))
                        if 0 <= u - 2 < NU2:
                            idx = u - 2
                            t2 = t2s[idx]
                            for ct in range(CT):
                                last = t.matmul(
                                    ps_r[ct][:],
                                    vtm[r][:, t2 * 256 + ct * 128:
                                           t2 * 256 + (ct + 1) * 128],
                                    sw_t[idx % 2][:], start=(idx == 0),
                                    stop=(t2 == 15), skip_group_check=True)
                        return last
                    fns["tensor"] = tf2
                    if 0 <= u - 1 < NU2:
                        uv = u - 1
                        mi = t2s[uv] - 4 * t1c + 3
                        fns["vector"] = (lambda v, mi=mi, uv=uv: v.tensor_mul(
                            sw_t[uv % 2][:], ps_s[uv % 2][:], wdec[mi][:]))
                    p.stage(**fns)

                def evr(sc, r=r):
                    for ct in range(CT):
                        last = sc.activation(retr[ct][:], ps_r[ct][:], AF.Copy,
                                             scale=osc[:, 0:1])
                    return last
                p.stage(scalar=evr)

                for vt in range(DTI):
                    def mmo2(t, vt=vt):
                        for ct in range(CT):
                            last = t.matmul(
                                ps_w[:],
                                owT[:, ct * 1024 + vt * 128:
                                    ct * 1024 + (vt + 1) * 128],
                                retr[ct][:], start=(ct == 0), stop=(ct == CT - 1))
                        return last
                    p.stage(tensor=mmo2)

                    cg = r * 4 + t1c
                    def ldf(s, vt=vt, cg=cg):
                        p.dma(s, fac[:], ffnacc_hbm[vt, :, csl(cg)])
                        return None
                    p.stage(sync=ldf)
                    p.stage(vector=lambda v: v.tensor_add(ot[:], ps_w[:],
                                                          fac[:]))
                    p.stage(sync=lambda s, vt=vt, cg=cg: (p.dma(
                        s, out_hbm[vt, :, csl(cg)], ot[:]), None)[1])



W_BLOCK2 = [
    ("l2a_basis_T", [256, 1024]), ("l2a_qcT", [256, 256]),
    ("l2a_kcT", [256, 256]), ("l2a_vcT", [256, 256]), ("l2a_ocT", [256, 256]),
    ("l2a_decay", [1]), ("l2a_oscale", [1]),
    ("l2f_rwT", [D, 16]),
    ("l2f_w1T", [16, D, 512]), ("l2f_w2T", [16, 512, D]),
    ("l2f_sfc1T", [D, 2048]), ("l2f_sfc2T", [D, 2048]),
    ("l2f_sfc3T", [2048, D]),
]


def emit_block2(p, C, win, x, x3):
    xn32 = p.dram("b2_xn32", [DTI, 128, T], F32)
    xnbf = p.dram("b2_xnbf", [DTI, 128, T], BF16)
    ffnacc = p.dram("b2_ffnacc", [DTI, 128, T], F32)

    emit_ln(p, C, x, xn32, xnbf)
    with p.scoped():
        combT = p.sb([32, T], F32, "combT")
        emit_router_top2softmax(p, C, xn32, win["l2f_rwT"], combT, E=16)
        experts = []
        for e in range(16):
            experts.append(dict(gate=win["l2f_w1T"][e], up=None,
                                down=win["l2f_w2T"][e], e=e, FTI=4))
        for half in range(2):
            fs = slice(half * 1024, (half + 1) * 1024)
            experts.append(dict(gate=win["l2f_sfc1T"][:, fs],
                                up=win["l2f_sfc2T"][:, fs],
                                down=win["l2f_sfc3T"][fs, :], e=None, FTI=8))
        emit_moe_ffn(p, C, xnbf, x, ffnacc, combT, experts, tag="m2f")

    W2 = dict(basis_T=win["l2a_basis_T"], qcT=win["l2a_qcT"], kcT=win["l2a_kcT"],
              vcT=win["l2a_vcT"], ocT=win["l2a_ocT"],
              decay_logit=win["l2a_decay"], out_scale=win["l2a_oscale"])
    emit_l2_attn(p, C, W2, xnbf, ffnacc, x3)


def build_block2(debug=False, add_in=False):
    p = Prog()
    x_in = p.dram("x", [DTI, 128, T], F32, kind="ExternalInput")
    if add_in:
        x2b = p.dram("x2b", [DTI, 128, T], F32, kind="ExternalInput")
        x = p.dram("xsum", [DTI, 128, T], F32)
    else:
        x2b = None
        x = x_in
    x3 = p.dram("x3", [DTI, 128, T], F32, kind="ExternalOutput")
    win = declare_w(p, W_BLOCK2)
    C = emit_consts(p)
    if add_in:
        emit_add(p, x_in, x2b, x)
    emit_block2(p, C, win, x, x3)
    p.final_wait()
    return p.finish()


def prep_block2_inputs(inputs):
    f32 = np.float32
    w = {}
    w["l2a_basis_T"] = np.ascontiguousarray(inputs["L2_attn_basis"].T, f32)
    w["l2a_qcT"] = np.ascontiguousarray(inputs["L2_attn_q_coeffs"].T, f32)
    w["l2a_kcT"] = np.ascontiguousarray(inputs["L2_attn_k_coeffs"].T, f32)
    w["l2a_vcT"] = np.ascontiguousarray(inputs["L2_attn_v_coeffs"].T, f32)
    w["l2a_ocT"] = np.ascontiguousarray(inputs["L2_attn_o_coeffs"].T, f32)
    w["l2a_decay"] = np.asarray(inputs["L2_attn_decay_logit"], f32).reshape(1)
    w["l2a_oscale"] = np.asarray(inputs["L2_attn_out_scale"], f32).reshape(1)
    w["l2f_rwT"] = np.ascontiguousarray(inputs["L2_ffn_gate_w"].T, f32)
    w["l2f_w1T"] = np.ascontiguousarray(
        np.asarray(inputs["L2_ffn_w1"], f32).transpose(0, 2, 1))
    w["l2f_w2T"] = np.ascontiguousarray(
        np.asarray(inputs["L2_ffn_w2"], f32).transpose(0, 2, 1))
    w["l2f_sfc1T"] = np.ascontiguousarray(inputs["L2_ffn_sfc1"].T, f32)
    w["l2f_sfc2T"] = np.ascontiguousarray(inputs["L2_ffn_sfc2"].T, f32)
    w["l2f_sfc3T"] = np.ascontiguousarray(inputs["L2_ffn_sfc3"].T, f32)
    return w


def run_block2(inputs, x_full, x2b_full=None, debug=False):
    w = prep_block2_inputs(inputs)
    shards = _fm_shard(x_full)
    nc = build_block2(debug=debug, add_in=x2b_full is not None)
    if x2b_full is not None:
        shards2 = _fm_shard(x2b_full)
        in_maps = [dict(w, x=shards[c], x2b=shards2[c]) for c in range(NCORE)]
    else:
        in_maps = [dict(w, x=shards[c]) for c in range(NCORE)]
    res = run_bass_kernel_spmd(nc, in_maps, core_ids=list(range(NCORE)))
    if debug:
        return res
    return _fm_unshard([res.results[c]["x3"] for c in range(NCORE)])



def emit_l3_attn(p, C, W3, xnbf_hbm, ffnacc_hbm, out_hbm, apre_hbm):
    ones_row = C["ones_row"]
    with p.scoped():
        xnb = [p.sb([128, SEQ], BF16, "l3xnb") for _ in range(DTI)]
        onesb = p.sb([128, 1], BF16, "l3ones")
        pen = [p.sb([128, CH], F32, "l3pen") for _ in range(4)]
        qf = [p.sb([128, SEQ], BF16, "l3qf") for _ in range(DTI)]
        kf = [p.sb([128, SEQ], BF16, "l3kf") for _ in range(DTI)]
        vtm = p.sb([128, 16 * 1024], BF16, "l3vtm")
        ps_p = p.psum([128, CH], F32, "l3psp")
        ps_s = [p.psum([128, CH], F32, "l3pss") for _ in range(3)]
        ps_pv = p.psum([64, CH], F32, "l3pspv")
        ps_den = p.psum([1, CH], F32, "l3psden")
        ps_db = p.psum([64, CH], F32, "l3psdb")
        pt_t = [p.sb([128, CH], BF16, "l3pt") for _ in range(2)]
        rd_t = p.sb([1, CH], F32, "l3rd")
        at_s = p.sb([64, CH], F32, "l3ats")
        bm8 = p.sb([128, 1], F32, "l3bm8")
        at_b = p.sb([64, CH], BF16, "l3atb")
        fac = p.sb([128, CH], F32, "l3fac")
        ot = p.sb([128, CH], F32, "l3ot")

        def mkpen(g):
            for m in range(4):
                g.memset(pen[m][:], -50.0)
                last = g.affine_select(out=pen[m][:], in_=pen[m][:],
                                       pattern=[[-1, CH]], base=128 * m,
                                       channel_multiplier=1,
                                       compare_op=ALU.is_gt, fill=0.0)
            g.memset(onesb[:], 1.0)
            g.memset(bm8[:], -8.0)
            return last
        p.stage(gpsimd=mkpen)

        for r in range(RPC):
            rsl = slice(r * SEQ, (r + 1) * SEQ)
            def ldx(s, rsl=rsl):
                for i in range(DTI):
                    p.dma(s, xnb[i][:], xnbf_hbm[i, :, rsl])
                return None
            p.stage(sync=ldx)
            for nm, dst, scl in (("Wq", qf, 0.125), ("Wk", kf, 0.125)):
                with p.scoped():
                    wt = p.sb([128, DTI * 1024], BF16, "l3w")
                    p.stage(gpsimd=lambda gp, nm=nm: (p.dma(
                        gp, wt[:].rearrange("p (a f) -> p a f", a=DTI),
                        W3[nm].rearrange("(a p) f -> p a f", p=128)), None)[1])
                    for fo in range(DTI):
                        for cc in range(4):
                            def mmq(t, fo=fo, cc=cc, r=r):
                                for i in range(DTI):
                                    last = t.matmul(
                                        ps_p[:],
                                        wt[:, (i * 8 + fo) * 128:
                                           (i * 8 + fo + 1) * 128],
                                        xnb[i][:, cc * CH:(cc + 1) * CH],
                                        start=(i == 0), stop=(i == DTI - 1))
                                return last
                            p.stage(tensor=mmq)
                            p.stage(scalar=lambda sc, dst=dst, fo=fo, cc=cc,
                                    scl=scl: sc.activation(
                                        dst[fo][:, cc * CH:(cc + 1) * CH],
                                        ps_p[:], AF.Copy, scale=scl))

            with p.scoped():
                wt = p.sb([128, DTI * 1024], BF16, "l3wv")
                p.stage(gpsimd=lambda gp: (p.dma(
                    gp, wt[:].rearrange("p (a f) -> p a f", a=DTI),
                    W3["Wv"].rearrange("(a p) f -> p a f", p=128)), None)[1])
                for t2 in range(16):
                    for dc in range(2):
                        def mmv(t, t2=t2, dc=dc, r=r):
                            for i in range(DTI):
                                last = t.matmul(
                                    ps_p[:],
                                    xnb[i][:, t2 * 128:(t2 + 1) * 128],
                                    wt[:, i * 1024 + dc * CH:
                                       i * 1024 + (dc + 1) * CH],
                                    start=(i == 0), stop=(i == DTI - 1))
                            return last
                        p.stage(tensor=mmv)
                        p.stage(scalar=lambda sc, t2=t2, dc=dc: sc.activation(
                            vtm[:, t2 * 1024 + dc * CH:t2 * 1024 + (dc + 1) * CH],
                            ps_p[:], AF.Copy))

            for h in range(16):
                hi, ho = h // 2, (h % 2) * 64
                for t1c in range(4):
                    n_t2 = 4 * t1c + 4
                    for u in range(n_t2 + 3):
                        fns = {}

                        def tfa(t, u=u, h=h, hi=hi, ho=ho, t1c=t1c, r=r,
                                n_t2=n_t2):
                            last = None
                            if u < n_t2:
                                last = t.matmul(
                                    ps_s[u % 3][:],
                                    kf[hi][ho:ho + 64, u * 128:(u + 1) * 128],
                                    qf[hi][ho:ho + 64, t1c * CH:(t1c + 1) * CH],
                                    start=True, stop=True)
                            if 0 <= u - 3 < n_t2:
                                t2 = u - 3
                                t.matmul(ps_pv[:],
                                         vtm[:, t2 * 1024 + h * 64:
                                             t2 * 1024 + h * 64 + 64],
                                         pt_t[t2 % 2][:], start=(t2 == 0),
                                         stop=(t2 == n_t2 - 1),
                                         skip_group_check=True)
                                last = t.matmul(ps_den[:], onesb[:],
                                                pt_t[t2 % 2][:],
                                                start=(t2 == 0),
                                                stop=(t2 == n_t2 - 1),
                                                skip_group_check=True)
                            return last
                        fns["tensor"] = tfa
                        if 0 <= u - 1 < n_t2 and (u - 1) - 4 * t1c >= 0:
                            m = (u - 1) - 4 * t1c
                            fns["vector"] = (lambda v, m=m, u=u: v.tensor_add(
                                ps_s[(u - 1) % 3][:], ps_s[(u - 1) % 3][:],
                                pen[m][:]))
                        if 0 <= u - 2 < n_t2:
                            fns["scalar"] = (lambda sc, u=u: sc.activation(
                                pt_t[(u - 2) % 2][:], ps_s[(u - 2) % 3][:],
                                AF.Exp, bias=bm8[:, 0:1]))
                        p.stage(**fns)

                    def nrm2(v):
                        return v.reciprocal(rd_t[:], ps_den[:])
                    p.stage(vector=nrm2,
                            scalar=lambda sc: sc.activation(at_s[:], ps_pv[:],
                                                            AF.Copy))
                    p.stage(tensor=lambda t: t.matmul(
                        ps_db[:], ones_row[0:1, 0:64], rd_t[:],
                        start=True, stop=True))
                    p.stage(vector=lambda v: v.tensor_mul(at_b[:], at_s[:],
                                                          ps_db[:]))

                    def stp(s, hi=hi, ho=ho, r=r, t1c=t1c):
                        p.dma(s, apre_hbm[hi, ho:ho + 64,
                                          r * SEQ + t1c * CH:
                                          r * SEQ + (t1c + 1) * CH], at_b[:])
                        return None
                    p.stage(sync=stp)

        with p.scoped():
            wt = p.sb([128, DTI * 1024], BF16, "l3wr")
            apc = [p.sb([128, CH], BF16, "l3apc") for _ in range(DTI)]
            p.stage(gpsimd=lambda gp: (p.dma(
                gp, wt[:].rearrange("p (a f) -> p a f", a=DTI),
                W3["Wr"].rearrange("(a p) f -> p a f", p=128)), None)[1])
            for c in range(NT):
                def lda(s, c=c):
                    for i in range(DTI):
                        p.dma(s, apc[i][:], apre_hbm[i, :, csl(c)])
                    return None
                p.stage(sync=lda)
                for fo in range(DTI):
                    def mmr2(t, fo=fo, c=c):
                        for i in range(DTI):
                            last = t.matmul(
                                ps_p[:],
                                wt[:, (i * 8 + fo) * 128:(i * 8 + fo + 1) * 128],
                                apc[i][:], start=(i == 0), stop=(i == DTI - 1))
                        return last
                    p.stage(tensor=mmr2)

                    def ldf(s, fo=fo, c=c):
                        p.dma(s, fac[:], ffnacc_hbm[fo, :, csl(c)])
                        return None
                    p.stage(sync=ldf)
                    p.stage(vector=lambda v: v.tensor_add(ot[:], ps_p[:], fac[:]))
                    p.stage(sync=lambda s, fo=fo, c=c: (p.dma(
                        s, out_hbm[fo, :, csl(c)], ot[:]), None)[1])



W_BLOCK3 = [
    ("l3a_Wq", [D, D]), ("l3a_Wk", [D, D]), ("l3a_Wv", [D, D]),
    ("l3a_Wr", [D, D]),
    ("l3f_rwT", [D, 32]), ("l3f_bias", [32]),
    ("l3f_gateT", [32, D, 512]), ("l3f_upT", [32, D, 512]),
    ("l3f_downT", [32, 512, D]),
    ("l3f_shgT", [D, 1024]), ("l3f_shuT", [D, 1024]), ("l3f_shdT", [1024, D]),
]


def emit_block3(p, C, win, x, x4):
    xn32 = p.dram("b3_xn32", [DTI, 128, T], F32)
    xnbf = p.dram("b3_xnbf", [DTI, 128, T], BF16)
    ffnacc = p.dram("b3_ffnacc", [DTI, 128, T], F32)
    apre = p.dram("b3_apre", [DTI, 128, T], BF16)

    emit_ln(p, C, x, xn32, xnbf)
    with p.scoped():
        combT = p.sb([32, T], F32, "combT")
        emit_router_grouped(p, C, xn32, win["l3f_rwT"], win["l3f_bias"], combT,
                            E=32, GRP=4, K=4, scale=2.5, mask_fill=0.0)
        experts = []
        for e in range(32):
            experts.append(dict(gate=win["l3f_gateT"][e], up=win["l3f_upT"][e],
                                down=win["l3f_downT"][e], e=e, FTI=4))
        experts.append(dict(gate=win["l3f_shgT"], up=win["l3f_shuT"],
                            down=win["l3f_shdT"], e=None, FTI=8))
        emit_moe_ffn(p, C, xnbf, x, ffnacc, combT, experts, tag="m3f")

    W3 = dict(Wq=win["l3a_Wq"], Wk=win["l3a_Wk"], Wv=win["l3a_Wv"],
              Wr=win["l3a_Wr"])
    emit_l3_attn(p, C, W3, xnbf, ffnacc, x4, apre)


def build_block3(debug=False):
    p = Prog()
    x = p.dram("x", [DTI, 128, T], F32, kind="ExternalInput")
    x4 = p.dram("x4", [DTI, 128, T], F32, kind="ExternalOutput")
    win = declare_w(p, W_BLOCK3)
    C = emit_consts(p)
    emit_block3(p, C, win, x, x4)
    p.final_wait()
    return p.finish()


def prep_block3_inputs(inputs):
    f32 = np.float32
    w = {}
    w["l3a_Wq"] = np.ascontiguousarray(inputs["L3_attn_Wq"], f32)
    w["l3a_Wk"] = np.ascontiguousarray(inputs["L3_attn_Wk"], f32)
    w["l3a_Wv"] = np.ascontiguousarray(inputs["L3_attn_Wv"], f32)
    w["l3a_Wr"] = np.ascontiguousarray(inputs["L3_attn_Wr"], f32)
    w["l3f_rwT"] = np.ascontiguousarray(inputs["L3_ffn_router_weight"].T, f32)
    w["l3f_bias"] = np.ascontiguousarray(inputs["L3_ffn_router_bias"], f32)
    w["l3f_gateT"] = np.ascontiguousarray(
        np.asarray(inputs["L3_ffn_gate_w"], f32).transpose(0, 2, 1))
    w["l3f_upT"] = np.ascontiguousarray(
        np.asarray(inputs["L3_ffn_up_w"], f32).transpose(0, 2, 1))
    w["l3f_downT"] = np.ascontiguousarray(
        np.asarray(inputs["L3_ffn_down_w"], f32).transpose(0, 2, 1))
    w["l3f_shgT"] = np.ascontiguousarray(inputs["L3_ffn_shared_gate_w"].T, f32)
    w["l3f_shuT"] = np.ascontiguousarray(inputs["L3_ffn_shared_up_w"].T, f32)
    w["l3f_shdT"] = np.ascontiguousarray(inputs["L3_ffn_shared_down_w"].T, f32)
    return w


def run_block3(inputs, x_full, debug=False):
    w = prep_block3_inputs(inputs)
    shards = _fm_shard(x_full)
    nc = build_block3(debug=debug)
    in_maps = [dict(w, x=shards[c]) for c in range(NCORE)]
    res = run_bass_kernel_spmd(nc, in_maps, core_ids=list(range(NCORE)))
    if debug:
        return res
    return _fm_unshard([res.results[c]["x4"] for c in range(NCORE)])



def emit_router_softmax_top2(p, C, xn32_hbm, rwT_hbm, mT, E=8):
    ones_row, identf = C["ones_row"], C["identf"]
    with p.scoped():
        rwt = p.sb([128, DTI * E], F32, "r1wt")
        xs = [p.sb([128, CH], F32, "r1xs") for _ in range(DTI)]
        lg = p.psum([128, CH], F32, "r1lg")
        lgs = p.sb([128, E], F32, "r1lgs")
        mx = p.sb([128, 1], F32, "r1mx")
        sm = p.sb([128, E], F32, "r1sm")
        ssum = p.sb([128, 1], F32, "r1ss")
        m1 = p.sb([128, 1], F32, "r1m1")
        sel1 = p.sb([128, E], F32, "r1s1")
        sel2 = p.sb([128, E], F32, "r1s2")
        msk = p.sb([128, E], F32, "r1msk")
        pack = p.sb([128, 18], F32, "r1pack")
        ct_ps = p.psum([18, CH], F32, "r1ct")

        p.stage(sync=lambda s: (p.dma(
            s, rwt[:].rearrange("p (a e) -> p a e", a=DTI),
            rwT_hbm.rearrange("(a p) e -> p a e", p=128)), None)[1])
        p.stage(vector=lambda v: v.memset(mT[:], 0.0))

        for c in range(NT):
            def ld(s, c=c):
                for i in range(DTI):
                    p.dma(s, xs[i][:], xn32_hbm[i, :, csl(c)])
                return None
            p.stage(sync=ld)

            for tt in range(4):
                def mm(t, c=c, tt=tt):
                    for i in range(DTI):
                        last = t.matmul(
                            lg[0:128, 0:E], xs[i][:, tt * 128:(tt + 1) * 128],
                            rwt[:, i * E:(i + 1) * E],
                            start=(i == 0), stop=(i == DTI - 1))
                    return last
                p.stage(tensor=mm)

                def r1(v):
                    def W(instr):
                        instr.then_inc(p.sem("vv"), 1)
                        p.cnt["vv"] += 1
                        v.wait_ge(p.sem("vv"), p.cnt["vv"])
                        return instr
                    W(v.tensor_copy(lgs[:], lg[0:128, 0:E]))
                    W(v.tensor_reduce(mx[:], lgs[:], AX.X, ALU.max))
                    return v.tensor_scalar(mx[:], mx[:], -1.0, None, ALU.mult)
                p.stage(vector=r1)
                p.stage(scalar=lambda sc: sc.activation(
                    sm[:], lgs[:], AF.Exp, bias=mx[:, 0:1],
                    accum_out=ssum[:]))

                def r2(v):
                    def W(instr):
                        instr.then_inc(p.sem("vv"), 1)
                        p.cnt["vv"] += 1
                        v.wait_ge(p.sem("vv"), p.cnt["vv"])
                        return instr
                    W(v.reciprocal(ssum[:], ssum[:]))
                    W(v.tensor_scalar(sm[:], sm[:], ssum[:, 0:1], None, ALU.mult))
                    W(v.tensor_reduce(m1[:], sm[:], AX.X, ALU.max))
                    W(v.tensor_scalar(sel1[:], sm[:], m1[:, 0:1], None, ALU.is_ge))
                    W(v.tensor_copy(pack[:, 16:17], m1[:]))
                    W(v.scalar_tensor_tensor(msk[:], sel1[:], -BIG, sm[:],
                                             ALU.mult, ALU.add))
                    W(v.tensor_reduce(m1[:], msk[:], AX.X, ALU.max))
                    W(v.tensor_scalar(sel2[:], msk[:], m1[:, 0:1], None, ALU.is_ge))
                    W(v.tensor_copy(pack[:, 17:18], m1[:]))
                    W(v.tensor_copy(pack[:, 0:E], sel1[:]))
                    return v.tensor_copy(pack[:, 8:8 + E], sel2[:])
                p.stage(vector=r2)

                p.stage(tensor=lambda t: t.transpose(ct_ps[0:18, 0:128],
                                                     pack[:, 0:18], identf[:]))
                p.stage(vector=lambda v, c=c, tt=tt: v.tensor_copy(
                    mT[0:18, (c * 4 + tt) * 128:(c * 4 + tt + 1) * 128],
                    ct_ps[0:18, 0:128]))



def emit_l1_ffn(p, C, xnbf_hbm, seed_hbm, out_hbm, mT, W1, tag="l1f"):
    sel32 = C["sel32"]
    NG = 8
    CPG = NT // NG
    E = 8
    with p.scoped():
        xnb = [[p.sb([128, CH], BF16, tag + "xnb") for _ in range(DTI)]
               for _ in range(CPG)]
        xsb = [[[p.sb([128, CH], BF16, tag + "xsb") for _ in range(DTI)]
                for _ in range(CPG)] for _ in range(2)]
        acc = [[p.sb([128, CH], F32, tag + "acc") for _ in range(DTI)]
               for _ in range(CPG)]
        w1t = [p.sb([128, DTI * 1024], BF16, tag + "w1") for _ in range(2)]
        w2t = [p.sb([128, DTI * 1024], BF16, tag + "w2") for _ in range(2)]
        w3t = [p.sb([128, DTI * 1024], BF16, tag + "w3") for _ in range(2)]
        bcols = [p.sb([128, 3 * DTI], F32, tag + "bc") for _ in range(2)]
        ps_h = [p.psum([128, CH], F32, tag + "psh") for _ in range(2)]
        ps_x3 = [p.psum([128, CH], F32, tag + "psx3") for _ in range(2)]
        ps_h2 = [p.psum([128, CH], F32, tag + "psh2") for _ in range(2)]
        ps_cb = p.psum([128, CH], F32, tag + "pscb")
        cb_s = p.sb([128, CH], F32, tag + "cbs")
        h_t = [p.sb([128, CH], BF16, tag + "h") for _ in range(DTI)]
        x3_t = [p.sb([128, CH], BF16, tag + "x3") for _ in range(DTI)]
        x2_t = [p.sb([128, CH], BF16, tag + "x2") for _ in range(2)]
        tmp_t = p.sb([128, CH], F32, tag + "tmp")

        def ldw(gp, e, par):
            p.dma_pf(gp, w1t[par][:].rearrange("p (a f) -> p a f", a=DTI),
                     W1["w1T"][e].rearrange("(a p) f -> p a f", p=128))
            p.dma_pf(gp, w2t[par][:].rearrange("p (a f) -> p a f", a=DTI),
                     W1["w2T"][e].rearrange("(a p) f -> p a f", p=128))
            p.dma_pf(gp, w3t[par][:].rearrange("p (a f) -> p a f", a=DTI),
                     W1["w3T"][e].rearrange("(a p) f -> p a f", p=128))
            p.dma_pf(gp, bcols[par][:, 0:DTI], W1["b1c"][e])
            p.dma_pf(gp, bcols[par][:, DTI:2 * DTI], W1["b2c"][e])
            p.dma_pf(gp, bcols[par][:, 2 * DTI:3 * DTI], W1["b3c"][e])
            return None

        for g in range(NG):
            chunks = list(range(g * CPG, (g + 1) * CPG))
            pf_after = {}

            def seed(s, g=g, chunks=chunks):
                for ci, c in enumerate(chunks):
                    for i in range(DTI):
                        p.dma(s, acc[ci][i][:], seed_hbm[i, :, csl(c)])
                        p.dma(s, xnb[ci][i][:], xnbf_hbm[i, :, csl(c)])
                return None
            p.stage(sync=seed, gpsimd=lambda gp: ldw(gp, 0, 0))
            pf_after[0] = p.cnt["pf"]

            for k in range(2):
                for ci, c in enumerate(chunks):
                    def bc(t, k=k, c=c):
                        return t.matmul(ps_cb[:],
                                        sel32[:, (16 + k) * 128:(17 + k) * 128],
                                        mT[0:32, csl(c)], start=True, stop=True)
                    p.stage(tensor=bc)
                    for i in range(DTI):
                        p.stage(vector=lambda v, k=k, ci=ci, i=i: v.tensor_mul(
                            xsb[k][ci][i][:], xnb[ci][i][:], ps_cb[:]))

            for e in range(E):
                par = e % 2
                if e + 1 < E:
                    p.stage(gpsimd=lambda gp, e=e: ldw(gp, e + 1, (e + 1) % 2))
                    pf_after[e + 1] = p.cnt["pf"]
                pf_tgt = pf_after[e]

                for k in range(2):
                    for ci, c in enumerate(chunks):
                        def bcm(t, e=e, k=k, c=c, pf_tgt=pf_tgt, first=(k == 0 and ci == 0)):
                            if first:
                                t.wait_ge(p.sem("pf"), pf_tgt)
                            return t.matmul(
                                ps_cb[:], sel32[:, (k * 8 + e) * 128:
                                                (k * 8 + e + 1) * 128],
                                mT[0:32, csl(c)], start=True, stop=True)
                        p.stage(tensor=bcm)
                        p.stage(vector=lambda v: v.tensor_copy(cb_s[:],
                                                               ps_cb[:]))

                        for u in range(DTI + 2):
                            fns = {}
                            if u < DTI:
                                def tf(t, par=par, k=k, ci=ci, fo=u, u=u):
                                    for i in range(DTI):
                                        t.matmul(
                                            ps_h[u % 2][:],
                                            w1t[par][:, (i * 8 + fo) * 128:
                                                     (i * 8 + fo + 1) * 128],
                                            xsb[k][ci][i][:], start=(i == 0),
                                            stop=(i == DTI - 1))
                                    for i in range(DTI):
                                        last = t.matmul(
                                            ps_x3[u % 2][:],
                                            w3t[par][:, (i * 8 + fo) * 128:
                                                     (i * 8 + fo + 1) * 128],
                                            xsb[k][ci][i][:], start=(i == 0),
                                            stop=(i == DTI - 1))
                                    return last
                                fns["tensor"] = tf
                            if 0 <= u - 1 < DTI:
                                us = u - 1
                                fns["scalar"] = (
                                    lambda sc, us=us, par=par: sc.activation(
                                        h_t[us][:], ps_h[us % 2][:], AF.Silu,
                                        bias=bcols[par][:, us:us + 1]))
                            if 0 <= u - 2 < DTI:
                                uv = u - 2
                                fns["vector"] = (
                                    lambda v, uv=uv, par=par: v.tensor_scalar(
                                        x3_t[uv][:], ps_x3[uv % 2][:],
                                        bcols[par][:, 2 * DTI + uv:
                                                    2 * DTI + uv + 1],
                                        None, ALU.add))
                            p.stage(**fns)

                        for w in range(DTI + 1):
                            fns = {}
                            if w < DTI:
                                def tf2(t, par=par, fo=w, w=w):
                                    for fi in range(DTI):
                                        last = t.matmul(
                                            ps_h2[w % 2][:],
                                            w2t[par][:, (fi * 8 + fo) * 128:
                                                     (fi * 8 + fo + 1) * 128],
                                            h_t[fi][:], start=(fi == 0),
                                            stop=(fi == DTI - 1))
                                    return last
                                fns["tensor"] = tf2
                            if 0 <= w - 1 < DTI:
                                wv = w - 1

                                def vf2(v, wv=wv, par=par, ci=ci):
                                    def W(instr):
                                        instr.then_inc(p.sem("vv"), 1)
                                        p.cnt["vv"] += 1
                                        v.wait_ge(p.sem("vv"), p.cnt["vv"])
                                        return instr
                                    W(v.tensor_scalar(
                                        x2_t[wv % 2][:], ps_h2[wv % 2][:],
                                        bcols[par][:, DTI + wv:DTI + wv + 1],
                                        None, ALU.add))
                                    v.tensor_mul(tmp_t[:], x2_t[wv % 2][:],
                                                 x3_t[wv][:])
                                    v.tensor_mul(tmp_t[:], tmp_t[:], cb_s[:])
                                    return v.tensor_add(acc[ci][wv][:],
                                                        acc[ci][wv][:], tmp_t[:])
                                fns["vector"] = vf2
                            p.stage(**fns)

            def store(s, g=g, chunks=chunks):
                for ci, c in enumerate(chunks):
                    for i in range(DTI):
                        p.dma(s, out_hbm[i, :, csl(c)], acc[ci][i][:])
                return None
            p.stage(sync=store)



def emit_l1_attn(p, C, W1a, xnabf_hbm, att1_hbm):
    ones_row, identf, ident = C["ones_row"], C["identf"], C["ident"]
    with p.scoped():
        beta_r = p.sb([1, 16], F32, "l1beta")
        ps_bb = p.psum([128, CH], F32, "l1psbb")
        ebb = p.sb([128, 16], F32, "l1ebb")
        qsc = p.sb([128, DTI], F32, "l1qsc")
        qbe = p.sb([128, DTI], F32, "l1qbe")
        kbe = p.sb([128, DTI], F32, "l1kbe")
        bqc = p.sb([128, DTI], F32, "l1bqc")
        bkc = p.sb([128, DTI], F32, "l1bkc")
        bvc = p.sb([128, DTI], F32, "l1bvc")
        boc = p.sb([128, DTI], F32, "l1boc")
        mask128 = p.sb([128, 128], F32, "l1mask")
        onesb = p.sb([128, 1], BF16, "l1ones")

        def ldb(s):
            p.dma(s, beta_r[:], W1a["beta"][None, :])
            p.dma(s, bqc[:], W1a["bqc"])
            p.dma(s, bkc[:], W1a["bkc"])
            p.dma(s, bvc[:], W1a["bvc"])
            p.dma(s, boc[:], W1a["boc"])
            return None
        p.stage(sync=ldb)
        p.stage(scalar=lambda sc: sc.activation(beta_r[:], beta_r[:], AF.Exp))
        p.stage(tensor=lambda t: t.matmul(ps_bb[:, 0:16], ones_row[0:1, :],
                                          beta_r[:], start=True, stop=True))
        p.stage(vector=lambda v: v.tensor_copy(ebb[:], ps_bb[:, 0:16]))

        def mkscale(v):
            def W(instr):
                instr.then_inc(p.sem("vv"), 1)
                p.cnt["vv"] += 1
                v.wait_ge(p.sem("vv"), p.cnt["vv"])
                return instr
            for ft in range(DTI):
                W(v.tensor_copy(qsc[0:64, ft:ft + 1], ebb[0:64, 2 * ft:2 * ft + 1]))
                W(v.tensor_copy(qsc[64:128, ft:ft + 1],
                                ebb[64:128, 2 * ft + 1:2 * ft + 2]))
            W(v.tensor_scalar(qsc[:], qsc[:], 8.0, None, ALU.mult))
            W(v.reciprocal(qsc[:], qsc[:]))
            W(v.tensor_mul(qbe[:], bqc[:], qsc[:]))
            return v.tensor_mul(kbe[:], bkc[:], qsc[:])
        p.stage(vector=mkscale)

        def mkmask(g):
            g.memset(mask128[:], 1.0)
            g.affine_select(out=mask128[:].rearrange("p (a b) -> p a b", a=8),
                            in_=mask128[:].rearrange("p (a b) -> p a b", a=8),
                            pattern=[[-16, 8], [0, 16]], base=0,
                            channel_multiplier=1, compare_op=ALU.is_ge, fill=0.0)
            g.affine_select(out=mask128[:].rearrange("p (a b) -> p a b", a=8),
                            in_=mask128[:].rearrange("p (a b) -> p a b", a=8),
                            pattern=[[16, 8], [0, 16]], base=15,
                            channel_multiplier=-1, compare_op=ALU.is_ge, fill=0.0)
            g.affine_select(out=mask128[:].rearrange("p (a b) -> p a b", a=8),
                            in_=mask128[:].rearrange("p (a b) -> p a b", a=8),
                            pattern=[[16, 8], [1, 16]], base=0,
                            channel_multiplier=-1, compare_op=ALU.is_ge, fill=0.0)
            g.memset(onesb[:], 1.0)
            return g.memset(mask128[0:1, 0:1], 1.0)
        p.stage(gpsimd=mkmask)

        ps_p = p.psum([128, CH], F32, "l1psp")
        ps_s = [p.psum([128, 128], F32, "l1pss") for _ in range(2)]
        ps_pv = [p.psum([128, 128], F32, "l1pspv") for _ in range(2)]
        ps_t2 = [p.psum([128, 128], BF16, "l1pst")]
        ta = p.sb([128, CH], F32, "l1ta")
        tb = p.sb([128, CH], F32, "l1tb")
        sm_t = [p.sb([128, 128], BF16, "l1sm") for _ in range(2)]
        rden = p.sb([128, 1], F32, "l1rden")
        atm = p.sb([128, 16 * 64], BF16, "l1atm")
        fac = p.sb([128, CH], F32, "l1fac")

        PS = 1024
        NTG = PS // 128
        HT = T // PS
        for hh in range(HT):
            hsl = slice(hh * PS, (hh + 1) * PS)
            with p.scoped():
                xnb = [p.sb([128, PS], BF16, "l1xnb") for _ in range(DTI)]
                qf = [p.sb([128, PS], BF16, "l1qf") for _ in range(DTI)]
                kf = [p.sb([128, PS], BF16, "l1kf") for _ in range(DTI)]
                vtm = p.sb([128, NTG * 1024], BF16, "l1vtm")
                apre = [p.sb([128, PS], BF16, "l1apre") for _ in range(DTI)]

                def ldx(s, hsl=hsl):
                    for i in range(DTI):
                        p.dma(s, xnb[i][:], xnabf_hbm[i, :, hsl])
                    return None
                p.stage(sync=ldx)

                for nm, dst, be in (("WqT", qf, qbe), ("WkT", kf, kbe)):
                    with p.scoped():
                        wt = p.sb([128, DTI * 1024], BF16, "l1w")
                        p.stage(gpsimd=lambda gp, nm=nm: (p.dma(
                            gp, wt[:].rearrange("p (a f) -> p a f", a=DTI),
                            W1a[nm].rearrange("(a p) f -> p a f", p=128)),
                            None)[1])
                        for fo in range(DTI):
                            for cc in range(PS // CH):
                                def mmq(t, fo=fo, cc=cc):
                                    for i in range(DTI):
                                        last = t.matmul(
                                            ps_p[:],
                                            wt[:, (i * 8 + fo) * 128:
                                               (i * 8 + fo + 1) * 128],
                                            xnb[i][:, cc * CH:(cc + 1) * CH],
                                            start=(i == 0), stop=(i == DTI - 1))
                                    return last
                                p.stage(tensor=mmq)

                                def sp1(sc, fo=fo, be=be):
                                    sc.activation(ta[:], ps_p[:], AF.Abs,
                                                  bias=be[:, fo:fo + 1],
                                                  scale=qsc[:, fo:fo + 1])
                                    sc.activation(tb[:], ta[:], AF.Exp,
                                                  scale=-1.0)
                                    sc.activation(ta[:], tb[:], AF.Ln, bias=1.0)
                                    return sc.activation(
                                        tb[:], ps_p[:], AF.Relu,
                                        bias=be[:, fo:fo + 1],
                                        scale=qsc[:, fo:fo + 1])
                                p.stage(scalar=sp1)
                                p.stage(vector=lambda v, dst=dst, fo=fo, cc=cc:
                                        v.tensor_add(
                                            dst[fo][:, cc * CH:(cc + 1) * CH],
                                            ta[:], tb[:]))

                with p.scoped():
                    wt = p.sb([128, DTI * 1024], BF16, "l1wv")
                    bvrow = p.sb([1, 1024], F32, "l1bvr")
                    bv_s = p.sb([128, CH], F32, "l1bvs")
                    vv_t = p.sb([128, CH], F32, "l1vvt")
                    ps_bv = p.psum([128, CH], F32, "l1psbv")
                    p.stage(gpsimd=lambda gp: (p.dma(
                        gp, wt[:].rearrange("p (a f) -> p a f", a=DTI),
                        W1a["WvT"].rearrange("(a p) f -> p a f", p=128)), None)[1])
                    p.stage(sync=lambda s: (p.dma(
                        s, bvrow[:], W1a["bv"][None, :]), None)[1])

                    for dc in range(2):
                        def bvb(t, dc=dc):
                            return t.matmul(ps_bv[:], ones_row[0:1, :],
                                            bvrow[:, dc * CH:(dc + 1) * CH],
                                            start=True, stop=True)
                        p.stage(tensor=bvb)
                        p.stage(vector=lambda v: v.tensor_copy(bv_s[:],
                                                               ps_bv[:]))
                        for tg in range(NTG):
                            def mmv(t, tg=tg, dc=dc):
                                for i in range(DTI):
                                    last = t.matmul(
                                        ps_p[:],
                                        xnb[i][:, tg * 128:(tg + 1) * 128],
                                        wt[:, i * 1024 + dc * CH:
                                           i * 1024 + (dc + 1) * CH],
                                        start=(i == 0), stop=(i == DTI - 1))
                                return last
                            p.stage(tensor=mmv)
                            p.stage(vector=lambda v: v.tensor_add(
                                vv_t[:], ps_p[:], bv_s[:]))
                            p.stage(scalar=lambda sc, tg=tg, dc=dc: sc.activation(
                                vtm[:, tg * 1024 + dc * CH:
                                    tg * 1024 + (dc + 1) * CH],
                                vv_t[:], AF.Copy))

                units_a = [(tg, h) for tg in range(NTG) for h in range(16)]
                NA = len(units_a)
                for w in range(NA + 4):
                    fns = {}

                    def tfb(t, w=w):
                        last = None
                        if w < NA:
                            tg, h = units_a[w]
                            hi, ho = h // 2, (h % 2) * 64
                            gsl = slice(tg * 128, (tg + 1) * 128)
                            last = t.matmul(ps_s[w % 2][:],
                                            kf[hi][ho:ho + 64, gsl],
                                            qf[hi][ho:ho + 64, gsl],
                                            start=True, stop=True)
                        if 0 <= w - 2 < NA:
                            tg, h = units_a[w - 2]
                            t.matmul(ps_pv[(w - 2) % 2][:, 0:64],
                                     sm_t[(w - 2) % 2][:],
                                     vtm[:, tg * 1024 + h * 64:
                                         tg * 1024 + h * 64 + 64],
                                     start=True, stop=True)
                            last = t.matmul(ps_pv[(w - 2) % 2][:, 64:65],
                                            sm_t[(w - 2) % 2][:], onesb[:],
                                            start=True, stop=True)
                        return last
                    fns["tensor"] = tfb

                    def vfb(v, w=w):
                        last = None
                        if 0 <= w - 1 < NA:
                            last = v.tensor_mul(sm_t[(w - 1) % 2][:],
                                                ps_s[(w - 1) % 2][:],
                                                mask128[:])
                        if 0 <= w - 3 < NA:
                            tg, h = units_a[w - 3]

                            def W(instr):
                                instr.then_inc(p.sem("vv"), 1)
                                p.cnt["vv"] += 1
                                v.wait_ge(p.sem("vv"), p.cnt["vv"])
                                return instr
                            W(v.reciprocal(rden[:],
                                           ps_pv[(w - 3) % 2][:, 64:65]))
                            last = v.tensor_scalar(
                                atm[:, h * 64:(h + 1) * 64],
                                ps_pv[(w - 3) % 2][:, 0:64],
                                rden[:, 0:1], None, ALU.mult)
                        return last
                    if 0 <= w - 1 < NA or 0 <= w - 3 < NA:
                        fns["vector"] = vfb
                    p.stage(**fns)

                    if 0 <= w - 3 < NA and units_a[w - 3][1] == 15:
                        tg_d = units_a[w - 3][0]
                        for hp2 in range(DTI):
                            p.stage(tensor=lambda t, hp=hp2: t.transpose(
                                ps_t2[0][:],
                                atm[:, hp * 128:(hp + 1) * 128], ident[:]))
                            p.stage(vector=lambda v, hp=hp2, tg=tg_d:
                                    v.tensor_copy(
                                        apre[hp][:, tg * 128:(tg + 1) * 128],
                                        ps_t2[0][:]))

                with p.scoped():
                    wt = p.sb([128, DTI * 1024], BF16, "l1wo")
                    p.stage(gpsimd=lambda gp: (p.dma(
                        gp, wt[:].rearrange("p (a f) -> p a f", a=DTI),
                        W1a["WoT"].rearrange("(a p) f -> p a f", p=128)), None)[1])
                    for fo in range(DTI):
                        for cc in range(PS // CH):
                            def mmo(t, fo=fo, cc=cc):
                                for i in range(DTI):
                                    last = t.matmul(
                                        ps_p[:],
                                        wt[:, (i * 8 + fo) * 128:
                                           (i * 8 + fo + 1) * 128],
                                        apre[i][:, cc * CH:(cc + 1) * CH],
                                        start=(i == 0), stop=(i == DTI - 1))
                                return last
                            p.stage(tensor=mmo)
                            p.stage(vector=lambda v, fo=fo: v.tensor_scalar(
                                fac[:], ps_p[:], boc[:, fo:fo + 1], None,
                                ALU.add))

                            def sto(s, fo=fo, cc=cc, hh=hh):
                                p.dma(s, att1_hbm[fo, :,
                                                  hh * PS + cc * CH:
                                                  hh * PS + (cc + 1) * CH],
                                      fac[:])
                                return None
                            p.stage(sync=sto)



W_BLOCK1F = [
    ("l1f_rwT", [D, 8]),
    ("l1f_w1T", [8, D, D]), ("l1f_w2T", [8, D, D]), ("l1f_w3T", [8, D, D]),
    ("l1f_b1c", [8, 128, DTI]), ("l1f_b2c", [8, 128, DTI]),
    ("l1f_b3c", [8, 128, DTI]),
    ("l1f_sw1T", [D, 2048]), ("l1f_sw2T", [D, 2048]), ("l1f_sw3T", [2048, D]),
]
W_BLOCK1A = [
    ("l1a_beta", [16]),
    ("l1a_WqT", [D, D]), ("l1a_WkT", [D, D]), ("l1a_WvT", [D, D]),
    ("l1a_WoT", [D, D]),
    ("l1a_bqc", [128, DTI]), ("l1a_bkc", [128, DTI]), ("l1a_bvc", [128, DTI]),
    ("l1a_bv", [D]), ("l1a_boc", [128, DTI]),
]


def emit_block1_ffn(p, C, win, x, y1p):
    xn32 = p.dram("b1_xn32", [DTI, 128, T], F32)
    xnbf = p.dram("b1_xnbf", [DTI, 128, T], BF16)
    shacc = p.dram("b1_shacc", [DTI, 128, T], F32)

    emit_ln(p, C, x, xn32, xnbf)
    with p.scoped():
        mT = p.sb([32, T], F32, "mT")
        emit_router_softmax_top2(p, C, xn32, win["l1f_rwT"], mT, E=8)
        experts = []
        for half in range(2):
            fs = slice(half * 1024, (half + 1) * 1024)
            experts.append(dict(gate=win["l1f_sw1T"][:, fs],
                                up=win["l1f_sw2T"][:, fs],
                                down=win["l1f_sw3T"][fs, :], e=None, FTI=8))
        emit_moe_ffn(p, C, xnbf, x, shacc, None, experts, tag="l1sh")
        W1 = dict(w1T=win["l1f_w1T"], w2T=win["l1f_w2T"], w3T=win["l1f_w3T"],
                  b1c=win["l1f_b1c"], b2c=win["l1f_b2c"], b3c=win["l1f_b3c"])
        emit_l1_ffn(p, C, xnbf, shacc, y1p, mT, W1)


def emit_block1_attn(p, C, win, xa, att1):
    xnabf = p.dram("b1_xnabf", [DTI, 128, T], BF16)
    emit_ln(p, C, xa, None, xnabf)
    W1a = dict(beta=win["l1a_beta"], WqT=win["l1a_WqT"], WkT=win["l1a_WkT"],
               WvT=win["l1a_WvT"], WoT=win["l1a_WoT"], bqc=win["l1a_bqc"],
               bkc=win["l1a_bkc"], bvc=win["l1a_bvc"], bv=win["l1a_bv"],
               boc=win["l1a_boc"])
    emit_l1_attn(p, C, W1a, xnabf, att1)


def build_blockA(debug=False):
    p = Prog()
    x = p.dram("x", [DTI, 128, T], F32, kind="ExternalInput")
    x1 = p.dram("x1", [DTI, 128, T], F32, kind="ExternalOutput")
    y1p = p.dram("y1p", [DTI, 128, T], F32, kind="ExternalOutput")
    win = declare_w(p, W_BLOCK0 + W_BLOCK1F)
    C = emit_consts(p)
    emit_block0(p, C, win, x, x1, debug=debug)
    emit_block1_ffn(p, C, win, x1, y1p)
    p.final_wait()
    return p.finish()


def build_blockB(debug=False):
    p = Prog()
    xa = p.dram("xa", [DTI, 128, T], F32, kind="ExternalInput")
    att1 = p.dram("att1", [DTI, 128, T], F32, kind="ExternalOutput")
    win = declare_w(p, W_BLOCK1A)
    C = emit_consts(p)
    emit_block1_attn(p, C, win, xa, att1)
    p.final_wait()
    return p.finish()


def build_blockC(debug=False):
    p = Prog()
    y1p = p.dram("y1p", [DTI, 128, T], F32, kind="ExternalInput")
    att = p.dram("att", [DTI, 128, T], F32, kind="ExternalInput")
    x2 = p.dram("x2", [DTI, 128, T], F32)
    x3 = p.dram("x3", [DTI, 128, T], F32)
    x4 = p.dram("x4", [DTI, 128, T], F32, kind="ExternalOutput")
    win = declare_w(p, W_BLOCK2 + W_BLOCK3)
    C = emit_consts(p)
    emit_add(p, y1p, att, x2)
    emit_block2(p, C, win, x2, x3)
    emit_block3(p, C, win, x3, x4)
    p.final_wait()
    return p.finish()


def prep_block1_inputs(inputs):
    f32 = np.float32

    def col128(v):
        return np.ascontiguousarray(np.asarray(v, f32).reshape(-1, 128).T)

    w = {}
    w["l1a_beta"] = np.ascontiguousarray(inputs["L1_attn_beta"], f32)
    w["l1a_WqT"] = np.ascontiguousarray(inputs["L1_attn_Wq"].T, f32)
    w["l1a_WkT"] = np.ascontiguousarray(inputs["L1_attn_Wk"].T, f32)
    w["l1a_WvT"] = np.ascontiguousarray(inputs["L1_attn_Wv"].T, f32)
    w["l1a_WoT"] = np.ascontiguousarray(inputs["L1_attn_Wo"].T, f32)
    w["l1a_bqc"] = col128(inputs["L1_attn_bq"])
    w["l1a_bkc"] = col128(inputs["L1_attn_bk"])
    w["l1a_bvc"] = col128(inputs["L1_attn_bv"])
    w["l1a_bv"] = np.ascontiguousarray(inputs["L1_attn_bv"], f32)
    w["l1a_boc"] = col128(inputs["L1_attn_bo"])
    w["l1f_rwT"] = np.ascontiguousarray(inputs["L1_ffn_gate_w"].T, f32)
    w["l1f_w1T"] = np.ascontiguousarray(
        np.asarray(inputs["L1_ffn_w1"], f32).transpose(0, 2, 1))
    w["l1f_w2T"] = np.ascontiguousarray(
        np.asarray(inputs["L1_ffn_w2"], f32).transpose(0, 2, 1))
    w["l1f_w3T"] = np.ascontiguousarray(
        np.asarray(inputs["L1_ffn_w3"], f32).transpose(0, 2, 1))
    w["l1f_b1c"] = np.ascontiguousarray(
        np.asarray(inputs["L1_ffn_b1"], f32).reshape(8, 8, 128).transpose(0, 2, 1))
    w["l1f_b2c"] = np.ascontiguousarray(
        np.asarray(inputs["L1_ffn_b2"], f32).reshape(8, 8, 128).transpose(0, 2, 1))
    w["l1f_b3c"] = np.ascontiguousarray(
        np.asarray(inputs["L1_ffn_b3"], f32).reshape(8, 8, 128).transpose(0, 2, 1))
    w["l1f_sw1T"] = np.ascontiguousarray(inputs["L1_ffn_sw1"].T, f32)
    w["l1f_sw2T"] = np.ascontiguousarray(inputs["L1_ffn_sw2"].T, f32)
    w["l1f_sw3T"] = np.ascontiguousarray(inputs["L1_ffn_sw3"].T, f32)
    return w


def _fm_shard_tau(x_full):
    out = []
    for c in range(NCORE):
        xc = x_full[:, 256 * c:256 * (c + 1), :]
        fm = np.ascontiguousarray(
            xc.transpose(2, 1, 0).reshape(DTI, 128, T))
        out.append(fm)
    return out


def _fm_unshard_tau(shards):
    x = np.empty((16, SEQ, D), np.float32)
    for c in range(NCORE):
        fm = shards[c].reshape(D, 256, 16)
        x[:, 256 * c:256 * (c + 1), :] = fm.transpose(2, 1, 0)
    return x


def _build_b1_test():
    p = Prog()
    x = p.dram("x", [DTI, 128, T], F32, kind="ExternalInput")
    xa = p.dram("xa", [DTI, 128, T], F32, kind="ExternalInput")
    y1p = p.dram("y1p", [DTI, 128, T], F32, kind="ExternalOutput")
    att1 = p.dram("att1", [DTI, 128, T], F32, kind="ExternalOutput")
    win = declare_w(p, W_BLOCK1F + W_BLOCK1A)
    C = emit_consts(p)
    emit_block1_ffn(p, C, win, x, y1p)
    emit_block1_attn(p, C, win, xa, att1)
    p.final_wait()
    return p.finish()


def run_block1(inputs, x_full, debug=False):
    w = prep_block1_inputs(inputs)
    shards = _fm_shard(x_full)
    ashards = _fm_shard_tau(x_full)
    nc = _build_b1_test()
    in_maps = [dict(w, x=shards[c], xa=ashards[c]) for c in range(NCORE)]
    res = run_bass_kernel_spmd(nc, in_maps, core_ids=list(range(NCORE)))
    if debug:
        return res
    y1p = _fm_unshard([res.results[c]["y1p"] for c in range(NCORE)])
    att = _fm_unshard_tau([res.results[c]["att1"] for c in range(NCORE)])
    return y1p, att


# version 08 (final)
# speedup vs baseline: 1.8338x; 1.0166x over previous
import contextlib
from collections import defaultdict

import numpy as np

import concourse.bass as bass
import concourse.mybir as mybir
from concourse.bass_utils import run_bass_kernel_spmd

F32 = mybir.dt.float32
BF16 = mybir.dt.bfloat16
AF = mybir.ActivationFunctionType
ALU = mybir.AluOpType
AX = mybir.AxisListType

NCORE = 8
RPC = 2
T = 4096
SEQ = 2048
NT = 8
CH = 512
D = 1024
DTI = 8
BIG = 1e30


class Prog:
    def __init__(self):
        self.nc = bass.Bass("TRN2", target_bir_lowering=False)
        self.es = contextlib.ExitStack()
        self.root_es = self.es
        self.block = self.es.enter_context(self.nc.Block())
        self.sems = {}
        self.cnt = defaultdict(int)
        self._names = defaultdict(int)
        self.sem("q")

    def sem(self, name):
        if name not in self.sems:
            self.sems[name] = self.root_es.enter_context(self.nc.semaphore(name))
        return self.sems[name]

    def sb(self, shape, dtype=F32, tag="sb"):
        self._names[tag] += 1
        return self.es.enter_context(
            self.nc.sbuf_tensor(f"{tag}_{self._names[tag]}", shape, dtype))

    def psum(self, shape, dtype=F32, tag="ps"):
        self._names[tag] += 1
        return self.es.enter_context(
            self.nc.psum_tensor(f"{tag}_{self._names[tag]}", shape, dtype))

    def dram(self, name, shape, dtype=F32, kind="Internal"):
        return self.nc.dram_tensor(name, shape, dtype, kind=kind).ap()

    @contextlib.contextmanager
    def scoped(self):
        es = contextlib.ExitStack()
        old = self.es
        self.es = es
        try:
            yield
        finally:
            self.es = old
            es.close()

    def dma(self, e, out, in_, **kw):
        instr = e.dma_start(out=out, in_=in_, **kw)
        instr.then_inc(self.sem("q"), 16)
        self.cnt["q"] += 16
        return instr

    def dma_pf(self, e, out, in_, **kw):
        instr = e.dma_start(out=out, in_=in_, **kw)
        instr.then_inc(self.sem("pf"), 16)
        self.cnt["pf"] += 16
        return instr

    def stage(self, **fns):
        b = self.block
        start = self.cnt["q"]
        decs = dict(sync=b.sync, tensor=b.tensor, vector=b.vector,
                    scalar=b.scalar, gpsimd=b.gpsimd)
        for name, fn in fns.items():
            dec = decs[name]

            @dec
            def _(e, fn=fn, start=start):
                e.wait_ge(self.sem("q"), start)
                last = fn(e)
                if last is not None:
                    last.then_inc(self.sem("q"), 1)
                    self.cnt["q"] += 1

    def final_wait(self):
        b = self.block
        tgt = self.cnt["q"]

        @b.sync
        def _(e):
            e.wait_ge(self.sem("q"), tgt)

    def finish(self):
        self.es.close()
        return self.nc


def csl(c):
    return slice(c * CH, (c + 1) * CH)



def emit_consts(p):
    C = {}
    C["ones_row"] = p.sb([1, 128], F32, "onesr")
    C["ones_col"] = p.sb([128, 1], F32, "onesc")
    C["ident"] = p.sb([128, 128], BF16, "idb")
    C["identf"] = p.sb([128, 128], F32, "idf")
    C["sel32"] = p.sb([32, 32 * 128], F32, "sel32")

    def f(g):
        g.memset(C["ones_row"][:], 1.0)
        g.memset(C["ones_col"][:], 1.0)
        g.memset(C["ident"][:], 0.0)
        g.affine_select(out=C["ident"][:], in_=C["ident"][:], pattern=[[-1, 128]],
                        base=0, channel_multiplier=1, compare_op=ALU.not_equal,
                        fill=1.0)
        g.memset(C["identf"][:], 0.0)
        g.affine_select(out=C["identf"][:], in_=C["identf"][:], pattern=[[-1, 128]],
                        base=0, channel_multiplier=1, compare_op=ALU.not_equal,
                        fill=1.0)
        g.memset(C["sel32"][:], 0.0)
        return g.affine_select(
            out=C["sel32"][:].rearrange("p (e c) -> p e c", e=32),
            in_=C["sel32"][:].rearrange("p (e c) -> p e c", e=32),
            pattern=[[-1, 32], [0, 128]], base=0, channel_multiplier=1,
            compare_op=ALU.not_equal, fill=1.0)

    p.stage(gpsimd=f)
    return C



def emit_ln(p, C, x_hbm, xn32_hbm, xnbf_hbm, hbf_hbm=None, normw_hbm=None,
            x2_hbm=None):
    with p.scoped():
        ones_row = C["ones_row"]
        ones_col = C["ones_col"]
        xt = [p.sb([128, CH], F32, "lnx") for _ in range(DTI)]
        x2t = p.sb([128, CH], F32, "lnx2") if x2_hbm is not None else None
        sqt = [p.sb([128, CH], F32, "lnsq") for _ in range(DTI)]
        s1 = p.psum([1, CH], F32, "lns1")
        s2 = p.psum([1, CH], F32, "lns2")
        rows = p.sb([1, 8 * CH], F32, "lnrow")
        meanR = p.sb([1, T], F32, "lnmean")
        rstdR = p.sb([1, T], F32, "lnrstd")
        rmsR = p.sb([1, T], F32, "lnrms") if hbf_hbm is not None else None
        bc = [p.psum([128, CH], F32, "lnbc") for _ in range(3)]
        t1 = p.sb([128, CH], F32, "lnt1")
        xnt = [p.sb([128, CH], F32, "lnxnt") for _ in range(3)]
        xnb = [p.sb([128, CH], BF16, "lnxnb") for _ in range(2)]
        hbt = [p.sb([128, CH], BF16, "lnhbt") for _ in range(3)]
        nwc = p.sb([128, DTI], F32, "lnnw") if hbf_hbm is not None else None

        def S(i):
            return rows[:, i * CH:(i + 1) * CH]

        if nwc is not None:
            p.stage(sync=lambda s: (p.dma(s, nwc[:], normw_hbm), None)[1])

        def load_x(s, c):
            for i in range(DTI):
                p.dma(s, xt[i][:], x_hbm[i, :, csl(c)])
                if x2_hbm is not None:
                    p.dma(s, x2t[:], x2_hbm[i, :, csl(c)])
            return None

        for c in range(NT):
            if x2_hbm is None:
                p.stage(sync=lambda s, c=c: load_x(s, c))
            else:
                for i in range(DTI):
                    def ld1(s, c=c, i=i):
                        p.dma(s, xt[i][:], x_hbm[i, :, csl(c)])
                        p.dma(s, x2t[:], x2_hbm[i, :, csl(c)])
                    p.stage(sync=ld1)
                    p.stage(vector=lambda v, i=i: v.tensor_add(
                        xt[i][:], xt[i][:], x2t[:]))

            def sq(sc, c=c):
                for i in range(DTI):
                    last = sc.activation(sqt[i][:], xt[i][:], AF.Square)
                return last
            p.stage(scalar=sq)

            def mm(t, c=c):
                for i in range(DTI):
                    t.matmul(s1[:], ones_col[:], xt[i][:],
                             start=(i == 0), stop=(i == DTI - 1))
                for i in range(DTI):
                    last = t.matmul(s2[:], ones_col[:], sqt[i][:],
                                    start=(i == 0), stop=(i == DTI - 1))
                return last
            p.stage(tensor=mm)

            def stats_v(v, c=c):
                v.tensor_scalar(meanR[:, csl(c)], s1[:], 1.0 / D, None, ALU.mult)
                v.tensor_scalar(S(1), s2[:], 1.0 / D, None, ALU.mult)
                v.tensor_mul(S(2), meanR[:, csl(c)], meanR[:, csl(c)])
                v.tensor_sub(S(1), S(1), S(2))
                return v.tensor_scalar(S(2), S(1), 1e-5, None, ALU.add)
            p.stage(vector=stats_v)

            def stats_s(sc, c=c):
                sc.activation(S(3), S(2), AF.Ln)
                last = sc.activation(rstdR[:, csl(c)], S(3), AF.Exp, scale=-0.5)
                if rmsR is not None:
                    last = sc.activation(S(4), S(3), AF.Exp, scale=-1.0)
                return last
            p.stage(scalar=stats_s)

            if rmsR is not None:
                def rms_v(v, c=c):
                    v.tensor_mul(S(5), S(1), S(4))
                    return v.tensor_scalar(S(5), S(5), 1e-5, None, ALU.add)
                p.stage(vector=rms_v)

                def rms_s(sc, c=c):
                    sc.activation(S(6), S(5), AF.Ln)
                    return sc.activation(rmsR[:, csl(c)], S(6), AF.Exp, scale=-0.5)
                p.stage(scalar=rms_s)

            def bcmm(t, c=c):
                t.matmul(bc[0][:], ones_row[0:1, :], meanR[:, csl(c)],
                         start=True, stop=True)
                last = t.matmul(bc[1][:], ones_row[0:1, :], rstdR[:, csl(c)],
                                start=True, stop=True)
                if rmsR is not None:
                    last = t.matmul(bc[2][:], ones_row[0:1, :], rmsR[:, csl(c)],
                                    start=True, stop=True)
                return last
            p.stage(tensor=bcmm)

            for u in range(DTI + 2):
                fns = {}
                if u < DTI:
                    def ap_v(v, i=u, u=u):
                        v.tensor_sub(t1[:], xt[i][:], bc[0][:])
                        last = v.tensor_mul(xnt[u % 3][:], t1[:], bc[1][:])
                        if hbf_hbm is not None:
                            last = v.scalar_tensor_tensor(
                                hbt[u % 3][:], xnt[u % 3][:], nwc[:, i:i + 1],
                                bc[2][:], ALU.mult, ALU.mult)
                        return last
                    fns["vector"] = ap_v
                if 0 <= u - 1 < DTI:
                    fns["scalar"] = (lambda sc, us=u - 1: sc.activation(
                        xnb[us % 2][:], xnt[us % 3][:], AF.Copy))
                if 0 <= u - 2 < DTI:
                    def st(s, c=c, i=u - 2, uv=u - 2):
                        if xn32_hbm is not None:
                            p.dma(s, xn32_hbm[i, :, csl(c)], xnt[uv % 3][:])
                        p.dma(s, xnbf_hbm[i, :, csl(c)], xnb[uv % 2][:])
                        if hbf_hbm is not None:
                            p.dma(s, hbf_hbm[i, :, csl(c)], hbt[uv % 3][:])
                        return None
                    fns["sync"] = st
                p.stage(**fns)



def emit_router_grouped(p, C, xn32_hbm, rwT_hbm, bias_hbm, combT, E, GRP, K,
                        scale, mask_fill=-1.0, dbg=None):
    EPG = E // GRP
    with p.scoped():
        ones_row, identf = C["ones_row"], C["identf"]
        rwt = p.sb([128, DTI * E], F32, "rwt")
        br = p.sb([1, E], F32, "rbr")
        bb_ps = p.psum([128, CH], F32, "rbb")
        bb = p.sb([128, E], F32, "rbbs")
        xs = [p.sb([128, CH], F32, "rxs") for _ in range(DTI)]
        lg = p.psum([128, CH], F32, "rlg")
        sc_t = p.sb([128, E], F32, "rsc")
        s_t = p.sb([128, E], F32, "rst")
        g1 = p.sb([128, GRP], F32, "rg1")
        g2 = p.sb([128, GRP], F32, "rg2")
        gm = p.sb([128, GRP], F32, "rgm")
        m1 = p.sb([128, 1], F32, "rm1")
        ms = p.sb([128, E], F32, "rmsk")
        tmp = p.sb([128, E], F32, "rtmp")
        selk = p.sb([128, E], F32, "rselk")
        twk = p.sb([128, 1], F32, "rtwk")
        twsum = p.sb([128, 1], F32, "rtws")
        comb = p.sb([128, E], F32, "rcomb")
        ct_ps = p.psum([E, CH], F32, "rct")

        def ld0(s):
            p.dma(s, rwt[:].rearrange("p (a e) -> p a e", a=DTI),
                  rwT_hbm.rearrange("(a p) e -> p a e", p=128))
            p.dma(s, br[:], bias_hbm[None, :])
            return None
        p.stage(sync=ld0)
        p.stage(vector=lambda v: v.memset(combT[:], 0.0))
        p.stage(tensor=lambda t: t.matmul(bb_ps[:, 0:E], ones_row[0:1, :], br[:],
                                          start=True, stop=True))
        p.stage(vector=lambda v: v.tensor_copy(bb[:], bb_ps[:, 0:E]))

        for c in range(NT):
            def ld(s, c=c):
                for i in range(DTI):
                    p.dma(s, xs[i][:], xn32_hbm[i, :, csl(c)])
                return None
            p.stage(sync=ld)

            for tt in range(4):
                def mm(t, c=c, tt=tt):
                    for i in range(DTI):
                        last = t.matmul(
                            lg[0:128, 0:E], xs[i][:, tt * 128:(tt + 1) * 128],
                            rwt[:, i * E:(i + 1) * E],
                            start=(i == 0), stop=(i == DTI - 1))
                    return last
                p.stage(tensor=mm)
                p.stage(scalar=lambda sc: sc.activation(sc_t[:], lg[0:128, 0:E],
                                                        AF.Sigmoid))

                def route_v(v):
                    def W(instr):
                        instr.then_inc(p.sem("vv"), 1)
                        p.cnt["vv"] += 1
                        v.wait_ge(p.sem("vv"), p.cnt["vv"])
                        return instr
                    W(v.tensor_add(s_t[:], sc_t[:], bb[:]))
                    def gsl(j):
                        return s_t[:].rearrange("p (g e) -> p g e", g=GRP)[:, :, j]
                    first = True
                    for jA in range(EPG):
                        for jB in range(jA + 1, EPG):
                            W(v.tensor_add(g2[:], gsl(jA), gsl(jB)))
                            if first:
                                W(v.tensor_copy(g1[:], g2[:]))
                                first = False
                            else:
                                W(v.tensor_max(g1[:], g1[:], g2[:]))
                    W(v.tensor_reduce(m1[:], g1[:], AX.X, ALU.max))
                    W(v.tensor_scalar(gm[:], g1[:], m1[:, 0:1], None, ALU.is_ge))
                    W(v.scalar_tensor_tensor(g2[:], gm[:], -BIG, g1[:], ALU.mult,
                                             ALU.add))
                    W(v.tensor_reduce(m1[:], g2[:], AX.X, ALU.max))
                    W(v.tensor_scalar(g2[:], g2[:], m1[:, 0:1], None, ALU.is_ge))
                    W(v.tensor_add(gm[:], gm[:], g2[:]))
                    W(v.tensor_scalar(ms[:], s_t[:], -mask_fill, None, ALU.add))
                    for j in range(EPG):
                        W(v.tensor_mul(
                            ms[:].rearrange("p (g e) -> p g e", g=GRP)[:, :, j],
                            ms[:].rearrange("p (g e) -> p g e", g=GRP)[:, :, j],
                            gm[:]))
                    W(v.tensor_scalar(ms[:], ms[:], mask_fill, None, ALU.add))
                    W(v.memset(comb[:], 0.0))
                    W(v.memset(twsum[:], 0.0))
                    for k in range(K):
                        W(v.tensor_reduce(m1[:], ms[:], AX.X, ALU.max))
                        W(v.tensor_scalar(selk[:], ms[:], m1[:, 0:1], None,
                                          ALU.is_ge))
                        W(v.tensor_mul(tmp[:], sc_t[:], selk[:]))
                        W(v.tensor_reduce(twk[:], tmp[:], AX.X, ALU.add))
                        W(v.scalar_tensor_tensor(comb[:], selk[:], twk[:, 0:1],
                                                 comb[:], ALU.mult, ALU.add))
                        W(v.tensor_add(twsum[:], twsum[:], twk[:]))
                        W(v.scalar_tensor_tensor(ms[:], selk[:], -BIG, ms[:],
                                                 ALU.mult, ALU.add))
                    W(v.tensor_scalar(twsum[:], twsum[:], 1e-20, None, ALU.add))
                    W(v.reciprocal(twsum[:], twsum[:]))
                    W(v.tensor_scalar(twsum[:], twsum[:], float(scale), None,
                                      ALU.mult))
                    return v.tensor_scalar(comb[:], comb[:], twsum[:, 0:1],
                                           None, ALU.mult)
                p.stage(vector=route_v)

                if dbg is not None and c == 0 and tt == 0:
                    def dmp(s):
                        p.dma(s, dbg["sc"][:, :], sc_t[:])
                        p.dma(s, dbg["st"][:, :], s_t[:])
                        p.dma(s, dbg["g1"][:, :], g1[:])
                        p.dma(s, dbg["gm"][:, :], gm[:])
                        p.dma(s, dbg["ms"][:, :], ms[:])
                        p.dma(s, dbg["comb"][:, :], comb[:])
                        return None
                    p.stage(sync=dmp)
                p.stage(tensor=lambda t: t.transpose(ct_ps[0:E, 0:128],
                                                     comb[:, 0:E], identf[:]))
                p.stage(vector=lambda v, c=c, tt=tt: v.tensor_copy(
                    combT[0:E, (c * 4 + tt) * 128:(c * 4 + tt + 1) * 128],
                    ct_ps[0:E, 0:128]))



def emit_moe_ffn(p, C, xnbf_hbm, x_seed_hbm, out_hbm, combT, experts, tag="moe"):
    sel32 = C["sel32"]
    NG = 4
    CPG = NT // NG
    with p.scoped():
        max_fti = max(ex["FTI"] for ex in experts)
        xnb = [[p.sb([128, CH], BF16, tag + "xnb") for _ in range(DTI)]
               for _ in range(CPG)]
        acc = [[p.sb([128, CH], F32, tag + "acc") for _ in range(DTI)]
               for _ in range(CPG)]
        wg = [p.sb([128, DTI * 128 * max_fti], BF16, tag + "wg")
              for _ in range(2)]
        wu = [p.sb([128, DTI * 128 * max_fti], BF16, tag + "wu")
              for _ in range(2)]
        wd = [p.sb([128, max_fti * 128 * DTI], BF16, tag + "wd")
              for _ in range(2)]
        ps_g = [p.psum([128, CH], F32, tag + "psg") for _ in range(2)]
        ps_u = [p.psum([128, CH], F32, tag + "psu") for _ in range(3)]
        ps_cb = p.psum([128, CH], F32, tag + "pscb")
        cb_s = [p.sb([128, CH], F32, tag + "cbs") for _ in range(2)]
        ps_dn = [p.psum([128, CH], F32, tag + "psdn") for _ in range(2)]
        sg_t = [p.sb([128, CH], BF16, tag + "sg") for _ in range(2)]
        tmp_t = p.sb([128, CH], F32, tag + "tmp")
        h_t = [p.sb([128, CH], BF16, tag + "h") for _ in range(max_fti)]

        def ldw(gp, ex, par):
            FTI = ex["FTI"]
            p.dma_pf(gp, wg[par][:, 0:DTI * 128 * FTI].rearrange(
                "p (a f) -> p a f", a=DTI),
                ex["gate"].rearrange("(a p) f -> p a f", p=128))
            if ex["up"] is not None:
                p.dma_pf(gp, wu[par][:, 0:DTI * 128 * FTI].rearrange(
                    "p (a f) -> p a f", a=DTI),
                    ex["up"].rearrange("(a p) f -> p a f", p=128))
            p.dma_pf(gp, wd[par][:, 0:FTI * 128 * DTI].rearrange(
                "p (a f) -> p a f", a=FTI),
                ex["down"].rearrange("(a p) f -> p a f", p=128))
            return None

        for g in range(NG):
            chunks = list(range(g * CPG, (g + 1) * CPG))
            pf_after = {}

            def seed(s, g=g, chunks=chunks):
                for ci, c in enumerate(chunks):
                    for i in range(DTI):
                        p.dma(s, acc[ci][i][:], x_seed_hbm[i, :, csl(c)])
                        p.dma(s, xnb[ci][i][:], xnbf_hbm[i, :, csl(c)])
                return None
            p.stage(sync=seed, gpsimd=lambda gp: ldw(gp, experts[0], 0))
            pf_after[0] = p.cnt["pf"]

            for ei, ex in enumerate(experts):
                FTI = ex["FTI"]
                par = ei % 2
                if ei + 1 < len(experts):
                    p.stage(gpsimd=lambda gp, ei=ei: ldw(
                        gp, experts[ei + 1], (ei + 1) % 2))
                    pf_after[ei + 1] = p.cnt["pf"]

                units = [(ci, fi) for ci in range(CPG) for fi in range(FTI)]
                NU = len(units)
                pf_tgt = pf_after[ei]

                for u in range(NU + 2):
                    fns = {}
                    if u < NU:
                        ci_u, fi_u = units[u]

                        def tf(t, ex=ex, par=par, ci=ci_u, fi=fi_u, FTI=FTI,
                               u=u, pf_tgt=pf_tgt):
                            if u == 0:
                                t.wait_ge(p.sem("pf"), pf_tgt)
                            c = chunks[ci]
                            if ex["e"] is not None and fi == 0:
                                t.matmul(ps_cb[:],
                                         sel32[:, ex["e"] * 128:
                                               (ex["e"] + 1) * 128],
                                         combT[0:32, csl(c)],
                                         start=True, stop=True)
                            for i in range(DTI):
                                last = t.matmul(
                                    ps_g[u % 2][:],
                                    wg[par][:, (i * FTI + fi) * 128:
                                            (i * FTI + fi + 1) * 128],
                                    xnb[ci][i][:], start=(i == 0),
                                    stop=(i == DTI - 1))
                            if ex["up"] is not None:
                                for i in range(DTI):
                                    last = t.matmul(
                                        ps_u[u % 3][:],
                                        wu[par][:, (i * FTI + fi) * 128:
                                                (i * FTI + fi + 1) * 128],
                                        xnb[ci][i][:], start=(i == 0),
                                        stop=(i == DTI - 1))
                            return last
                        fns["tensor"] = tf
                    if 0 <= u - 1 < NU:
                        us = u - 1
                        fns["scalar"] = (lambda sc, us=us: sc.activation(
                            sg_t[us % 2][:], ps_g[us % 2][:], AF.Silu))
                    if 0 <= u - 2 < NU:
                        uv = u - 2
                        ci_v, fi_v = units[uv]

                        def vf(v, ex=ex, fi=fi_v, uv=uv, ci=ci_v):
                            if ex["up"] is not None:
                                v.tensor_mul(tmp_t[:], sg_t[uv % 2][:],
                                             ps_u[uv % 3][:])
                                if ex["e"] is not None:
                                    return v.tensor_mul(h_t[fi][:], tmp_t[:],
                                                        cb_s[ci % 2][:])
                                return v.tensor_copy(h_t[fi][:], tmp_t[:])
                            if ex["e"] is not None:
                                return v.tensor_mul(h_t[fi][:], sg_t[uv % 2][:],
                                                    cb_s[ci % 2][:])
                            return v.tensor_copy(h_t[fi][:], sg_t[uv % 2][:])
                        fns["vector"] = vf
                    p.stage(**fns)
                    if u < NU and ex["e"] is not None and units[u][1] == 0:
                        p.stage(vector=lambda v, pcb=units[u][0] % 2:
                                v.tensor_copy(cb_s[pcb][:], ps_cb[:]))

                    if 0 <= u - 2 < NU and units[u - 2][1] == FTI - 1:
                        ci_d = units[u - 2][0]
                        for w2 in range(DTI + 1):
                            fns2 = {}
                            if w2 < DTI:
                                def dmm(t, ex=ex, par=par, i=w2, FTI=FTI,
                                        w2=w2):
                                    for fi in range(FTI):
                                        last = t.matmul(
                                            ps_dn[w2 % 2][:],
                                            wd[par][:, (fi * DTI + i) * 128:
                                                    (fi * DTI + i + 1) * 128],
                                            h_t[fi][:], start=(fi == 0),
                                            stop=(fi == FTI - 1))
                                    return last
                                fns2["tensor"] = dmm
                            if 0 <= w2 - 1 < DTI:
                                def dacc(v, ci=ci_d, i=w2 - 1, w2=w2):
                                    return v.tensor_add(acc[ci][i][:],
                                                        acc[ci][i][:],
                                                        ps_dn[(w2 - 1) % 2][:])
                                fns2["vector"] = dacc
                            p.stage(**fns2)

            def store(s, g=g, chunks=chunks):
                for ci, c in enumerate(chunks):
                    for i in range(DTI):
                        p.dma(s, out_hbm[i, :, csl(c)], acc[ci][i][:])
                return None
            p.stage(sync=store)



def emit_mamba(p, C, W, hbf_hbm, xn32_hbm, ffnacc_hbm, x1_hbm, scratch):
    DI = 2048
    UTI = 16
    ucbf, zbf, dtbf, dtubf, y3bf = (scratch["ucbf"], scratch["zbf"],
                                    scratch["dtbf"], scratch["dtubf"],
                                    scratch["y3bf"])
    sel96 = p.sb([96, 32 * 128], F32, "sel96")

    def selb(g):
        g.memset(sel96[:], 0.0)
        return g.affine_select(
            out=sel96[:].rearrange("p (e c) -> p e c", e=32),
            in_=sel96[:].rearrange("p (e c) -> p e c", e=32),
            pattern=[[-1, 32], [0, 128]], base=-64, channel_multiplier=1,
            compare_op=ALU.not_equal, fill=1.0)
    p.stage(gpsimd=selb)

    cw = p.sb([128, 64], F32, "mcw")
    cb = p.sb([128, 16], F32, "mcb")
    dcol = p.sb([128, 16], F32, "mD")
    dtb = p.sb([128, 16], F32, "mdtb")
    a_t = p.sb([128, 16 * 16], F32, "mA")

    def ldc(s):
        p.dma(s, cw[:], W["conv_w"])
        p.dma(s, cb[:], W["conv_b"])
        p.dma(s, dcol[:], W["D"])
        p.dma(s, dtb[:], W["dt_b"])
        p.dma(s, a_t[:], W["A_log"])
        return None
    p.stage(sync=ldc)
    p.stage(scalar=lambda sc: sc.activation(a_t[:], a_t[:], AF.Exp))
    p.stage(vector=lambda v: v.tensor_scalar(a_t[:], a_t[:], -1.0, None,
                                             ALU.mult))

    with p.scoped():
        hb = [p.sb([128, T], BF16, "mhb") for _ in range(DTI)]
        wip = p.sb([128, DTI * 128], BF16, "mwip")
        ps_x = [p.psum([128, CH], F32, "mpsx") for _ in range(2)]
        u_raw = p.sb([128, T], F32, "muraw")
        cvt = p.sb([128, SEQ], F32, "mcvt")
        ucb = p.sb([128, SEQ], BF16, "mucb")
        zb = [p.sb([128, CH], BF16, "mzb") for _ in range(2)]

        def ldh(s):
            for i in range(DTI):
                p.dma(s, hb[i][:], hbf_hbm[i, :, :])
            return None
        p.stage(sync=ldh)

        for fb in range(32):
            is_u = fb < 16
            p.stage(gpsimd=lambda gp, fb=fb: (p.dma(
                gp, wip[:].rearrange("p (a f) -> p a f", a=DTI),
                W["in_proj"].rearrange("(a p) f -> p a f", p=128)
                [:, :, fb * 128:(fb + 1) * 128]), None)[1])

            for u in range(NT + 1):
                fns = {}
                if u < NT:
                    def mm(t, fb=fb, c=u):
                        for i in range(DTI):
                            last = t.matmul(ps_x[c % 2][:],
                                            wip[:, i * 128:(i + 1) * 128],
                                            hb[i][:, csl(c)], start=(i == 0),
                                            stop=(i == DTI - 1))
                        return last
                    fns["tensor"] = mm
                if 0 <= u - 1 < NT:
                    c = u - 1
                    if is_u:
                        fns["vector"] = (lambda v, c=c: v.tensor_copy(
                            u_raw[:, csl(c)], ps_x[c % 2][:]))
                    else:
                        fns["scalar"] = (lambda sc, c=c: sc.activation(
                            zb[c % 2][:], ps_x[c % 2][:], AF.Copy))
                        if c >= 1:
                            fns["sync"] = (lambda s, fb=fb, c=c - 1: (p.dma(
                                s, zbf[fb - 16, :, csl(c)], zb[c % 2][:]),
                                None)[1])
                p.stage(**fns)
            if not is_u:
                p.stage(sync=lambda s, fb=fb: (p.dma(
                    s, zbf[fb - 16, :, csl(NT - 1)], zb[(NT - 1) % 2][:]),
                    None)[1])

            if is_u:
                for r in range(RPC):
                    base = r * SEQ

                    def conv(v, fb=fb, base=base):
                        u = u_raw
                        v.tensor_scalar(cvt[:], u[:, base:base + SEQ],
                                        cw[:, fb * 4 + 3:fb * 4 + 4], None,
                                        ALU.mult)
                        v.scalar_tensor_tensor(
                            cvt[:, 1:SEQ], u[:, base:base + SEQ - 1],
                            cw[:, fb * 4 + 2:fb * 4 + 3], cvt[:, 1:SEQ],
                            ALU.mult, ALU.add)
                        v.scalar_tensor_tensor(
                            cvt[:, 2:SEQ], u[:, base:base + SEQ - 2],
                            cw[:, fb * 4 + 1:fb * 4 + 2], cvt[:, 2:SEQ],
                            ALU.mult, ALU.add)
                        return v.scalar_tensor_tensor(
                            cvt[:, 3:SEQ], u[:, base:base + SEQ - 3],
                            cw[:, fb * 4 + 0:fb * 4 + 1], cvt[:, 3:SEQ],
                            ALU.mult, ALU.add)
                    p.stage(vector=conv)
                    p.stage(scalar=lambda sc, fb=fb: sc.activation(
                        ucb[:], cvt[:], AF.Silu, bias=cb[:, fb:fb + 1]))
                    p.stage(sync=lambda s, fb=fb, base=base: (p.dma(
                        s, ucbf[fb, :, base:base + SEQ], ucb[:]), None)[1])

    xdbl = p.sb([96, T], F32, "mxdbl")
    xdbl_bf = p.sb([64, T], BF16, "mxdblbf")
    with p.scoped():
        wxp = p.sb([128, 16 * 96], BF16, "mwxp")
        uct = [[p.sb([128, CH], BF16, "muct") for _ in range(UTI)]
               for _ in range(2)]
        ps_xd = [p.psum([96, CH], F32, "mpsxd") for _ in range(2)]

        p.stage(gpsimd=lambda gp: (p.dma(
            gp, wxp[:].rearrange("p (a f) -> p a f", a=16),
            W["x_proj"].rearrange("(a p) f -> p a f", p=128)), None)[1])

        def ldu(s, c, par):
            for k in range(UTI):
                p.dma(s, uct[par][k][:], ucbf[k, :, csl(c)])
            return None
        p.stage(sync=lambda s: ldu(s, 0, 0))
        for u in range(NT + 1):
            fns = {}
            if u < NT:
                if u + 1 < NT:
                    fns["sync"] = (lambda s, u=u: ldu(s, u + 1, (u + 1) % 2))

                def mm(t, c=u):
                    for k in range(UTI):
                        last = t.matmul(ps_xd[c % 2][:],
                                        wxp[:, k * 96:(k + 1) * 96],
                                        uct[c % 2][k][:], start=(k == 0),
                                        stop=(k == 15))
                    return last
                fns["tensor"] = mm
            if 0 <= u - 1 < NT:
                c = u - 1
                fns["vector"] = (lambda v, c=c: v.tensor_copy(
                    xdbl[:, csl(c)], ps_xd[c % 2][:]))
                fns["scalar"] = (lambda sc, c=c: sc.activation(
                    xdbl_bf[:, csl(c)], ps_xd[c % 2][0:64, :], AF.Copy))
            p.stage(**fns)

    with p.scoped():
        wdt = p.sb([64, 16 * 128], BF16, "mwdt")
        ps_dt = [p.psum([128, CH], F32, "mpsdt") for _ in range(2)]
        ta = [p.sb([128, CH], F32, "mta") for _ in range(2)]
        tb = p.sb([128, CH], F32, "mtb")
        td = [p.sb([128, CH], F32, "mtd") for _ in range(2)]
        dtf = p.sb([128, CH], F32, "mdtf")
        uct1 = [p.sb([128, CH], BF16, "muct1") for _ in range(2)]
        dtb16 = [p.sb([128, CH], BF16, "mdtb16") for _ in range(2)]
        dtub16 = [p.sb([128, CH], BF16, "mdtub16") for _ in range(2)]

        p.stage(gpsimd=lambda gp: (p.dma(
            gp, wdt[:], W["dt_w"]), None)[1])
        units3 = [(f, c) for f in range(16) for c in range(NT)]
        N3 = len(units3)
        p.stage(sync=lambda s: (p.dma(
            s, uct1[0][:], ucbf[units3[0][0], :, csl(units3[0][1])]), None)[1])
        for u in range(N3 + 3):
            fns = {}
            if u < N3:
                f_u, c_u = units3[u]

                def mm3(t, f=f_u, c=c_u, u=u):
                    return t.matmul(ps_dt[u % 2][:],
                                    wdt[:, f * 128:(f + 1) * 128],
                                    xdbl_bf[:, csl(c)], start=True, stop=True)
                fns["tensor"] = mm3
                sy = []
                if u + 1 < N3:
                    sy.append((u + 1,))
            if 0 <= u - 1 < N3:
                us = u - 1
                f_s = units3[us][0]

                def sp3(sc, f=f_s, us=us):
                    sc.activation(ta[us % 2][:], ps_dt[us % 2][:], AF.Abs,
                                  bias=dtb[:, f:f + 1])
                    sc.activation(tb[:], ta[us % 2][:], AF.Exp, scale=-1.0)
                    sc.activation(ta[us % 2][:], tb[:], AF.Ln, bias=1.0)
                    return sc.activation(td[us % 2][:], ps_dt[us % 2][:],
                                         AF.Relu, bias=dtb[:, f:f + 1])
                fns["scalar"] = sp3
            if 0 <= u - 2 < N3:
                uv = u - 2

                def fv3(v, uv=uv):
                    v.tensor_add(dtf[:], ta[uv % 2][:], td[uv % 2][:])
                    v.tensor_copy(dtb16[uv % 2][:], dtf[:])
                    return v.tensor_mul(dtub16[uv % 2][:], dtf[:],
                                        uct1[uv % 2][:])
                fns["vector"] = fv3

            def sy3(s, u=u):
                if u + 1 < N3:
                    f2, c2 = units3[u + 1]
                    p.dma(s, uct1[(u + 1) % 2][:], ucbf[f2, :, csl(c2)])
                if 0 <= u - 3 < N3:
                    f3, c3 = units3[u - 3]
                    p.dma(s, dtbf[f3, :, csl(c3)], dtb16[(u - 3) % 2][:])
                    p.dma(s, dtubf[f3, :, csl(c3)], dtub16[(u - 3) % 2][:])
                return None
            fns["sync"] = sy3
            p.stage(**fns)

    HS = SEQ // 2
    with p.scoped():
        dt_t = p.sb([128, SEQ], BF16, "mdt")
        dtu_t = p.sb([128, SEQ], BF16, "mdtu")
        ps_b = [p.psum([128, HS], F32, "mpsb") for _ in range(2)]
        ps_c = [p.psum([128, HS], F32, "mpsc") for _ in range(2)]
        da_s = [p.sb([128, HS], BF16, "mda") for _ in range(2)]
        dbu_t = p.sb([128, HS], BF16, "mdbu")
        h_t = [p.sb([128, HS], BF16, "mh") for _ in range(2)]
        hc_t = p.sb([128, HS], BF16, "mhc")
        yacc = p.sb([128, SEQ], BF16, "myacc")
        zero_col = p.sb([128, 1], F32, "mzc")
        uct2 = p.sb([128, SEQ], BF16, "muct2")
        zt2 = p.sb([128, SEQ], BF16, "mzt2")
        szt = p.sb([128, SEQ], F32, "mszt")
        y2t = p.sb([128, SEQ], F32, "my2")
        y3t = p.sb([128, SEQ], BF16, "my3")

        p.stage(gpsimd=lambda g: g.memset(zero_col[:], 0.0))

        for r in range(RPC):
            rsl = slice(r * SEQ, (r + 1) * SEQ)
            for f in range(16):
                def lddt(s, f=f, rsl=rsl):
                    p.dma(s, dt_t[:], dtbf[f, :, rsl])
                    p.dma(s, dtu_t[:], dtubf[f, :, rsl])
                    return None
                def ldz(s, f=f, rsl=rsl):
                    p.dma(s, uct2[:], ucbf[f, :, rsl])
                    p.dma(s, zt2[:], zbf[f, :, rsl])
                    return None
                p.stage(sync=lddt)

                units = [(n, tc) for n in range(16) for tc in range(2)]
                NV = len(units)

                def bcast(t, v, r=r):
                    n, tc = units[v]
                    for j in range(2):
                        jsl = slice(tc * HS + j * CH, tc * HS + (j + 1) * CH)
                        psl = slice(j * CH, (j + 1) * CH)
                        rs = slice(r * SEQ + tc * HS + j * CH,
                                   r * SEQ + tc * HS + (j + 1) * CH)
                        t.matmul(ps_b[v % 2][:, psl],
                                 sel96[:, n * 128:(n + 1) * 128],
                                 xdbl[:, rs], start=True, stop=True)
                        last = t.matmul(ps_c[v % 2][:, psl],
                                        sel96[:, (16 + n) * 128:(17 + n) * 128],
                                        xdbl[:, rs], start=True, stop=True)
                    return last

                for v in range(NV + 1):
                    fns = {}
                    if v < NV:
                        def tfm(t, v=v):
                            return bcast(t, v)
                        fns["tensor"] = tfm

                        def sfm(sc, v=v, f=f):
                            n, tc = units[v]
                            return sc.activation(
                                da_s[v % 2][:], dt_t[:, tc * HS:(tc + 1) * HS],
                                AF.Exp,
                                scale=a_t[:, f * 16 + n:f * 16 + n + 1])
                        fns["scalar"] = sfm
                    if 0 <= v - 1 < NV:
                        vv_ = v - 1

                        def vfm(v_, vv_=vv_):
                            n, tc = units[vv_]
                            tsl = slice(tc * HS, (tc + 1) * HS)
                            v_.tensor_mul(dbu_t[:],
                                          dtu_t[:, tsl], ps_b[vv_ % 2][:])
                            init = (zero_col[:, 0:1] if tc == 0
                                    else h_t[(vv_ - 1) % 2][:, HS - 1:HS])
                            v_.tensor_tensor_scan(h_t[vv_ % 2][:], da_s[vv_ % 2][:],
                                                  dbu_t[:], init,
                                                  ALU.mult, ALU.add)
                            v_.tensor_mul(hc_t[:], h_t[vv_ % 2][:],
                                          ps_c[vv_ % 2][:])
                            if units[vv_][0] == 0:
                                return v_.tensor_copy(yacc[:, tsl], hc_t[:])
                            return v_.tensor_add(yacc[:, tsl], yacc[:, tsl],
                                                 hc_t[:])
                        fns["vector"] = vfm
                    if v == NV:
                        fns["sync"] = lambda s, f=f, rsl=rsl: ldz(s, f, rsl)
                    p.stage(**fns)

                p.stage(scalar=lambda sc: sc.activation(szt[:], zt2[:], AF.Silu))

                def fin_v(v, f=f):
                    v.scalar_tensor_tensor(y2t[:], uct2[:], dcol[:, f:f + 1],
                                           yacc[:], ALU.mult, ALU.add)
                    return v.tensor_mul(y3t[:], y2t[:], szt[:])
                p.stage(vector=fin_v)
                p.stage(sync=lambda s, f=f, rsl=rsl: (p.dma(
                    s, y3bf[f, :, rsl], y3t[:]), None)[1])

    with p.scoped():
        wop = p.sb([128, 16 * 1024], BF16, "mwop")
        y3c = [[p.sb([128, CH], BF16, "my3c") for _ in range(16)]
               for _ in range(2)]
        xnc = [p.sb([128, CH], F32, "mxnc") for _ in range(2)]
        fac = [p.sb([128, CH], F32, "mfac") for _ in range(2)]
        ps_o = [p.psum([128, CH], F32, "mpso") for _ in range(2)]
        x1t = [p.sb([128, CH], F32, "mx1") for _ in range(2)]

        p.stage(gpsimd=lambda gp: (p.dma(
            gp, wop[:].rearrange("p (a f) -> p a f", a=16),
            W["out_proj"].rearrange("(a p) f -> p a f", p=128)), None)[1])

        units5 = [(c, i) for c in range(NT) for i in range(DTI)]
        N5 = len(units5)

        def ldxf(s, u):
            c, i = units5[u]
            p.dma(s, xnc[u % 2][:], xn32_hbm[i, :, csl(c)])
            p.dma(s, fac[u % 2][:], ffnacc_hbm[i, :, csl(c)])
            return None

        def ldy(s, c, par):
            for k in range(16):
                p.dma(s, y3c[par][k][:], y3bf[k, :, csl(c)])
            return None

        p.stage(sync=lambda s: (ldy(s, 0, 0), ldxf(s, 0), None)[2])
        for u in range(N5 + 2):
            fns = {}
            if u < N5:
                c_u, i_u = units5[u]

                def mm(t, c=c_u, i=i_u, u=u):
                    for k in range(16):
                        last = t.matmul(ps_o[u % 2][:],
                                        wop[:, (k * 8 + i) * 128:
                                            (k * 8 + i + 1) * 128],
                                        y3c[(c % 2)][k][:], start=(k == 0),
                                        stop=(k == 15))
                    return last
                fns["tensor"] = mm
            if 0 <= u - 1 < N5:
                uv = u - 1

                def fin(v, uv=uv):
                    v.tensor_add(x1t[uv % 2][:], ps_o[uv % 2][:],
                                 xnc[uv % 2][:])
                    return v.tensor_add(x1t[uv % 2][:], x1t[uv % 2][:],
                                        fac[uv % 2][:])
                fns["vector"] = fin

            def sy5(s, u=u):
                if u + 1 < N5:
                    ldxf(s, u + 1)
                    if units5[u + 1][1] == 0 and u + 2 < N5 + 16:
                        pass
                if u + 8 < N5 and units5[u + 8][1] == 0:
                    ldy(s, units5[u + 8][0], units5[u + 8][0] % 2)
                if 0 <= u - 2 < N5:
                    c2, i2 = units5[u - 2]
                    p.dma(s, x1_hbm[i2, :, csl(c2)], x1t[(u - 2) % 2][:])
                return None
            fns["sync"] = sy5
            p.stage(**fns)



def declare_w(p, names):
    win = {}
    for nm, sh in names:
        win[nm] = p.dram(nm, sh, F32, kind="ExternalInput")
    return win


def emit_block0(p, C, win, x, x1, debug=False):
    dk = "ExternalOutput" if debug else "Internal"
    xn32 = p.dram("b0_xn32", [DTI, 128, T], F32, kind=dk)
    xnbf = p.dram("b0_xnbf", [DTI, 128, T], BF16)
    hbf = p.dram("b0_hbf", [DTI, 128, T], BF16, kind=dk)
    ffnacc = p.dram("b0_ffnacc", [DTI, 128, T], F32, kind=dk)
    scratch = dict(
        ucbf=p.dram("b0_ucbf", [16, 128, T], BF16, kind=dk),
        zbf=p.dram("b0_zbf", [16, 128, T], BF16),
        dtbf=p.dram("b0_dtbf", [16, 128, T], BF16, kind=dk),
        dtubf=p.dram("b0_dtubf", [16, 128, T], BF16),
        y3bf=p.dram("b0_y3bf", [16, 128, T], BF16, kind=dk),
    )

    emit_ln(p, C, x, xn32, xnbf, hbf_hbm=hbf, normw_hbm=win["l0a_norm_w"])

    with p.scoped():
        combT = p.sb([32, T], F32, "combT")
        emit_router_grouped(p, C, xn32, win["l0f_rwT"], win["l0f_bias"], combT,
                            E=16, GRP=4, K=4, scale=2.5)
        experts = []
        for e in range(16):
            experts.append(dict(gate=win["l0f_gate"][e], up=win["l0f_up"][e],
                                down=win["l0f_down"][e], e=e, FTI=8))
        for half in range(2):
            fs = slice(half * 1024, (half + 1) * 1024)
            experts.append(dict(gate=win["l0f_shg"][:, fs],
                                up=win["l0f_shu"][:, fs],
                                down=win["l0f_shd"][fs, :], e=None, FTI=8))
        emit_moe_ffn(p, C, xnbf, x, ffnacc, combT, experts, tag="m0f")

    W = dict(in_proj=win["l0a_in_proj"], conv_w=win["l0a_conv_w"],
             conv_b=win["l0a_conv_b"], x_proj=win["l0a_x_proj"],
             dt_w=win["l0a_dt_w"], dt_b=win["l0a_dt_b"], A_log=win["l0a_A_log"],
             D=win["l0a_D"], out_proj=win["l0a_out_proj"])
    with p.scoped():
        emit_mamba(p, C, W, hbf, xn32, ffnacc, x1, scratch)


W_BLOCK0 = [
    ("l0a_norm_w", [128, DTI]), ("l0a_in_proj", [D, 4096]),
    ("l0a_conv_w", [128, 64]),
    ("l0a_conv_b", [128, 16]), ("l0a_x_proj", [2048, 96]),
    ("l0a_dt_w", [64, 2048]),
    ("l0a_dt_b", [128, 16]), ("l0a_A_log", [128, 256]), ("l0a_D", [128, 16]),
    ("l0a_out_proj", [2048, D]),
    ("l0f_rwT", [D, 16]), ("l0f_bias", [16]),
    ("l0f_gate", [16, D, 1024]), ("l0f_up", [16, D, 1024]),
    ("l0f_down", [16, 1024, D]),
    ("l0f_shg", [D, 2048]), ("l0f_shu", [D, 2048]), ("l0f_shd", [2048, D]),
]


def build_block0(debug=False):
    p = Prog()
    x = p.dram("x", [DTI, 128, T], F32, kind="ExternalInput")
    x1 = p.dram("x1", [DTI, 128, T], F32, kind="ExternalOutput")
    win = declare_w(p, W_BLOCK0)
    C = emit_consts(p)
    emit_block0(p, C, win, x, x1, debug=debug)
    p.final_wait()
    return p.finish()



def _fm_shard(x_full):
    out = []
    for c in range(NCORE):
        xc = x_full[2 * c:2 * c + 2]
        fm = np.ascontiguousarray(
            xc.transpose(2, 0, 1).reshape(DTI, 128, T))
        out.append(fm)
    return out


def _fm_unshard(shards):
    x = np.empty((16, SEQ, D), np.float32)
    for c in range(NCORE):
        fm = shards[c].reshape(D, RPC, SEQ)
        x[2 * c:2 * c + 2] = fm.transpose(1, 2, 0)
    return x


def prep_block0_inputs(inputs):
    f32 = np.float32
    def col128(v):
        return np.ascontiguousarray(np.asarray(v, f32).reshape(-1, 128).T)

    w = {}
    w["l0a_norm_w"] = col128(inputs["L0_attn_norm_w"])
    w["l0a_in_proj"] = np.ascontiguousarray(inputs["L0_attn_in_proj_w"], f32)
    w["l0a_conv_w"] = np.ascontiguousarray(
        np.asarray(inputs["L0_attn_conv_w"], f32).reshape(16, 128, 4)
        .transpose(1, 0, 2).reshape(128, 64))
    w["l0a_conv_b"] = col128(inputs["L0_attn_conv_b"])
    w["l0a_x_proj"] = np.ascontiguousarray(inputs["L0_attn_x_proj_w"], f32)
    w["l0a_dt_w"] = np.ascontiguousarray(inputs["L0_attn_dt_proj_w"], f32)
    w["l0a_dt_b"] = col128(inputs["L0_attn_dt_proj_b"])
    w["l0a_A_log"] = np.ascontiguousarray(
        np.asarray(inputs["L0_attn_A_log"], f32).reshape(16, 128, 16)
        .transpose(1, 0, 2).reshape(128, 256))
    w["l0a_D"] = col128(inputs["L0_attn_D"])
    w["l0a_out_proj"] = np.ascontiguousarray(inputs["L0_attn_out_proj_w"], f32)
    w["l0f_rwT"] = np.ascontiguousarray(inputs["L0_ffn_router_w"].T, f32)
    w["l0f_bias"] = np.ascontiguousarray(inputs["L0_ffn_bias"], f32)
    w["l0f_gate"] = np.ascontiguousarray(inputs["L0_ffn_gate_w"], f32)
    w["l0f_up"] = np.ascontiguousarray(inputs["L0_ffn_up_w"], f32)
    w["l0f_down"] = np.ascontiguousarray(inputs["L0_ffn_down_w"], f32)
    w["l0f_shg"] = np.ascontiguousarray(inputs["L0_ffn_sh_gate_w"], f32)
    w["l0f_shu"] = np.ascontiguousarray(inputs["L0_ffn_sh_up_w"], f32)
    w["l0f_shd"] = np.ascontiguousarray(inputs["L0_ffn_sh_down_w"], f32)
    return w


def run_block0(inputs, x_full, debug=False):
    w = prep_block0_inputs(inputs)
    shards = _fm_shard(x_full)
    nc = build_block0(debug=debug)
    in_maps = [dict(w, x=shards[c]) for c in range(NCORE)]
    res = run_bass_kernel_spmd(nc, in_maps, core_ids=list(range(NCORE)))
    if debug:
        return res
    return _fm_unshard([res.results[c]["x1"] for c in range(NCORE)])


def kernel(**inputs):
    _ = inputs.get("L0_attn_hormone_vectors")
    x = np.asarray(inputs["L0_attn_x"], np.float32)
    wA = dict(prep_block0_inputs(inputs))
    w1 = prep_block1_inputs(inputs)
    for k in W_BLOCK1F:
        wA[k[0]] = w1[k[0]]
    wB = {k[0]: w1[k[0]] for k in W_BLOCK1A}
    wC = dict(prep_block2_inputs(inputs))
    wC.update(prep_block3_inputs(inputs))

    shards = _fm_shard(x)
    ncA = build_blockA()
    resA = run_bass_kernel_spmd(ncA, [dict(wA, x=shards[c]) for c in range(NCORE)],
                                core_ids=list(range(NCORE)))
    x1 = _fm_unshard([resA.results[c]["x1"] for c in range(NCORE)])
    y1p_shards = [resA.results[c]["y1p"] for c in range(NCORE)]

    ncB = build_blockB()
    ashards = _fm_shard_tau(x1)
    resB = run_bass_kernel_spmd(ncB, [dict(wB, xa=ashards[c])
                                      for c in range(NCORE)],
                               core_ids=list(range(NCORE)))
    att = _fm_unshard_tau([resB.results[c]["att1"] for c in range(NCORE)])
    att_shards = _fm_shard(att)

    ncC = build_blockC()
    resC = run_bass_kernel_spmd(
        ncC, [dict(wC, y1p=y1p_shards[c], att=att_shards[c])
              for c in range(NCORE)], core_ids=list(range(NCORE)))
    return _fm_unshard([resC.results[c]["x4"] for c in range(NCORE)])



def emit_add(p, a_hbm, b_hbm, out_hbm, nt=DTI):
    with p.scoped():
        at = p.sb([128, CH], F32, "adda")
        bt = p.sb([128, CH], F32, "addb")
        ot = p.sb([128, CH], F32, "addo")
        for i in range(nt):
            for c in range(NT):
                def ld(s, i=i, c=c):
                    p.dma(s, at[:], a_hbm[i, :, csl(c)])
                    p.dma(s, bt[:], b_hbm[i, :, csl(c)])
                    return None
                p.stage(sync=ld)
                p.stage(vector=lambda v: v.tensor_add(ot[:], at[:], bt[:]))
                p.stage(sync=lambda s, i=i, c=c: (p.dma(
                    s, out_hbm[i, :, csl(c)], ot[:]), None)[1])



def emit_router_top2softmax(p, C, xn32_hbm, rwT_hbm, combT, E):
    ones_row, identf = C["ones_row"], C["identf"]
    with p.scoped():
        rwt = p.sb([128, DTI * E], F32, "r2wt")
        xs = [p.sb([128, CH], F32, "r2xs") for _ in range(DTI)]
        lg = p.psum([128, CH], F32, "r2lg")
        lgs = p.sb([128, E], F32, "r2lgs")
        m1 = p.sb([128, 1], F32, "r2m1")
        m2 = p.sb([128, 1], F32, "r2m2")
        sel1 = p.sb([128, E], F32, "r2s1")
        sel2 = p.sb([128, E], F32, "r2s2")
        msk = p.sb([128, E], F32, "r2msk")
        w1c = p.sb([128, 1], F32, "r2w1")
        comb = p.sb([128, E], F32, "r2comb")
        ct_ps = p.psum([E, CH], F32, "r2ct")

        p.stage(sync=lambda s: (p.dma(
            s, rwt[:].rearrange("p (a e) -> p a e", a=DTI),
            rwT_hbm.rearrange("(a p) e -> p a e", p=128)), None)[1])
        p.stage(vector=lambda v: v.memset(combT[:], 0.0))

        for c in range(NT):
            def ld(s, c=c):
                for i in range(DTI):
                    p.dma(s, xs[i][:], xn32_hbm[i, :, csl(c)])
                return None
            p.stage(sync=ld)

            for tt in range(4):
                def mm(t, c=c, tt=tt):
                    for i in range(DTI):
                        last = t.matmul(
                            lg[0:128, 0:E], xs[i][:, tt * 128:(tt + 1) * 128],
                            rwt[:, i * E:(i + 1) * E],
                            start=(i == 0), stop=(i == DTI - 1))
                    return last
                p.stage(tensor=mm)

                def route_v(v):
                    def W(instr):
                        instr.then_inc(p.sem("vv"), 1)
                        p.cnt["vv"] += 1
                        v.wait_ge(p.sem("vv"), p.cnt["vv"])
                        return instr
                    W(v.tensor_copy(lgs[:], lg[0:128, 0:E]))
                    W(v.tensor_reduce(m1[:], lgs[:], AX.X, ALU.max))
                    W(v.tensor_scalar(sel1[:], lgs[:], m1[:, 0:1], None,
                                      ALU.is_ge))
                    W(v.scalar_tensor_tensor(msk[:], sel1[:], -BIG, lgs[:],
                                             ALU.mult, ALU.add))
                    W(v.tensor_reduce(m2[:], msk[:], AX.X, ALU.max))
                    W(v.tensor_scalar(sel2[:], msk[:], m2[:, 0:1], None,
                                      ALU.is_ge))
                    W(v.tensor_sub(m2[:], m2[:], m1[:]))
                    return None
                p.stage(vector=route_v)
                p.stage(scalar=lambda sc: sc.activation(m2[:], m2[:], AF.Exp))

                def route_v2(v):
                    def W(instr):
                        instr.then_inc(p.sem("vv"), 1)
                        p.cnt["vv"] += 1
                        v.wait_ge(p.sem("vv"), p.cnt["vv"])
                        return instr
                    W(v.tensor_scalar(w1c[:], m2[:], 1.0, None, ALU.add))
                    W(v.reciprocal(w1c[:], w1c[:]))
                    W(v.tensor_scalar(comb[:], sel1[:], w1c[:, 0:1], None,
                                      ALU.mult))
                    W(v.tensor_scalar(w1c[:], w1c[:], -1.0, None, ALU.mult))
                    W(v.tensor_scalar(w1c[:], w1c[:], 1.0, None, ALU.add))
                    return v.scalar_tensor_tensor(comb[:], sel2[:], w1c[:, 0:1],
                                                  comb[:], ALU.mult, ALU.add)
                p.stage(vector=route_v2)

                p.stage(tensor=lambda t: t.transpose(ct_ps[0:E, 0:128],
                                                     comb[:, 0:E], identf[:]))
                p.stage(vector=lambda v, c=c, tt=tt: v.tensor_copy(
                    combT[0:E, (c * 4 + tt) * 128:(c * 4 + tt + 1) * 128],
                    ct_ps[0:E, 0:128]))



def emit_l2_attn(p, C, W2, xnbf_hbm, ffnacc_hbm, out_hbm):
    ones_row = C["ones_row"]
    ident = C["ident"]
    CT = 2
    with p.scoped():
        bt = [p.sb([128, 1024], F32, "l2bt") for _ in range(CT)]
        cf = [p.sb([128, 256], F32, "l2cf") for _ in range(CT)]
        ps_w = p.psum([128, CH], F32, "l2psw")
        qwb = p.sb([128, DTI * 256], BF16, "l2qw")
        kwb = p.sb([128, DTI * 256], BF16, "l2kw")
        vwb = p.sb([128, DTI * 256], BF16, "l2vw")
        owT = p.sb([128, CT * 1024], BF16, "l2ow")
        sc2 = p.sb([1, 2], F32, "l2sc")
        ldc = p.sb([128, 1], F32, "l2ldc")
        osc = p.sb([128, 1], F32, "l2osc")
        ps_b1 = p.psum([128, CH], F32, "l2psb")

        def ldsc(s):
            p.dma(s, sc2[:, 0:1], W2["decay_logit"][None, :])
            p.dma(s, sc2[:, 1:2], W2["out_scale"][None, :])
            return None
        p.stage(sync=ldsc)
        p.stage(scalar=lambda sc: sc.activation(sc2[:, 0:1], sc2[:, 0:1],
                                                AF.Sigmoid))

        def lnd(sc):
            return sc.activation(sc2[:, 0:1], sc2[:, 0:1], AF.Ln)
        p.stage(scalar=lnd)

        def bc1(t):
            t.matmul(ps_b1[:, 0:1], ones_row[0:1, :], sc2[:, 0:1],
                     start=True, stop=True)
            return t.matmul(ps_b1[:, 1:2], ones_row[0:1, :], sc2[:, 1:2],
                            start=True, stop=True)
        p.stage(tensor=bc1)

        def cp1(v):
            v.tensor_copy(ldc[:], ps_b1[:, 0:1])
            return v.tensor_copy(osc[:], ps_b1[:, 1:2])
        p.stage(vector=cp1)

        wdec = [p.sb([128, CH], F32, "l2wd") for _ in range(19)]
        for mi, m in enumerate(range(-3, 16)):
            def bw(g, mi=mi, m=m):
                return g.iota(wdec[mi][:], [[-1, CH]], base=128 * m - 1,
                              channel_multiplier=1,
                              allow_small_or_imprecise_dtypes=True)
            p.stage(gpsimd=bw)

            def sw(v, mi=mi):
                return v.tensor_scalar(wdec[mi][:], wdec[mi][:], ldc[:, 0:1],
                                       None, ALU.mult)
            p.stage(vector=sw)
            p.stage(scalar=lambda sc, mi=mi: sc.activation(
                wdec[mi][:], wdec[mi][:], AF.Exp))

            def msk(g, mi=mi, m=m):
                return g.affine_select(out=wdec[mi][:], in_=wdec[mi][:],
                                       pattern=[[-1, CH]], base=128 * m,
                                       channel_multiplier=1,
                                       compare_op=ALU.is_gt, fill=0.0)
            p.stage(gpsimd=msk)

        def ldb(s):
            for j in range(CT):
                p.dma(s, bt[j][:], W2["basis_T"][j * 128:(j + 1) * 128, :])
            return None
        p.stage(sync=ldb)

        for nm, dst in (("qcT", qwb), ("kcT", kwb), ("vcT", vwb)):
            def ldcf(s, nm=nm):
                for j in range(CT):
                    p.dma(s, cf[j][:, 0:256], W2[nm][j * 128:(j + 1) * 128, :])
                return None
            p.stage(sync=ldcf)
            for vt in range(DTI):
                def mmw(t, vt=vt):
                    for j in range(CT):
                        last = t.matmul(ps_w[:, 0:256],
                                        bt[j][:, vt * 128:(vt + 1) * 128],
                                        cf[j][:, 0:256], start=(j == 0),
                                        stop=(j == CT - 1))
                    return last
                p.stage(tensor=mmw)
                p.stage(scalar=lambda sc, vt=vt, dst=dst: sc.activation(
                    dst[:, vt * 256:(vt + 1) * 256], ps_w[:, 0:256], AF.Copy))

        def ldcf2(s):
            for j in range(CT):
                p.dma(s, cf[j][:, 0:256], W2["ocT"][j * 128:(j + 1) * 128, :])
            return None
        p.stage(sync=ldcf2)
        for ct in range(CT):
            for vc in range(2):
                def mmo(t, ct=ct, vc=vc):
                    for j in range(CT):
                        last = t.matmul(
                            ps_w[:], cf[j][:, ct * 128:(ct + 1) * 128],
                            bt[j][:, vc * 512:(vc + 1) * 512],
                            start=(j == 0), stop=(j == CT - 1))
                    return last
                p.stage(tensor=mmo)
                p.stage(scalar=lambda sc, ct=ct, vc=vc: sc.activation(
                    owT[:, ct * 1024 + vc * 512: ct * 1024 + (vc + 1) * 512],
                    ps_w[:], AF.Copy))

        xnb = [p.sb([128, CH], BF16, "l2xnb") for _ in range(DTI)]
        qf = [p.sb([128, T], BF16, "l2qf") for _ in range(CT)]
        kf = [p.sb([128, T], BF16, "l2kf") for _ in range(CT)]
        vf = [p.sb([128, T], BF16, "l2vf") for _ in range(CT)]
        for c in range(NT):
            def ldx(s, c=c):
                for i in range(DTI):
                    p.dma(s, xnb[i][:], xnbf_hbm[i, :, csl(c)])
                return None
            p.stage(sync=ldx)
            for wsb, dst in ((qwb, qf), (kwb, kf), (vwb, vf)):
                for ct in range(CT):
                    def mmp(t, wsb=wsb, ct=ct, c=c):
                        for i in range(DTI):
                            last = t.matmul(
                                ps_w[:],
                                wsb[:, i * 256 + ct * 128:
                                    i * 256 + (ct + 1) * 128],
                                xnb[i][:], start=(i == 0), stop=(i == DTI - 1))
                        return last
                    p.stage(tensor=mmp)
                    p.stage(scalar=lambda sc, dst=dst, ct=ct, c=c: sc.activation(
                        dst[ct][:, csl(c)], ps_w[:], AF.Copy))

        vtm = [p.sb([128, 16 * 256], BF16, "l2vtm") for _ in range(RPC)]
        ps_t = p.psum([128, 128], BF16, "l2pst")
        for r in range(RPC):
            for t2 in range(16):
                for ct in range(CT):
                    def tp(t, r=r, t2=t2, ct=ct):
                        return t.transpose(
                            ps_t[:], vf[ct][:, r * SEQ + t2 * 128:
                                            r * SEQ + (t2 + 1) * 128], ident[:])
                    p.stage(tensor=tp)
                    p.stage(vector=lambda v, r=r, t2=t2, ct=ct: v.tensor_copy(
                        vtm[r][:, t2 * 256 + ct * 128:t2 * 256 + (ct + 1) * 128],
                        ps_t[:]))

        ps_s = [p.psum([128, CH], F32, "l2pss") for _ in range(2)]
        ps_r = [p.psum([128, CH], F32, "l2psr") for _ in range(CT)]
        sw_t = [p.sb([128, CH], BF16, "l2sw") for _ in range(2)]
        retr = [p.sb([128, CH], BF16, "l2retr") for _ in range(CT)]
        fac = p.sb([128, CH], F32, "l2fac")
        ot = p.sb([128, CH], F32, "l2ot")
        for r in range(RPC):
            for t1c in range(4):
                t2lo = max(0, 4 * t1c - 3)
                t2s = list(range(t2lo, 16))
                NU2 = len(t2s)
                for u in range(NU2 + 2):
                    fns = {}

                    def tf2(t, u=u, r=r, t1c=t1c, t2s=t2s, NU2=NU2):
                        last = None
                        if u < NU2:
                            t2 = t2s[u]
                            for ct in range(CT):
                                last = t.matmul(
                                    ps_s[u % 2][:],
                                    kf[ct][:, r * SEQ + t2 * 128:
                                           r * SEQ + (t2 + 1) * 128],
                                    qf[ct][:, r * SEQ + t1c * CH:
                                           r * SEQ + (t1c + 1) * CH],
                                    start=(ct == 0), stop=(ct == CT - 1))
                        if 0 <= u - 2 < NU2:
                            idx = u - 2
                            t2 = t2s[idx]
                            for ct in range(CT):
                                last = t.matmul(
                                    ps_r[ct][:],
                                    vtm[r][:, t2 * 256 + ct * 128:
                                           t2 * 256 + (ct + 1) * 128],
                                    sw_t[idx % 2][:], start=(idx == 0),
                                    stop=(t2 == 15), skip_group_check=True)
                        return last
                    fns["tensor"] = tf2
                    if 0 <= u - 1 < NU2:
                        uv = u - 1
                        mi = t2s[uv] - 4 * t1c + 3
                        fns["vector"] = (lambda v, mi=mi, uv=uv: v.tensor_mul(
                            sw_t[uv % 2][:], ps_s[uv % 2][:], wdec[mi][:]))
                    p.stage(**fns)

                def evr(sc, r=r):
                    for ct in range(CT):
                        last = sc.activation(retr[ct][:], ps_r[ct][:], AF.Copy,
                                             scale=osc[:, 0:1])
                    return last
                p.stage(scalar=evr)

                for vt in range(DTI):
                    def mmo2(t, vt=vt):
                        for ct in range(CT):
                            last = t.matmul(
                                ps_w[:],
                                owT[:, ct * 1024 + vt * 128:
                                    ct * 1024 + (vt + 1) * 128],
                                retr[ct][:], start=(ct == 0), stop=(ct == CT - 1))
                        return last
                    p.stage(tensor=mmo2)

                    cg = r * 4 + t1c
                    def ldf(s, vt=vt, cg=cg):
                        p.dma(s, fac[:], ffnacc_hbm[vt, :, csl(cg)])
                        return None
                    p.stage(sync=ldf)
                    p.stage(vector=lambda v: v.tensor_add(ot[:], ps_w[:],
                                                          fac[:]))
                    p.stage(sync=lambda s, vt=vt, cg=cg: (p.dma(
                        s, out_hbm[vt, :, csl(cg)], ot[:]), None)[1])



W_BLOCK2 = [
    ("l2a_basis_T", [256, 1024]), ("l2a_qcT", [256, 256]),
    ("l2a_kcT", [256, 256]), ("l2a_vcT", [256, 256]), ("l2a_ocT", [256, 256]),
    ("l2a_decay", [1]), ("l2a_oscale", [1]),
    ("l2f_rwT", [D, 16]),
    ("l2f_w1T", [16, D, 512]), ("l2f_w2T", [16, 512, D]),
    ("l2f_sfc1T", [D, 2048]), ("l2f_sfc2T", [D, 2048]),
    ("l2f_sfc3T", [2048, D]),
]


def emit_block2(p, C, win, x, x3):
    xn32 = p.dram("b2_xn32", [DTI, 128, T], F32)
    xnbf = p.dram("b2_xnbf", [DTI, 128, T], BF16)
    ffnacc = p.dram("b2_ffnacc", [DTI, 128, T], F32)

    emit_ln(p, C, x, xn32, xnbf)
    with p.scoped():
        combT = p.sb([32, T], F32, "combT")
        emit_router_top2softmax(p, C, xn32, win["l2f_rwT"], combT, E=16)
        experts = []
        for e in range(16):
            experts.append(dict(gate=win["l2f_w1T"][e], up=None,
                                down=win["l2f_w2T"][e], e=e, FTI=4))
        for half in range(2):
            fs = slice(half * 1024, (half + 1) * 1024)
            experts.append(dict(gate=win["l2f_sfc1T"][:, fs],
                                up=win["l2f_sfc2T"][:, fs],
                                down=win["l2f_sfc3T"][fs, :], e=None, FTI=8))
        emit_moe_ffn(p, C, xnbf, x, ffnacc, combT, experts, tag="m2f")

    W2 = dict(basis_T=win["l2a_basis_T"], qcT=win["l2a_qcT"], kcT=win["l2a_kcT"],
              vcT=win["l2a_vcT"], ocT=win["l2a_ocT"],
              decay_logit=win["l2a_decay"], out_scale=win["l2a_oscale"])
    emit_l2_attn(p, C, W2, xnbf, ffnacc, x3)


def build_block2(debug=False, add_in=False):
    p = Prog()
    x_in = p.dram("x", [DTI, 128, T], F32, kind="ExternalInput")
    if add_in:
        x2b = p.dram("x2b", [DTI, 128, T], F32, kind="ExternalInput")
        x = p.dram("xsum", [DTI, 128, T], F32)
    else:
        x2b = None
        x = x_in
    x3 = p.dram("x3", [DTI, 128, T], F32, kind="ExternalOutput")
    win = declare_w(p, W_BLOCK2)
    C = emit_consts(p)
    if add_in:
        emit_add(p, x_in, x2b, x)
    emit_block2(p, C, win, x, x3)
    p.final_wait()
    return p.finish()


def prep_block2_inputs(inputs):
    f32 = np.float32
    w = {}
    w["l2a_basis_T"] = np.ascontiguousarray(inputs["L2_attn_basis"].T, f32)
    w["l2a_qcT"] = np.ascontiguousarray(inputs["L2_attn_q_coeffs"].T, f32)
    w["l2a_kcT"] = np.ascontiguousarray(inputs["L2_attn_k_coeffs"].T, f32)
    w["l2a_vcT"] = np.ascontiguousarray(inputs["L2_attn_v_coeffs"].T, f32)
    w["l2a_ocT"] = np.ascontiguousarray(inputs["L2_attn_o_coeffs"].T, f32)
    w["l2a_decay"] = np.asarray(inputs["L2_attn_decay_logit"], f32).reshape(1)
    w["l2a_oscale"] = np.asarray(inputs["L2_attn_out_scale"], f32).reshape(1)
    w["l2f_rwT"] = np.ascontiguousarray(inputs["L2_ffn_gate_w"].T, f32)
    w["l2f_w1T"] = np.ascontiguousarray(
        np.asarray(inputs["L2_ffn_w1"], f32).transpose(0, 2, 1))
    w["l2f_w2T"] = np.ascontiguousarray(
        np.asarray(inputs["L2_ffn_w2"], f32).transpose(0, 2, 1))
    w["l2f_sfc1T"] = np.ascontiguousarray(inputs["L2_ffn_sfc1"].T, f32)
    w["l2f_sfc2T"] = np.ascontiguousarray(inputs["L2_ffn_sfc2"].T, f32)
    w["l2f_sfc3T"] = np.ascontiguousarray(inputs["L2_ffn_sfc3"].T, f32)
    return w


def run_block2(inputs, x_full, x2b_full=None, debug=False):
    w = prep_block2_inputs(inputs)
    shards = _fm_shard(x_full)
    nc = build_block2(debug=debug, add_in=x2b_full is not None)
    if x2b_full is not None:
        shards2 = _fm_shard(x2b_full)
        in_maps = [dict(w, x=shards[c], x2b=shards2[c]) for c in range(NCORE)]
    else:
        in_maps = [dict(w, x=shards[c]) for c in range(NCORE)]
    res = run_bass_kernel_spmd(nc, in_maps, core_ids=list(range(NCORE)))
    if debug:
        return res
    return _fm_unshard([res.results[c]["x3"] for c in range(NCORE)])



def emit_l3_attn(p, C, W3, xnbf_hbm, ffnacc_hbm, out_hbm, apre_hbm):
    ones_row = C["ones_row"]
    with p.scoped():
        xnb = [p.sb([128, SEQ], BF16, "l3xnb") for _ in range(DTI)]
        onesb = p.sb([128, 1], BF16, "l3ones")
        pen = [p.sb([128, CH], BF16, "l3pen") for _ in range(4)]
        qf = [p.sb([128, SEQ], BF16, "l3qf") for _ in range(DTI)]
        kf = [p.sb([128, SEQ], BF16, "l3kf") for _ in range(DTI)]
        vtm = p.sb([128, 16 * 1024], BF16, "l3vtm")
        ps_p = p.psum([128, CH], F32, "l3psp")
        ps_s = [[p.psum([128, CH], F32, "l3pss") for _ in range(2)]
                for _ in range(2)]
        ps_pv = p.psum([64, CH], F32, "l3pspv")
        ps_den = p.psum([1, CH], F32, "l3psden")
        ps_db = p.psum([64, CH], F32, "l3psdb")
        pt_t = [[p.sb([128, CH], BF16, "l3pt") for _ in range(2)]
                for _ in range(3)]
        rd_t = p.sb([1, CH], F32, "l3rd")
        at_s = p.sb([64, CH], F32, "l3ats")
        bm8 = p.sb([128, 1], F32, "l3bm8")
        at_b = p.sb([64, CH], BF16, "l3atb")
        fac = p.sb([128, CH], F32, "l3fac")
        ot = p.sb([128, CH], F32, "l3ot")

        def mkpen(g):
            E50 = 1.9287498479639178e-22
            for m in range(4):
                g.memset(pen[m][:], 1.0)
                last = g.affine_select(out=pen[m][:], in_=pen[m][:],
                                       pattern=[[1, CH]], base=-128 * m,
                                       channel_multiplier=-1,
                                       compare_op=ALU.is_ge, fill=E50)
            g.memset(onesb[:], 1.0)
            g.memset(bm8[:], -8.0)
            return last
        p.stage(gpsimd=mkpen)

        for r in range(RPC):
            rsl = slice(r * SEQ, (r + 1) * SEQ)
            def ldx(s, rsl=rsl):
                for i in range(DTI):
                    p.dma(s, xnb[i][:], xnbf_hbm[i, :, rsl])
                return None
            p.stage(sync=ldx)
            for nm, dst, scl in (("Wq", qf, 0.125), ("Wk", kf, 0.125)):
                with p.scoped():
                    wt = p.sb([128, DTI * 1024], BF16, "l3w")
                    p.stage(gpsimd=lambda gp, nm=nm: (p.dma(
                        gp, wt[:].rearrange("p (a f) -> p a f", a=DTI),
                        W3[nm].rearrange("(a p) f -> p a f", p=128)), None)[1])
                    for fo in range(DTI):
                        for cc in range(4):
                            def mmq(t, fo=fo, cc=cc, r=r):
                                for i in range(DTI):
                                    last = t.matmul(
                                        ps_p[:],
                                        wt[:, (i * 8 + fo) * 128:
                                           (i * 8 + fo + 1) * 128],
                                        xnb[i][:, cc * CH:(cc + 1) * CH],
                                        start=(i == 0), stop=(i == DTI - 1))
                                return last
                            p.stage(tensor=mmq)
                            p.stage(scalar=lambda sc, dst=dst, fo=fo, cc=cc,
                                    scl=scl: sc.activation(
                                        dst[fo][:, cc * CH:(cc + 1) * CH],
                                        ps_p[:], AF.Copy, scale=scl))

            with p.scoped():
                wt = p.sb([128, DTI * 1024], BF16, "l3wv")
                p.stage(gpsimd=lambda gp: (p.dma(
                    gp, wt[:].rearrange("p (a f) -> p a f", a=DTI),
                    W3["Wv"].rearrange("(a p) f -> p a f", p=128)), None)[1])
                for t2 in range(16):
                    for dc in range(2):
                        def mmv(t, t2=t2, dc=dc, r=r):
                            for i in range(DTI):
                                last = t.matmul(
                                    ps_p[:],
                                    xnb[i][:, t2 * 128:(t2 + 1) * 128],
                                    wt[:, i * 1024 + dc * CH:
                                       i * 1024 + (dc + 1) * CH],
                                    start=(i == 0), stop=(i == DTI - 1))
                            return last
                        p.stage(tensor=mmv)
                        p.stage(scalar=lambda sc, t2=t2, dc=dc: sc.activation(
                            vtm[:, t2 * 1024 + dc * CH:t2 * 1024 + (dc + 1) * CH],
                            ps_p[:], AF.Copy))

            for h in range(16):
                hi, ho = h // 2, (h % 2) * 64
                for t1c in range(4):
                    n_t2 = 4 * t1c + 4
                    npair = n_t2 // 2
                    for w in range(npair + 3):
                        fns = {}

                        def tfa(t, w=w, h=h, hi=hi, ho=ho, t1c=t1c, r=r,
                                n_t2=n_t2, npair=npair):
                            last = None
                            if w < npair:
                                for j in range(2):
                                    t2 = 2 * w + j
                                    last = t.matmul(
                                        ps_s[w % 2][j][:],
                                        kf[hi][ho:ho + 64,
                                               t2 * 128:(t2 + 1) * 128],
                                        qf[hi][ho:ho + 64,
                                               t1c * CH:(t1c + 1) * CH],
                                        start=True, stop=True)
                            if 0 <= w - 3 < npair:
                                for j in range(2):
                                    t2 = 2 * (w - 3) + j
                                    t.matmul(ps_pv[:],
                                             vtm[:, t2 * 1024 + h * 64:
                                                 t2 * 1024 + h * 64 + 64],
                                             pt_t[(w - 3) % 3][j][:],
                                             start=(t2 == 0),
                                             stop=(t2 == n_t2 - 1),
                                             skip_group_check=True)
                                    last = t.matmul(ps_den[:], onesb[:],
                                                    pt_t[(w - 3) % 3][j][:],
                                                    start=(t2 == 0),
                                                    stop=(t2 == n_t2 - 1),
                                                    skip_group_check=True)
                            return last
                        fns["tensor"] = tfa
                        if 0 <= w - 1 < npair:
                            def sfa(sc, w=w):
                                for j in range(2):
                                    last = sc.activation(
                                        pt_t[(w - 1) % 3][j][:],
                                        ps_s[(w - 1) % 2][j][:],
                                        AF.Exp, bias=bm8[:, 0:1])
                                return last
                            fns["scalar"] = sfa
                        if 0 <= w - 2 < npair:
                            js = [j for j in range(2)
                                  if 2 * (w - 2) + j - 4 * t1c >= 0]
                            if js:
                                def vfa(v, w=w, t1c=t1c, js=tuple(js)):
                                    for j in js:
                                        m = 2 * (w - 2) + j - 4 * t1c
                                        last = v.tensor_mul(
                                            pt_t[(w - 2) % 3][j][:],
                                            pt_t[(w - 2) % 3][j][:],
                                            pen[m][:])
                                    return last
                                fns["vector"] = vfa
                        p.stage(**fns)

                    def nrm2(v):
                        return v.reciprocal(rd_t[:], ps_den[:])
                    p.stage(vector=nrm2,
                            scalar=lambda sc: sc.activation(at_s[:], ps_pv[:],
                                                            AF.Copy))
                    p.stage(tensor=lambda t: t.matmul(
                        ps_db[:], ones_row[0:1, 0:64], rd_t[:],
                        start=True, stop=True))
                    p.stage(vector=lambda v: v.tensor_mul(at_b[:], at_s[:],
                                                          ps_db[:]))

                    def stp(s, hi=hi, ho=ho, r=r, t1c=t1c):
                        p.dma(s, apre_hbm[hi, ho:ho + 64,
                                          r * SEQ + t1c * CH:
                                          r * SEQ + (t1c + 1) * CH], at_b[:])
                        return None
                    p.stage(sync=stp)

        with p.scoped():
            wt = p.sb([128, DTI * 1024], BF16, "l3wr")
            apc = [p.sb([128, CH], BF16, "l3apc") for _ in range(DTI)]
            p.stage(gpsimd=lambda gp: (p.dma(
                gp, wt[:].rearrange("p (a f) -> p a f", a=DTI),
                W3["Wr"].rearrange("(a p) f -> p a f", p=128)), None)[1])
            for c in range(NT):
                def lda(s, c=c):
                    for i in range(DTI):
                        p.dma(s, apc[i][:], apre_hbm[i, :, csl(c)])
                    return None
                p.stage(sync=lda)
                for fo in range(DTI):
                    def mmr2(t, fo=fo, c=c):
                        for i in range(DTI):
                            last = t.matmul(
                                ps_p[:],
                                wt[:, (i * 8 + fo) * 128:(i * 8 + fo + 1) * 128],
                                apc[i][:], start=(i == 0), stop=(i == DTI - 1))
                        return last
                    p.stage(tensor=mmr2)

                    def ldf(s, fo=fo, c=c):
                        p.dma(s, fac[:], ffnacc_hbm[fo, :, csl(c)])
                        return None
                    p.stage(sync=ldf)
                    p.stage(vector=lambda v: v.tensor_add(ot[:], ps_p[:], fac[:]))
                    p.stage(sync=lambda s, fo=fo, c=c: (p.dma(
                        s, out_hbm[fo, :, csl(c)], ot[:]), None)[1])



W_BLOCK3 = [
    ("l3a_Wq", [D, D]), ("l3a_Wk", [D, D]), ("l3a_Wv", [D, D]),
    ("l3a_Wr", [D, D]),
    ("l3f_rwT", [D, 32]), ("l3f_bias", [32]),
    ("l3f_gateT", [32, D, 512]), ("l3f_upT", [32, D, 512]),
    ("l3f_downT", [32, 512, D]),
    ("l3f_shgT", [D, 1024]), ("l3f_shuT", [D, 1024]), ("l3f_shdT", [1024, D]),
]


def emit_block3(p, C, win, x, x4):
    xn32 = p.dram("b3_xn32", [DTI, 128, T], F32)
    xnbf = p.dram("b3_xnbf", [DTI, 128, T], BF16)
    ffnacc = p.dram("b3_ffnacc", [DTI, 128, T], F32)
    apre = p.dram("b3_apre", [DTI, 128, T], BF16)

    emit_ln(p, C, x, xn32, xnbf)
    with p.scoped():
        combT = p.sb([32, T], F32, "combT")
        emit_router_grouped(p, C, xn32, win["l3f_rwT"], win["l3f_bias"], combT,
                            E=32, GRP=4, K=4, scale=2.5, mask_fill=0.0)
        experts = []
        for e in range(32):
            experts.append(dict(gate=win["l3f_gateT"][e], up=win["l3f_upT"][e],
                                down=win["l3f_downT"][e], e=e, FTI=4))
        experts.append(dict(gate=win["l3f_shgT"], up=win["l3f_shuT"],
                            down=win["l3f_shdT"], e=None, FTI=8))
        emit_moe_ffn(p, C, xnbf, x, ffnacc, combT, experts, tag="m3f")

    W3 = dict(Wq=win["l3a_Wq"], Wk=win["l3a_Wk"], Wv=win["l3a_Wv"],
              Wr=win["l3a_Wr"])
    emit_l3_attn(p, C, W3, xnbf, ffnacc, x4, apre)


def build_block3(debug=False):
    p = Prog()
    x = p.dram("x", [DTI, 128, T], F32, kind="ExternalInput")
    x4 = p.dram("x4", [DTI, 128, T], F32, kind="ExternalOutput")
    win = declare_w(p, W_BLOCK3)
    C = emit_consts(p)
    emit_block3(p, C, win, x, x4)
    p.final_wait()
    return p.finish()


def prep_block3_inputs(inputs):
    f32 = np.float32
    w = {}
    w["l3a_Wq"] = np.ascontiguousarray(inputs["L3_attn_Wq"], f32)
    w["l3a_Wk"] = np.ascontiguousarray(inputs["L3_attn_Wk"], f32)
    w["l3a_Wv"] = np.ascontiguousarray(inputs["L3_attn_Wv"], f32)
    w["l3a_Wr"] = np.ascontiguousarray(inputs["L3_attn_Wr"], f32)
    w["l3f_rwT"] = np.ascontiguousarray(inputs["L3_ffn_router_weight"].T, f32)
    w["l3f_bias"] = np.ascontiguousarray(inputs["L3_ffn_router_bias"], f32)
    w["l3f_gateT"] = np.ascontiguousarray(
        np.asarray(inputs["L3_ffn_gate_w"], f32).transpose(0, 2, 1))
    w["l3f_upT"] = np.ascontiguousarray(
        np.asarray(inputs["L3_ffn_up_w"], f32).transpose(0, 2, 1))
    w["l3f_downT"] = np.ascontiguousarray(
        np.asarray(inputs["L3_ffn_down_w"], f32).transpose(0, 2, 1))
    w["l3f_shgT"] = np.ascontiguousarray(inputs["L3_ffn_shared_gate_w"].T, f32)
    w["l3f_shuT"] = np.ascontiguousarray(inputs["L3_ffn_shared_up_w"].T, f32)
    w["l3f_shdT"] = np.ascontiguousarray(inputs["L3_ffn_shared_down_w"].T, f32)
    return w


def run_block3(inputs, x_full, debug=False):
    w = prep_block3_inputs(inputs)
    shards = _fm_shard(x_full)
    nc = build_block3(debug=debug)
    in_maps = [dict(w, x=shards[c]) for c in range(NCORE)]
    res = run_bass_kernel_spmd(nc, in_maps, core_ids=list(range(NCORE)))
    if debug:
        return res
    return _fm_unshard([res.results[c]["x4"] for c in range(NCORE)])



def emit_router_softmax_top2(p, C, xn32_hbm, rwT_hbm, mT, E=8):
    ones_row, identf = C["ones_row"], C["identf"]
    with p.scoped():
        rwt = p.sb([128, DTI * E], F32, "r1wt")
        xs = [p.sb([128, CH], F32, "r1xs") for _ in range(DTI)]
        lg = p.psum([128, CH], F32, "r1lg")
        lgs = p.sb([128, E], F32, "r1lgs")
        mx = p.sb([128, 1], F32, "r1mx")
        sm = p.sb([128, E], F32, "r1sm")
        ssum = p.sb([128, 1], F32, "r1ss")
        m1 = p.sb([128, 1], F32, "r1m1")
        sel1 = p.sb([128, E], F32, "r1s1")
        sel2 = p.sb([128, E], F32, "r1s2")
        msk = p.sb([128, E], F32, "r1msk")
        pack = p.sb([128, 18], F32, "r1pack")
        ct_ps = p.psum([18, CH], F32, "r1ct")

        p.stage(sync=lambda s: (p.dma(
            s, rwt[:].rearrange("p (a e) -> p a e", a=DTI),
            rwT_hbm.rearrange("(a p) e -> p a e", p=128)), None)[1])
        p.stage(vector=lambda v: v.memset(mT[:], 0.0))

        for c in range(NT):
            def ld(s, c=c):
                for i in range(DTI):
                    p.dma(s, xs[i][:], xn32_hbm[i, :, csl(c)])
                return None
            p.stage(sync=ld)

            for tt in range(4):
                def mm(t, c=c, tt=tt):
                    for i in range(DTI):
                        last = t.matmul(
                            lg[0:128, 0:E], xs[i][:, tt * 128:(tt + 1) * 128],
                            rwt[:, i * E:(i + 1) * E],
                            start=(i == 0), stop=(i == DTI - 1))
                    return last
                p.stage(tensor=mm)

                def r1(v):
                    def W(instr):
                        instr.then_inc(p.sem("vv"), 1)
                        p.cnt["vv"] += 1
                        v.wait_ge(p.sem("vv"), p.cnt["vv"])
                        return instr
                    W(v.tensor_copy(lgs[:], lg[0:128, 0:E]))
                    W(v.tensor_reduce(mx[:], lgs[:], AX.X, ALU.max))
                    return v.tensor_scalar(mx[:], mx[:], -1.0, None, ALU.mult)
                p.stage(vector=r1)
                p.stage(scalar=lambda sc: sc.activation(
                    sm[:], lgs[:], AF.Exp, bias=mx[:, 0:1],
                    accum_out=ssum[:]))

                def r2(v):
                    def W(instr):
                        instr.then_inc(p.sem("vv"), 1)
                        p.cnt["vv"] += 1
                        v.wait_ge(p.sem("vv"), p.cnt["vv"])
                        return instr
                    W(v.reciprocal(ssum[:], ssum[:]))
                    W(v.tensor_scalar(sm[:], sm[:], ssum[:, 0:1], None, ALU.mult))
                    W(v.tensor_reduce(m1[:], sm[:], AX.X, ALU.max))
                    W(v.tensor_scalar(sel1[:], sm[:], m1[:, 0:1], None, ALU.is_ge))
                    W(v.tensor_copy(pack[:, 16:17], m1[:]))
                    W(v.scalar_tensor_tensor(msk[:], sel1[:], -BIG, sm[:],
                                             ALU.mult, ALU.add))
                    W(v.tensor_reduce(m1[:], msk[:], AX.X, ALU.max))
                    W(v.tensor_scalar(sel2[:], msk[:], m1[:, 0:1], None, ALU.is_ge))
                    W(v.tensor_copy(pack[:, 17:18], m1[:]))
                    W(v.tensor_copy(pack[:, 0:E], sel1[:]))
                    return v.tensor_copy(pack[:, 8:8 + E], sel2[:])
                p.stage(vector=r2)

                p.stage(tensor=lambda t: t.transpose(ct_ps[0:18, 0:128],
                                                     pack[:, 0:18], identf[:]))
                p.stage(vector=lambda v, c=c, tt=tt: v.tensor_copy(
                    mT[0:18, (c * 4 + tt) * 128:(c * 4 + tt + 1) * 128],
                    ct_ps[0:18, 0:128]))



def emit_l1_ffn(p, C, xnbf_hbm, seed_hbm, out_hbm, mT, W1, tag="l1f"):
    sel32 = C["sel32"]
    NG = 8
    CPG = NT // NG
    E = 8
    with p.scoped():
        xnb = [[p.sb([128, CH], BF16, tag + "xnb") for _ in range(DTI)]
               for _ in range(CPG)]
        xsb = [[[p.sb([128, CH], BF16, tag + "xsb") for _ in range(DTI)]
                for _ in range(CPG)] for _ in range(2)]
        acc = [[p.sb([128, CH], F32, tag + "acc") for _ in range(DTI)]
               for _ in range(CPG)]
        w1t = [p.sb([128, DTI * 1024], BF16, tag + "w1") for _ in range(2)]
        w2t = [p.sb([128, DTI * 1024], BF16, tag + "w2") for _ in range(2)]
        w3t = [p.sb([128, DTI * 1024], BF16, tag + "w3") for _ in range(2)]
        bcols = [p.sb([128, 3 * DTI], F32, tag + "bc") for _ in range(2)]
        ps_h = [p.psum([128, CH], F32, tag + "psh") for _ in range(2)]
        ps_x3 = [p.psum([128, CH], F32, tag + "psx3") for _ in range(2)]
        ps_h2 = [p.psum([128, CH], F32, tag + "psh2") for _ in range(2)]
        ps_cb = p.psum([128, CH], F32, tag + "pscb")
        cb_s = p.sb([128, CH], F32, tag + "cbs")
        h_t = [p.sb([128, CH], BF16, tag + "h") for _ in range(DTI)]
        x3_t = [p.sb([128, CH], BF16, tag + "x3") for _ in range(DTI)]
        x2_t = [p.sb([128, CH], BF16, tag + "x2") for _ in range(2)]
        tmp_t = p.sb([128, CH], F32, tag + "tmp")

        def ldw(gp, e, par):
            p.dma_pf(gp, w1t[par][:].rearrange("p (a f) -> p a f", a=DTI),
                     W1["w1T"][e].rearrange("(a p) f -> p a f", p=128))
            p.dma_pf(gp, w2t[par][:].rearrange("p (a f) -> p a f", a=DTI),
                     W1["w2T"][e].rearrange("(a p) f -> p a f", p=128))
            p.dma_pf(gp, w3t[par][:].rearrange("p (a f) -> p a f", a=DTI),
                     W1["w3T"][e].rearrange("(a p) f -> p a f", p=128))
            p.dma_pf(gp, bcols[par][:, 0:DTI], W1["b1c"][e])
            p.dma_pf(gp, bcols[par][:, DTI:2 * DTI], W1["b2c"][e])
            p.dma_pf(gp, bcols[par][:, 2 * DTI:3 * DTI], W1["b3c"][e])
            return None

        for g in range(NG):
            chunks = list(range(g * CPG, (g + 1) * CPG))
            pf_after = {}

            def seed(s, g=g, chunks=chunks):
                for ci, c in enumerate(chunks):
                    for i in range(DTI):
                        p.dma(s, acc[ci][i][:], seed_hbm[i, :, csl(c)])
                        p.dma(s, xnb[ci][i][:], xnbf_hbm[i, :, csl(c)])
                return None
            p.stage(sync=seed, gpsimd=lambda gp: ldw(gp, 0, 0))
            pf_after[0] = p.cnt["pf"]

            for k in range(2):
                for ci, c in enumerate(chunks):
                    def bc(t, k=k, c=c):
                        return t.matmul(ps_cb[:],
                                        sel32[:, (16 + k) * 128:(17 + k) * 128],
                                        mT[0:32, csl(c)], start=True, stop=True)
                    p.stage(tensor=bc)
                    for i in range(DTI):
                        p.stage(vector=lambda v, k=k, ci=ci, i=i: v.tensor_mul(
                            xsb[k][ci][i][:], xnb[ci][i][:], ps_cb[:]))

            for e in range(E):
                par = e % 2
                if e + 1 < E:
                    p.stage(gpsimd=lambda gp, e=e: ldw(gp, e + 1, (e + 1) % 2))
                    pf_after[e + 1] = p.cnt["pf"]
                pf_tgt = pf_after[e]

                for k in range(2):
                    for ci, c in enumerate(chunks):
                        def bcm(t, e=e, k=k, c=c, pf_tgt=pf_tgt, first=(k == 0 and ci == 0)):
                            if first:
                                t.wait_ge(p.sem("pf"), pf_tgt)
                            return t.matmul(
                                ps_cb[:], sel32[:, (k * 8 + e) * 128:
                                                (k * 8 + e + 1) * 128],
                                mT[0:32, csl(c)], start=True, stop=True)
                        p.stage(tensor=bcm)
                        p.stage(vector=lambda v: v.tensor_copy(cb_s[:],
                                                               ps_cb[:]))

                        for u in range(DTI + 2):
                            fns = {}
                            if u < DTI:
                                def tf(t, par=par, k=k, ci=ci, fo=u, u=u):
                                    for i in range(DTI):
                                        t.matmul(
                                            ps_h[u % 2][:],
                                            w1t[par][:, (i * 8 + fo) * 128:
                                                     (i * 8 + fo + 1) * 128],
                                            xsb[k][ci][i][:], start=(i == 0),
                                            stop=(i == DTI - 1))
                                    for i in range(DTI):
                                        last = t.matmul(
                                            ps_x3[u % 2][:],
                                            w3t[par][:, (i * 8 + fo) * 128:
                                                     (i * 8 + fo + 1) * 128],
                                            xsb[k][ci][i][:], start=(i == 0),
                                            stop=(i == DTI - 1))
                                    return last
                                fns["tensor"] = tf
                            if 0 <= u - 1 < DTI:
                                us = u - 1
                                fns["scalar"] = (
                                    lambda sc, us=us, par=par: sc.activation(
                                        h_t[us][:], ps_h[us % 2][:], AF.Silu,
                                        bias=bcols[par][:, us:us + 1]))
                            if 0 <= u - 2 < DTI:
                                uv = u - 2
                                fns["vector"] = (
                                    lambda v, uv=uv, par=par: v.tensor_scalar(
                                        x3_t[uv][:], ps_x3[uv % 2][:],
                                        bcols[par][:, 2 * DTI + uv:
                                                    2 * DTI + uv + 1],
                                        None, ALU.add))
                            p.stage(**fns)

                        for w in range(DTI + 1):
                            fns = {}
                            if w < DTI:
                                def tf2(t, par=par, fo=w, w=w):
                                    for fi in range(DTI):
                                        last = t.matmul(
                                            ps_h2[w % 2][:],
                                            w2t[par][:, (fi * 8 + fo) * 128:
                                                     (fi * 8 + fo + 1) * 128],
                                            h_t[fi][:], start=(fi == 0),
                                            stop=(fi == DTI - 1))
                                    return last
                                fns["tensor"] = tf2
                            if 0 <= w - 1 < DTI:
                                wv = w - 1

                                def vf2(v, wv=wv, par=par, ci=ci):
                                    def W(instr):
                                        instr.then_inc(p.sem("vv"), 1)
                                        p.cnt["vv"] += 1
                                        v.wait_ge(p.sem("vv"), p.cnt["vv"])
                                        return instr
                                    W(v.tensor_scalar(
                                        x2_t[wv % 2][:], ps_h2[wv % 2][:],
                                        bcols[par][:, DTI + wv:DTI + wv + 1],
                                        None, ALU.add))
                                    v.tensor_mul(tmp_t[:], x2_t[wv % 2][:],
                                                 x3_t[wv][:])
                                    v.tensor_mul(tmp_t[:], tmp_t[:], cb_s[:])
                                    return v.tensor_add(acc[ci][wv][:],
                                                        acc[ci][wv][:], tmp_t[:])
                                fns["vector"] = vf2
                            p.stage(**fns)

            def store(s, g=g, chunks=chunks):
                for ci, c in enumerate(chunks):
                    for i in range(DTI):
                        p.dma(s, out_hbm[i, :, csl(c)], acc[ci][i][:])
                return None
            p.stage(sync=store)



def emit_l1_attn(p, C, W1a, xnabf_hbm, att1_hbm):
    ones_row, identf, ident = C["ones_row"], C["identf"], C["ident"]
    with p.scoped():
        beta_r = p.sb([1, 16], F32, "l1beta")
        ps_bb = p.psum([128, CH], F32, "l1psbb")
        ebb = p.sb([128, 16], F32, "l1ebb")
        qsc = p.sb([128, DTI], F32, "l1qsc")
        qbe = p.sb([128, DTI], F32, "l1qbe")
        kbe = p.sb([128, DTI], F32, "l1kbe")
        bqc = p.sb([128, DTI], F32, "l1bqc")
        bkc = p.sb([128, DTI], F32, "l1bkc")
        bvc = p.sb([128, DTI], F32, "l1bvc")
        boc = p.sb([128, DTI], F32, "l1boc")
        mask128 = p.sb([128, 128], F32, "l1mask")
        onesb = p.sb([128, 1], BF16, "l1ones")

        def ldb(s):
            p.dma(s, beta_r[:], W1a["beta"][None, :])
            p.dma(s, bqc[:], W1a["bqc"])
            p.dma(s, bkc[:], W1a["bkc"])
            p.dma(s, bvc[:], W1a["bvc"])
            p.dma(s, boc[:], W1a["boc"])
            return None
        p.stage(sync=ldb)
        p.stage(scalar=lambda sc: sc.activation(beta_r[:], beta_r[:], AF.Exp))
        p.stage(tensor=lambda t: t.matmul(ps_bb[:, 0:16], ones_row[0:1, :],
                                          beta_r[:], start=True, stop=True))
        p.stage(vector=lambda v: v.tensor_copy(ebb[:], ps_bb[:, 0:16]))

        def mkscale(v):
            def W(instr):
                instr.then_inc(p.sem("vv"), 1)
                p.cnt["vv"] += 1
                v.wait_ge(p.sem("vv"), p.cnt["vv"])
                return instr
            for ft in range(DTI):
                W(v.tensor_copy(qsc[0:64, ft:ft + 1], ebb[0:64, 2 * ft:2 * ft + 1]))
                W(v.tensor_copy(qsc[64:128, ft:ft + 1],
                                ebb[64:128, 2 * ft + 1:2 * ft + 2]))
            W(v.tensor_scalar(qsc[:], qsc[:], 8.0, None, ALU.mult))
            W(v.reciprocal(qsc[:], qsc[:]))
            W(v.tensor_mul(qbe[:], bqc[:], qsc[:]))
            return v.tensor_mul(kbe[:], bkc[:], qsc[:])
        p.stage(vector=mkscale)

        def mkmask(g):
            g.memset(mask128[:], 1.0)
            g.affine_select(out=mask128[:].rearrange("p (a b) -> p a b", a=8),
                            in_=mask128[:].rearrange("p (a b) -> p a b", a=8),
                            pattern=[[-16, 8], [0, 16]], base=0,
                            channel_multiplier=1, compare_op=ALU.is_ge, fill=0.0)
            g.affine_select(out=mask128[:].rearrange("p (a b) -> p a b", a=8),
                            in_=mask128[:].rearrange("p (a b) -> p a b", a=8),
                            pattern=[[16, 8], [0, 16]], base=15,
                            channel_multiplier=-1, compare_op=ALU.is_ge, fill=0.0)
            g.affine_select(out=mask128[:].rearrange("p (a b) -> p a b", a=8),
                            in_=mask128[:].rearrange("p (a b) -> p a b", a=8),
                            pattern=[[16, 8], [1, 16]], base=0,
                            channel_multiplier=-1, compare_op=ALU.is_ge, fill=0.0)
            g.memset(onesb[:], 1.0)
            return g.memset(mask128[0:1, 0:1], 1.0)
        p.stage(gpsimd=mkmask)

        ps_p = p.psum([128, CH], F32, "l1psp")
        ps_s = [p.psum([128, 128], F32, "l1pss") for _ in range(2)]
        ps_pv = [p.psum([128, 128], F32, "l1pspv") for _ in range(2)]
        ps_t2 = [p.psum([128, 128], BF16, "l1pst")]
        ta = p.sb([128, CH], F32, "l1ta")
        tb = p.sb([128, CH], F32, "l1tb")
        sm_t = [p.sb([128, 128], BF16, "l1sm") for _ in range(2)]
        rden = p.sb([128, 1], F32, "l1rden")
        atm = p.sb([128, 16 * 64], BF16, "l1atm")
        fac = p.sb([128, CH], F32, "l1fac")

        PS = 1024
        NTG = PS // 128
        HT = T // PS
        for hh in range(HT):
            hsl = slice(hh * PS, (hh + 1) * PS)
            with p.scoped():
                xnb = [p.sb([128, PS], BF16, "l1xnb") for _ in range(DTI)]
                qf = [p.sb([128, PS], BF16, "l1qf") for _ in range(DTI)]
                kf = [p.sb([128, PS], BF16, "l1kf") for _ in range(DTI)]
                vtm = p.sb([128, NTG * 1024], BF16, "l1vtm")
                apre = [p.sb([128, PS], BF16, "l1apre") for _ in range(DTI)]

                def ldx(s, hsl=hsl):
                    for i in range(DTI):
                        p.dma(s, xnb[i][:], xnabf_hbm[i, :, hsl])
                    return None
                p.stage(sync=ldx)

                for nm, dst, be in (("WqT", qf, qbe), ("WkT", kf, kbe)):
                    with p.scoped():
                        wt = p.sb([128, DTI * 1024], BF16, "l1w")
                        p.stage(gpsimd=lambda gp, nm=nm: (p.dma(
                            gp, wt[:].rearrange("p (a f) -> p a f", a=DTI),
                            W1a[nm].rearrange("(a p) f -> p a f", p=128)),
                            None)[1])
                        for fo in range(DTI):
                            for cc in range(PS // CH):
                                def mmq(t, fo=fo, cc=cc):
                                    for i in range(DTI):
                                        last = t.matmul(
                                            ps_p[:],
                                            wt[:, (i * 8 + fo) * 128:
                                               (i * 8 + fo + 1) * 128],
                                            xnb[i][:, cc * CH:(cc + 1) * CH],
                                            start=(i == 0), stop=(i == DTI - 1))
                                    return last
                                p.stage(tensor=mmq)

                                def sp1(sc, fo=fo, be=be):
                                    sc.activation(ta[:], ps_p[:], AF.Abs,
                                                  bias=be[:, fo:fo + 1],
                                                  scale=qsc[:, fo:fo + 1])
                                    sc.activation(tb[:], ta[:], AF.Exp,
                                                  scale=-1.0)
                                    sc.activation(ta[:], tb[:], AF.Ln, bias=1.0)
                                    return sc.activation(
                                        tb[:], ps_p[:], AF.Relu,
                                        bias=be[:, fo:fo + 1],
                                        scale=qsc[:, fo:fo + 1])
                                p.stage(scalar=sp1)
                                p.stage(vector=lambda v, dst=dst, fo=fo, cc=cc:
                                        v.tensor_add(
                                            dst[fo][:, cc * CH:(cc + 1) * CH],
                                            ta[:], tb[:]))

                with p.scoped():
                    wt = p.sb([128, DTI * 1024], BF16, "l1wv")
                    bvrow = p.sb([1, 1024], F32, "l1bvr")
                    bv_s = p.sb([128, CH], F32, "l1bvs")
                    vv_t = p.sb([128, CH], F32, "l1vvt")
                    ps_bv = p.psum([128, CH], F32, "l1psbv")
                    p.stage(gpsimd=lambda gp: (p.dma(
                        gp, wt[:].rearrange("p (a f) -> p a f", a=DTI),
                        W1a["WvT"].rearrange("(a p) f -> p a f", p=128)), None)[1])
                    p.stage(sync=lambda s: (p.dma(
                        s, bvrow[:], W1a["bv"][None, :]), None)[1])

                    for dc in range(2):
                        def bvb(t, dc=dc):
                            return t.matmul(ps_bv[:], ones_row[0:1, :],
                                            bvrow[:, dc * CH:(dc + 1) * CH],
                                            start=True, stop=True)
                        p.stage(tensor=bvb)
                        p.stage(vector=lambda v: v.tensor_copy(bv_s[:],
                                                               ps_bv[:]))
                        for tg in range(NTG):
                            def mmv(t, tg=tg, dc=dc):
                                for i in range(DTI):
                                    last = t.matmul(
                                        ps_p[:],
                                        xnb[i][:, tg * 128:(tg + 1) * 128],
                                        wt[:, i * 1024 + dc * CH:
                                           i * 1024 + (dc + 1) * CH],
                                        start=(i == 0), stop=(i == DTI - 1))
                                return last
                            p.stage(tensor=mmv)
                            p.stage(vector=lambda v: v.tensor_add(
                                vv_t[:], ps_p[:], bv_s[:]))
                            p.stage(scalar=lambda sc, tg=tg, dc=dc: sc.activation(
                                vtm[:, tg * 1024 + dc * CH:
                                    tg * 1024 + (dc + 1) * CH],
                                vv_t[:], AF.Copy))

                units_a = [(tg, h) for tg in range(NTG) for h in range(16)]
                NA = len(units_a)
                for w in range(NA + 4):
                    fns = {}

                    def tfb(t, w=w):
                        last = None
                        if w < NA:
                            tg, h = units_a[w]
                            hi, ho = h // 2, (h % 2) * 64
                            gsl = slice(tg * 128, (tg + 1) * 128)
                            last = t.matmul(ps_s[w % 2][:],
                                            kf[hi][ho:ho + 64, gsl],
                                            qf[hi][ho:ho + 64, gsl],
                                            start=True, stop=True)
                        if 0 <= w - 2 < NA:
                            tg, h = units_a[w - 2]
                            t.matmul(ps_pv[(w - 2) % 2][:, 0:64],
                                     sm_t[(w - 2) % 2][:],
                                     vtm[:, tg * 1024 + h * 64:
                                         tg * 1024 + h * 64 + 64],
                                     start=True, stop=True)
                            last = t.matmul(ps_pv[(w - 2) % 2][:, 64:65],
                                            sm_t[(w - 2) % 2][:], onesb[:],
                                            start=True, stop=True)
                        return last
                    fns["tensor"] = tfb

                    def vfb(v, w=w):
                        last = None
                        if 0 <= w - 1 < NA:
                            last = v.tensor_mul(sm_t[(w - 1) % 2][:],
                                                ps_s[(w - 1) % 2][:],
                                                mask128[:])
                        if 0 <= w - 3 < NA:
                            tg, h = units_a[w - 3]

                            def W(instr):
                                instr.then_inc(p.sem("vv"), 1)
                                p.cnt["vv"] += 1
                                v.wait_ge(p.sem("vv"), p.cnt["vv"])
                                return instr
                            W(v.reciprocal(rden[:],
                                           ps_pv[(w - 3) % 2][:, 64:65]))
                            last = v.tensor_scalar(
                                atm[:, h * 64:(h + 1) * 64],
                                ps_pv[(w - 3) % 2][:, 0:64],
                                rden[:, 0:1], None, ALU.mult)
                        return last
                    if 0 <= w - 1 < NA or 0 <= w - 3 < NA:
                        fns["vector"] = vfb
                    p.stage(**fns)

                    if 0 <= w - 3 < NA and units_a[w - 3][1] == 15:
                        tg_d = units_a[w - 3][0]
                        for hp2 in range(DTI):
                            p.stage(tensor=lambda t, hp=hp2: t.transpose(
                                ps_t2[0][:],
                                atm[:, hp * 128:(hp + 1) * 128], ident[:]))
                            p.stage(vector=lambda v, hp=hp2, tg=tg_d:
                                    v.tensor_copy(
                                        apre[hp][:, tg * 128:(tg + 1) * 128],
                                        ps_t2[0][:]))

                with p.scoped():
                    wt = p.sb([128, DTI * 1024], BF16, "l1wo")
                    p.stage(gpsimd=lambda gp: (p.dma(
                        gp, wt[:].rearrange("p (a f) -> p a f", a=DTI),
                        W1a["WoT"].rearrange("(a p) f -> p a f", p=128)), None)[1])
                    for fo in range(DTI):
                        for cc in range(PS // CH):
                            def mmo(t, fo=fo, cc=cc):
                                for i in range(DTI):
                                    last = t.matmul(
                                        ps_p[:],
                                        wt[:, (i * 8 + fo) * 128:
                                           (i * 8 + fo + 1) * 128],
                                        apre[i][:, cc * CH:(cc + 1) * CH],
                                        start=(i == 0), stop=(i == DTI - 1))
                                return last
                            p.stage(tensor=mmo)
                            p.stage(vector=lambda v, fo=fo: v.tensor_scalar(
                                fac[:], ps_p[:], boc[:, fo:fo + 1], None,
                                ALU.add))

                            def sto(s, fo=fo, cc=cc, hh=hh):
                                p.dma(s, att1_hbm[fo, :,
                                                  hh * PS + cc * CH:
                                                  hh * PS + (cc + 1) * CH],
                                      fac[:])
                                return None
                            p.stage(sync=sto)



W_BLOCK1F = [
    ("l1f_rwT", [D, 8]),
    ("l1f_w1T", [8, D, D]), ("l1f_w2T", [8, D, D]), ("l1f_w3T", [8, D, D]),
    ("l1f_b1c", [8, 128, DTI]), ("l1f_b2c", [8, 128, DTI]),
    ("l1f_b3c", [8, 128, DTI]),
    ("l1f_sw1T", [D, 2048]), ("l1f_sw2T", [D, 2048]), ("l1f_sw3T", [2048, D]),
]
W_BLOCK1A = [
    ("l1a_beta", [16]),
    ("l1a_WqT", [D, D]), ("l1a_WkT", [D, D]), ("l1a_WvT", [D, D]),
    ("l1a_WoT", [D, D]),
    ("l1a_bqc", [128, DTI]), ("l1a_bkc", [128, DTI]), ("l1a_bvc", [128, DTI]),
    ("l1a_bv", [D]), ("l1a_boc", [128, DTI]),
]


def emit_block1_ffn(p, C, win, x, y1p):
    xn32 = p.dram("b1_xn32", [DTI, 128, T], F32)
    xnbf = p.dram("b1_xnbf", [DTI, 128, T], BF16)
    shacc = p.dram("b1_shacc", [DTI, 128, T], F32)

    emit_ln(p, C, x, xn32, xnbf)
    with p.scoped():
        mT = p.sb([32, T], F32, "mT")
        emit_router_softmax_top2(p, C, xn32, win["l1f_rwT"], mT, E=8)
        experts = []
        for half in range(2):
            fs = slice(half * 1024, (half + 1) * 1024)
            experts.append(dict(gate=win["l1f_sw1T"][:, fs],
                                up=win["l1f_sw2T"][:, fs],
                                down=win["l1f_sw3T"][fs, :], e=None, FTI=8))
        emit_moe_ffn(p, C, xnbf, x, shacc, None, experts, tag="l1sh")
        W1 = dict(w1T=win["l1f_w1T"], w2T=win["l1f_w2T"], w3T=win["l1f_w3T"],
                  b1c=win["l1f_b1c"], b2c=win["l1f_b2c"], b3c=win["l1f_b3c"])
        emit_l1_ffn(p, C, xnbf, shacc, y1p, mT, W1)


def emit_block1_attn(p, C, win, xa, att1):
    xnabf = p.dram("b1_xnabf", [DTI, 128, T], BF16)
    emit_ln(p, C, xa, None, xnabf)
    W1a = dict(beta=win["l1a_beta"], WqT=win["l1a_WqT"], WkT=win["l1a_WkT"],
               WvT=win["l1a_WvT"], WoT=win["l1a_WoT"], bqc=win["l1a_bqc"],
               bkc=win["l1a_bkc"], bvc=win["l1a_bvc"], bv=win["l1a_bv"],
               boc=win["l1a_boc"])
    emit_l1_attn(p, C, W1a, xnabf, att1)


def build_blockA(debug=False):
    p = Prog()
    x = p.dram("x", [DTI, 128, T], F32, kind="ExternalInput")
    x1 = p.dram("x1", [DTI, 128, T], F32, kind="ExternalOutput")
    y1p = p.dram("y1p", [DTI, 128, T], F32, kind="ExternalOutput")
    win = declare_w(p, W_BLOCK0 + W_BLOCK1F)
    C = emit_consts(p)
    emit_block0(p, C, win, x, x1, debug=debug)
    emit_block1_ffn(p, C, win, x1, y1p)
    p.final_wait()
    return p.finish()


def build_blockB(debug=False):
    p = Prog()
    xa = p.dram("xa", [DTI, 128, T], F32, kind="ExternalInput")
    att1 = p.dram("att1", [DTI, 128, T], F32, kind="ExternalOutput")
    win = declare_w(p, W_BLOCK1A)
    C = emit_consts(p)
    emit_block1_attn(p, C, win, xa, att1)
    p.final_wait()
    return p.finish()


def build_blockC(debug=False):
    p = Prog()
    y1p = p.dram("y1p", [DTI, 128, T], F32, kind="ExternalInput")
    att = p.dram("att", [DTI, 128, T], F32, kind="ExternalInput")
    x2 = p.dram("x2", [DTI, 128, T], F32)
    x3 = p.dram("x3", [DTI, 128, T], F32)
    x4 = p.dram("x4", [DTI, 128, T], F32, kind="ExternalOutput")
    win = declare_w(p, W_BLOCK2 + W_BLOCK3)
    C = emit_consts(p)
    emit_add(p, y1p, att, x2)
    emit_block2(p, C, win, x2, x3)
    emit_block3(p, C, win, x3, x4)
    p.final_wait()
    return p.finish()


def prep_block1_inputs(inputs):
    f32 = np.float32

    def col128(v):
        return np.ascontiguousarray(np.asarray(v, f32).reshape(-1, 128).T)

    w = {}
    w["l1a_beta"] = np.ascontiguousarray(inputs["L1_attn_beta"], f32)
    w["l1a_WqT"] = np.ascontiguousarray(inputs["L1_attn_Wq"].T, f32)
    w["l1a_WkT"] = np.ascontiguousarray(inputs["L1_attn_Wk"].T, f32)
    w["l1a_WvT"] = np.ascontiguousarray(inputs["L1_attn_Wv"].T, f32)
    w["l1a_WoT"] = np.ascontiguousarray(inputs["L1_attn_Wo"].T, f32)
    w["l1a_bqc"] = col128(inputs["L1_attn_bq"])
    w["l1a_bkc"] = col128(inputs["L1_attn_bk"])
    w["l1a_bvc"] = col128(inputs["L1_attn_bv"])
    w["l1a_bv"] = np.ascontiguousarray(inputs["L1_attn_bv"], f32)
    w["l1a_boc"] = col128(inputs["L1_attn_bo"])
    w["l1f_rwT"] = np.ascontiguousarray(inputs["L1_ffn_gate_w"].T, f32)
    w["l1f_w1T"] = np.ascontiguousarray(
        np.asarray(inputs["L1_ffn_w1"], f32).transpose(0, 2, 1))
    w["l1f_w2T"] = np.ascontiguousarray(
        np.asarray(inputs["L1_ffn_w2"], f32).transpose(0, 2, 1))
    w["l1f_w3T"] = np.ascontiguousarray(
        np.asarray(inputs["L1_ffn_w3"], f32).transpose(0, 2, 1))
    w["l1f_b1c"] = np.ascontiguousarray(
        np.asarray(inputs["L1_ffn_b1"], f32).reshape(8, 8, 128).transpose(0, 2, 1))
    w["l1f_b2c"] = np.ascontiguousarray(
        np.asarray(inputs["L1_ffn_b2"], f32).reshape(8, 8, 128).transpose(0, 2, 1))
    w["l1f_b3c"] = np.ascontiguousarray(
        np.asarray(inputs["L1_ffn_b3"], f32).reshape(8, 8, 128).transpose(0, 2, 1))
    w["l1f_sw1T"] = np.ascontiguousarray(inputs["L1_ffn_sw1"].T, f32)
    w["l1f_sw2T"] = np.ascontiguousarray(inputs["L1_ffn_sw2"].T, f32)
    w["l1f_sw3T"] = np.ascontiguousarray(inputs["L1_ffn_sw3"].T, f32)
    return w


def _fm_shard_tau(x_full):
    out = []
    for c in range(NCORE):
        xc = x_full[:, 256 * c:256 * (c + 1), :]
        fm = np.ascontiguousarray(
            xc.transpose(2, 1, 0).reshape(DTI, 128, T))
        out.append(fm)
    return out


def _fm_unshard_tau(shards):
    x = np.empty((16, SEQ, D), np.float32)
    for c in range(NCORE):
        fm = shards[c].reshape(D, 256, 16)
        x[:, 256 * c:256 * (c + 1), :] = fm.transpose(2, 1, 0)
    return x


def _build_b1_test():
    p = Prog()
    x = p.dram("x", [DTI, 128, T], F32, kind="ExternalInput")
    xa = p.dram("xa", [DTI, 128, T], F32, kind="ExternalInput")
    y1p = p.dram("y1p", [DTI, 128, T], F32, kind="ExternalOutput")
    att1 = p.dram("att1", [DTI, 128, T], F32, kind="ExternalOutput")
    win = declare_w(p, W_BLOCK1F + W_BLOCK1A)
    C = emit_consts(p)
    emit_block1_ffn(p, C, win, x, y1p)
    emit_block1_attn(p, C, win, xa, att1)
    p.final_wait()
    return p.finish()


def run_block1(inputs, x_full, debug=False):
    w = prep_block1_inputs(inputs)
    shards = _fm_shard(x_full)
    ashards = _fm_shard_tau(x_full)
    nc = _build_b1_test()
    in_maps = [dict(w, x=shards[c], xa=ashards[c]) for c in range(NCORE)]
    res = run_bass_kernel_spmd(nc, in_maps, core_ids=list(range(NCORE)))
    if debug:
        return res
    y1p = _fm_unshard([res.results[c]["y1p"] for c in range(NCORE)])
    att = _fm_unshard_tau([res.results[c]["att1"] for c in range(NCORE)])
    return y1p, att
